# version 13
# baseline (speedup 1.0000x reference)
import numpy as np
import concourse.bass as bass
import concourse.mybir as mybir
from concourse.bass_utils import run_bass_kernel_spmd

F32 = mybir.dt.float32
BF16 = mybir.dt.bfloat16
I32 = mybir.dt.int32
AX = mybir.AxisListType.X
ALU = mybir.AluOpType
ACT = mybir.ActivationFunctionType

NEG = -1.0e30
B, S, D = 4, 2048, 4096
H, KVH, DH = 32, 8, 128
G = H // KVH
CB = 16
WIN = 64
TOK = 1024
NT = TOK // 128
NTF = S // 128
KC = D // 128
INTER = 11008
RANK = 1024
NF = INTER // 128
SCALE = float(DH) ** -0.5
EPS = 1e-5
N_CORES = 8



def _host_tables(h):
    t_abs = np.zeros(TOK, dtype=np.int64)
    for tau in range(NT):
        t_abs[tau * 128:(tau + 1) * 128] = (2 * tau + h) * 128 + np.arange(128)

    inv = 1.0 / (10000.0 ** (np.arange(0, DH, 2, dtype=np.float64) / DH))
    ang_own = t_abs[:, None].astype(np.float64) * inv[None, :]
    ang_all = np.arange(S, dtype=np.float64)[:, None] * inv[None, :]

    cm = np.full((TOK, 128), NEG, dtype=np.float32)
    cm[:, 127] = 0.0
    ends = np.arange(127) * CB + CB - 1
    cm[:, 0:127] = np.where(ends[None, :] < t_abs[:, None], 0.0, NEG)

    own = np.full((TOK, 256), NEG, dtype=np.float32)
    for i in range(TOK):
        a = int(t_abs[i])
        tau = i // 128
        base = 2 * tau * 128
        b0 = (a // CB) * CB
        own[i, b0 - base:a - base + 1] = 0.0

    sl = np.full((TOK, 384), NEG, dtype=np.float32)
    ar = np.arange(384)
    for i in range(TOK):
        a = int(t_abs[i])
        tau = i // 128
        ks = 2 * tau * 128 - 64 + ar
        vis = (ks >= a - WIN) & (ks <= a) & (ks >= 0)
        sl[i, vis] = 0.0

    return dict(
        cosq=np.cos(ang_own).astype(np.float32),
        sinq=np.sin(ang_own).astype(np.float32),
        cosk=np.cos(ang_all).astype(np.float32),
        sink=np.sin(ang_all).astype(np.float32),
        cmask=cm, ownmask=own, slmask=sl)



class Sem:
    def __init__(self, nc, name):
        self.h = nc.semaphore(name).__enter__()
        self.n = 0

    def inc(self, instr, k=1):
        instr.then_inc(self.h, k)
        self.n += k
        return instr

    def dinc(self, instr):
        return self.inc(instr, 16)

    def wait(self, eng, val=None):
        eng.wait_ge(self.h, self.n if val is None else val)


class P:

    def __init__(self, upto=100):
        self.upto = upto
        nc = bass.Bass("TRN2", target_bir_lowering=False)
        self.nc = nc
        self.inp = {}
        self.scr = {}
        self._declare()
        self.sg = Sem(nc, "sg")
        self.ss = Sem(nc, "ss")
        self.sa = Sem(nc, "sa")
        self.st = Sem(nc, "st")
        self.sv = Sem(nc, "sv")
        self.sq = Sem(nc, "sq")
        self.sp = Sem(nc, "sp")
        self.sw = [Sem(nc, "sw0"), Sem(nc, "sw1")]
        self.sgp = Sem(nc, "sgp")
        self.sk = [Sem(nc, "sk0"), Sem(nc, "sk1")]
        self.sx = [Sem(nc, "sx0"), Sem(nc, "sx1")]

    def _declare(self):
        nc = self.nc
        ii = self.inp

        def inp(name, shape, dtype=F32):
            ii[name] = nc.dram_tensor(name, list(shape), dtype,
                                      kind="ExternalInput").ap()

        inp("x_full", (S, D)); inp("x_own", (TOK, D))
        inp("w_qkv", (6144, D)); inp("w_o", (D, D))
        inp("k_posT", (DH, KVH * CB)); inp("v_posT", (DH, KVH * CB))
        inp("k_w1", (2048, 2048)); inp("k_b1c", (128, 16))
        inp("k_w2", (DH, 2048)); inp("k_b2c", (128, 1))
        inp("v_w1", (2048, 2048)); inp("v_b1c", (128, 16))
        inp("v_w2", (DH, 2048)); inp("v_b2b", (128, DH))
        inp("mem_kT", (DH, KVH)); inp("mem_vb", (1, KVH * DH))
        inp("w_gate", (96, D)); inp("b_gateb", (128, 96))
        inp("U1", (D, RANK)); inp("V1", (RANK, 2 * INTER))
        inp("U2", (INTER, RANK)); inp("V2", (RANK, D))
        inp("b1c", (128, 2 * NF)); inp("b2b", (128, D))
        inp("l1_mask", (D, D), I32); inp("l1_w", (D, D))
        inp("l1_biasb", (128, D))
        inp("lf_W", (D, D)); inp("lf_A", (32, D)); inp("lf_B", (D, 32))
        inp("lf_magb", (1, D)); inp("e4", (128, 4)); inp("ident", (128, 128))
        inp("cosq", (TOK, 64)); inp("sinq", (TOK, 64))
        inp("cosk", (S, 64)); inp("sink", (S, 64))
        inp("cmask", (TOK, 128)); inp("ownmask", (TOK, 256))
        inp("slmask", (TOK, 384))

        self.y_out = nc.dram_tensor("y_out", [TOK, D], F32,
                                    kind="ExternalOutput").ap()

        sc = self.scr

        def scratch(name, shape, dtype=BF16):
            sc[name] = nc.dram_tensor(name, list(shape), dtype,
                                      kind="Internal").ap()

        scratch("wq_bf", (6144, D)); scratch("wo_bf", (D, D))
        scratch("kw1_bf", (2048, 2048)); scratch("vw1_bf", (2048, 2048))
        scratch("kw2_bf", (DH, 2048)); scratch("vw2_bf", (DH, 2048))
        scratch("wg_bf", (96, D)); scratch("l1w_bf", (D, D))
        scratch("lfW_bf", (D, D)); scratch("lfB_bf", (D, 128))
        scratch("x_bf", (S, D)); scratch("xown_bf", (TOK, D))
        scratch("k_scr", (S, KVH * DH)); scratch("rk_scr", (S, KVH * DH))
        scratch("v_scr", (S, KVH * DH))
        scratch("q_scr", (TOK, D)); scratch("rq_scr", (TOK, D))
        scratch("comb_scr", (TOK, D))
        scratch("yh_scr", (TOK, D // 2), F32)
        scratch("res32", (TOK, D), F32)
        scratch("resbf", (TOK, D))
        scratch("effW_scr", (D, D))
        scratch("bm_scr", (128, 128), F32)
        scratch("row_scr", (1, D), F32)
        scratch("gates_scr", (128, NT * 96), F32)

    def p0_casts(self):
        nc, ii, sc = self.nc, self.inp, self.scr
        sg = self.sg
        ss, sa, sv = self.ss, self.sa, self.sv
        sw0 = self.sw[0]
        with (
            nc.sbuf_tensor("p0xa", [128, 2 * D], F32) as xstg,
            nc.sbuf_tensor("p0xb", [128, 2 * D], BF16) as xbst,
            nc.Block("p0") as blk,
        ):
            @blk.gpsimd
            def _(g):
                sg.dinc(g.dma_start(sc["wq_bf"][4096:6144, :],
                                    ii["w_qkv"][4096:6144, :]))
                sg.wait(g)

            def _waves23(g):
                sw0.dinc(g.dma_start(sc["xown_bf"][:, :], ii["x_own"][:, :]))
                sw0.dinc(g.dma_start(sc["wq_bf"][0:4096, :],
                                     ii["w_qkv"][0:4096, :]))
                sw0.dinc(g.dma_start(sc["wg_bf"][:, :], ii["w_gate"][:, :]))
                self._p0_wave2 = sw0.n
                for dst, src in [
                    ("wo_bf", "w_o"), ("kw1_bf", "k_w1"), ("vw1_bf", "v_w1"),
                    ("kw2_bf", "k_w2"), ("vw2_bf", "v_w2"),
                    ("l1w_bf", "l1_w"), ("lfW_bf", "lf_W"),
                ]:
                    sg.dinc(g.dma_start(sc[dst][:, :], ii[src][:, :]))
                sg.dinc(g.dma_start(sc["lfB_bf"][:, 0:32], ii["lf_B"][:, :]))
                self._p0_all = sg.n
            self._waves23 = _waves23

            sv_b = sv.n
            sk_b = [self.sk[0].n, self.sk[1].n]
            sx_b = [self.sx[0].n, self.sx[1].n]

            @blk.sync
            def _(s):
                for tc in range(NTF):
                    par = tc % 2
                    if tc >= 2:
                        s.wait_ge(sv.h, sv_b + tc - 1)
                    self.sk[par].dinc(s.dma_start(
                        xstg[:, par * D:(par + 1) * D],
                        ii["x_full"][tc * 128:(tc + 1) * 128, :]))

            @blk.vector
            def _(v):
                for tc in range(NTF):
                    par = tc % 2
                    v.wait_ge(self.sk[par].h, sk_b[par] + 16 * (tc // 2 + 1))
                    if tc >= 2:
                        v.wait_ge(self.sx[par].h, sx_b[par] + 16 * (tc // 2))
                    sv.inc(v.tensor_copy(xbst[:, par * D:(par + 1) * D],
                                         xstg[:, par * D:(par + 1) * D]))

            @blk.scalar
            def _(s2):
                for tc in range(NTF):
                    par = tc % 2
                    s2.wait_ge(sv.h, sv_b + tc + 1)
                    self.sx[par].dinc(s2.dma_start(
                        sc["x_bf"][tc * 128:(tc + 1) * 128, :],
                        xbst[:, par * D:(par + 1) * D]))
                self.sx[0].wait(s2)
                self.sx[1].wait(s2)

    def tm_gemm(self, tag, xsrc, M, wsrc, n0, Nsec, yacc, bufs,
                pre_t=False, rhs_cb=None, gp_hook=None):
        nc = self.nc
        ss, st, sp, sv = self.ss, self.st, self.sp, self.sv
        xg, wgb, psA, psB = bufs
        NTC = M // 128
        GK = 16
        ps_base = sp.n
        for grp in range(2):
            kcs = list(range(grp * GK, (grp + 1) * GK))
            with nc.Block(f"{tag}_g{grp}") as blk:
                if grp == 0 and gp_hook is not None:
                    blk.gpsimd(gp_hook)
                @blk.sync
                def _(s):
                    for j, kc in enumerate(kcs):
                        ss.dinc(s.dma_start(
                            xg[:, j * M:(j + 1) * M],
                            xsrc[:, kc * 128:(kc + 1) * 128], transpose=True))
                        if pre_t:
                            ss.dinc(s.dma_start(
                                wgb[:, j * Nsec:(j + 1) * Nsec],
                                wsrc[kc * 128:(kc + 1) * 128, n0:n0 + Nsec]))
                        else:
                            ss.dinc(s.dma_start(
                                wgb[:, j * Nsec:(j + 1) * Nsec],
                                wsrc[n0:n0 + Nsec, kc * 128:(kc + 1) * 128],
                                transpose=True))
                    ss.wait(s)

                if rhs_cb is not None:
                    @blk.vector
                    def _(v):
                        ss.wait(v)
                        for j, kc in enumerate(kcs):
                            rhs_cb(v, wgb, j, kc, n0)
                        sv.inc(v.drain())

                rdy_v = sv.n
                rdy_s = ss.n

                @blk.tensor
                def _(t):
                    t.wait_ge(ss.h, rdy_s)
                    if rhs_cb is not None:
                        t.wait_ge(sv.h, rdy_v)
                    for tc in range(NTC):
                        r = grp * NTC + tc
                        ps = psA if r % 2 == 0 else psB
                        t.wait_ge(sp.h, ps_base + max(0, r - 1))
                        last = None
                        for j in range(GK):
                            lhs = xg[:, j * M + tc * 128:j * M + (tc + 1) * 128]
                            for nn0 in range(0, Nsec, 512):
                                nn = min(512, Nsec - nn0)
                                last = t.matmul(
                                    ps[:, nn0:nn0 + nn], lhs,
                                    wgb[:, j * Nsec + nn0:j * Nsec + nn0 + nn],
                                    start=(j == 0), stop=(j == GK - 1))
                        st.inc(last)

                base = st.n - NTC

                @blk.vector
                def _(v):
                    v.drain()
                    for tc in range(NTC):
                        r = grp * NTC + tc
                        ps = psA if r % 2 == 0 else psB
                        v.wait_ge(st.h, base + tc + 1)
                        dst = yacc[:, tc * Nsec:(tc + 1) * Nsec]
                        if grp == 0:
                            sp.inc(v.tensor_copy(dst, ps[:, 0:Nsec]))
                        else:
                            sp.inc(v.tensor_tensor(dst, dst, ps[:, 0:Nsec],
                                                   op=ALU.add))

    def rope_tm(self, v, out, src, cos_ap, sin_ap, nh, tmp):
        s3 = src.rearrange("p (h d two) -> p h d two", h=nh, two=2)
        o3 = out.rearrange("p (h d two) -> p h d two", h=nh, two=2)
        t3 = tmp.rearrange("p (h d) -> p h d", h=nh)
        cb = cos_ap.unsqueeze(1).to_broadcast([128, nh, 64])
        sb = sin_ap.unsqueeze(1).to_broadcast([128, nh, 64])
        v.tensor_tensor(t3, s3[:, :, :, 1], sb, op=ALU.mult)
        v.tensor_tensor(o3[:, :, :, 0], s3[:, :, :, 0], cb, op=ALU.mult)
        v.drain()
        v.tensor_tensor(o3[:, :, :, 0], o3[:, :, :, 0], t3, op=ALU.subtract)
        v.drain()
        v.tensor_tensor(t3, s3[:, :, :, 0], sb, op=ALU.mult)
        v.tensor_tensor(o3[:, :, :, 1], s3[:, :, :, 1], cb, op=ALU.mult)
        v.drain()
        v.tensor_tensor(o3[:, :, :, 1], o3[:, :, :, 1], t3, op=ALU.add)

    def p1_qkv(self):
        nc, ii, sc = self.nc, self.inp, self.scr
        ss, sv, sa = self.ss, self.sv, self.sa

        with (
            nc.sbuf_tensor("xg1", [128, 16 * S], BF16) as xg,
            nc.sbuf_tensor("wgb1", [128, 16 * 1024], BF16) as wgb,
            nc.sbuf_tensor("yacc1", [128, NTF * 1024], F32) as yacc,
            nc.psum_tensor("psA1", [128, 1024], F32) as psA,
            nc.psum_tensor("psB1", [128, 1024], F32) as psB,
            nc.sbuf_tensor("ctab", [128, NTF * 64], F32) as ctab,
            nc.sbuf_tensor("stab", [128, NTF * 64], F32) as stab,
            nc.sbuf_tensor("stg", [128, 2 * 1024], BF16) as stg,
            nc.sbuf_tensor("stg2", [128, 2 * 1024], BF16) as stg2,
            nc.sbuf_tensor("rtmp", [128, KVH * 64], F32) as rtmp,
        ):
            for which in ("k", "v"):
                row0 = 4096 if which == "k" else 5120
                self.tm_gemm(f"p1{which}", sc["x_bf"], S, sc["wq_bf"], row0,
                             1024, yacc, (xg, wgb, psA, psB),
                             gp_hook=(self._waves23 if which == "k" else None))
                self._waves23 = None
                ndma = 2 if which == "k" else 1
                with nc.Block(f"p1{which}p") as blk:
                    @blk.sync
                    def _(s):
                        if which == "k":
                            ss.dinc(s.dma_start(
                                ctab[:, :].rearrange("p (n d) -> p n d", d=64),
                                ii["cosk"][:, :].rearrange("(n p) d -> p n d", p=128)))
                            ss.dinc(s.dma_start(
                                stab[:, :].rearrange("p (n d) -> p n d", d=64),
                                ii["sink"][:, :].rearrange("(n p) d -> p n d", p=128)))
                        ss.wait(s)

                    dma_rdy = self.ss.n
                    ps_rdy = self.sp.n
                    sx_b = [self.sx[0].n, self.sx[1].n]

                    @blk.vector
                    def _(v):
                        v.wait_ge(ss.h, dma_rdy)
                        v.wait_ge(self.sp.h, ps_rdy)
                        v.drain()
                        for tc in range(NTF):
                            if tc >= 2:
                                v.wait_ge(self.sx[tc % 2].h,
                                          sx_b[tc % 2] + 16 * ndma * (tc // 2))
                            hb = 1024 * (tc % 2)
                            ych = yacc[:, tc * 1024:(tc + 1) * 1024]
                            v.tensor_copy(stg[:, hb:hb + 1024], ych)
                            if which == "k":
                                self.rope_tm(v, stg2[:, hb:hb + 1024], ych,
                                             ctab[:, tc * 64:(tc + 1) * 64],
                                             stab[:, tc * 64:(tc + 1) * 64],
                                             KVH, rtmp)
                            sv.inc(v.drain())

                    vbase = sv.n - NTF

                    @blk.scalar
                    def _(s2):
                        for tc in range(NTF):
                            s2.wait_ge(sv.h, vbase + tc + 1)
                            hb = 1024 * (tc % 2)
                            dst = sc["k_scr"] if which == "k" else sc["v_scr"]
                            self.sx[tc % 2].dinc(s2.dma_start(
                                dst[tc * 128:(tc + 1) * 128, :],
                                stg[:, hb:hb + 1024]))
                            if which == "k":
                                self.sx[tc % 2].dinc(s2.dma_start(
                                    sc["rk_scr"][tc * 128:(tc + 1) * 128, :],
                                    stg2[:, hb:hb + 1024]))
                        self.sx[0].wait(s2)
                        self.sx[1].wait(s2)

        with (
            nc.sbuf_tensor("xg2", [128, 16 * TOK], BF16) as xg,
            nc.sbuf_tensor("wgb2", [128, 16 * 2048], BF16) as wgb,
            nc.sbuf_tensor("yacc2", [128, NT * 2048], F32) as yacc,
            nc.psum_tensor("psA2", [128, 2048], F32) as psA,
            nc.psum_tensor("psB2", [128, 2048], F32) as psB,
            nc.sbuf_tensor("ctab2", [128, NT * 64], F32) as ctab,
            nc.sbuf_tensor("stab2", [128, NT * 64], F32) as stab,
            nc.sbuf_tensor("stg3", [128, 2 * 2048], BF16) as stg,
            nc.sbuf_tensor("stg4", [128, 2 * 2048], BF16) as stg2,
            nc.sbuf_tensor("rtmp2", [128, 16 * 64], F32) as rtmp,
        ):
            with nc.Block("p1qt") as blk:
                @blk.sync
                def _(s):
                    s.wait_ge(self.sw[0].h, self._p0_wave2)
                    ss.dinc(s.dma_start(
                        ctab[:, :].rearrange("p (n d) -> p n d", d=64),
                        ii["cosq"][:, :].rearrange("(n p) d -> p n d", p=128)))
                    ss.dinc(s.dma_start(
                        stab[:, :].rearrange("p (n d) -> p n d", d=64),
                        ii["sinq"][:, :].rearrange("(n p) d -> p n d", p=128)))
                    ss.wait(s)

            for half in range(2):
                self.tm_gemm(f"p1q{half}", sc["xown_bf"], TOK, sc["wq_bf"],
                             half * 2048, 2048, yacc, (xg, wgb, psA, psB))
                with nc.Block(f"p1q{half}p") as blk:
                    ps_rdy = self.sp.n
                    sx_b = [self.sx[0].n, self.sx[1].n]

                    @blk.vector
                    def _(v):
                        v.wait_ge(self.sp.h, ps_rdy)
                        v.drain()
                        for tc in range(NT):
                            if tc >= 2:
                                v.wait_ge(self.sx[tc % 2].h,
                                          sx_b[tc % 2] + 32 * (tc // 2))
                            hb = 2048 * (tc % 2)
                            ych = yacc[:, tc * 2048:(tc + 1) * 2048]
                            v.tensor_copy(stg[:, hb:hb + 2048], ych)
                            self.rope_tm(v, stg2[:, hb:hb + 2048], ych,
                                         ctab[:, tc * 64:(tc + 1) * 64],
                                         stab[:, tc * 64:(tc + 1) * 64],
                                         16, rtmp)
                            sv.inc(v.drain())

                    vbase = sv.n - NT

                    @blk.scalar
                    def _(s2):
                        for tc in range(NT):
                            s2.wait_ge(sv.h, vbase + tc + 1)
                            hb = 2048 * (tc % 2)
                            cs = slice(half * 2048, (half + 1) * 2048)
                            self.sx[tc % 2].dinc(s2.dma_start(
                                sc["q_scr"][tc * 128:(tc + 1) * 128, cs],
                                stg[:, hb:hb + 2048]))
                            self.sx[tc % 2].dinc(s2.dma_start(
                                sc["rq_scr"][tc * 128:(tc + 1) * 128, cs],
                                stg2[:, hb:hb + 2048]))
                        self.sx[0].wait(s2)
                        self.sx[1].wait(s2)

        with (
            nc.sbuf_tensor("xg3", [128, 16 * TOK], BF16) as xg,
            nc.sbuf_tensor("wgb3", [128, 16 * 96], BF16) as wgb,
            nc.psum_tensor("psA3", [128, 96], F32) as psA,
            nc.psum_tensor("psB3", [128, 96], F32) as psB,
            nc.sbuf_tensor("bgt", [128, 96], F32) as bgt,
            nc.sbuf_tensor("gac", [128, NT * 96], F32) as gac,
        ):
            self.tm_gemm("p1g", sc["xown_bf"], TOK, sc["wg_bf"], 0, 96,
                         gac, (xg, wgb, psA, psB))
            with nc.Block("p1gp") as blk:
                @blk.sync
                def _(s):
                    ss.dinc(s.dma_start(bgt[:, :], ii["b_gateb"][:, :]))
                    ss.wait(s)

                ps_rdy = self.sp.n
                dma_rdy = ss.n

                @blk.vector
                def _(v):
                    v.wait_ge(self.sp.h, ps_rdy)
                    v.wait_ge(ss.h, dma_rdy)
                    v.drain()
                    bb = bgt[:, :].unsqueeze(1).to_broadcast([128, NT, 96])
                    v.tensor_tensor(
                        gac[:, :].rearrange("p (n g) -> p n g", g=96),
                        gac[:, :].rearrange("p (n g) -> p n g", g=96),
                        bb, op=ALU.add)
                    sv.inc(v.drain())

                @blk.scalar
                def _(s2):
                    self.sv.wait(s2)
                    self.sq.inc(s2.activation(gac[:, :], gac[:, :], ACT.Sigmoid))
                    self.sq.wait(s2)
                    sa.dinc(s2.dma_start(sc["gates_scr"][:, :], gac[:, :]))
                    sa.wait(s2)


    def p2_prep(self, rkp, vtm, ckT, cvt, gts):
        nc, ii, sc = self.nc, self.inp, self.scr
        ss, sa, sv, st, sq, sp = self.ss, self.sa, self.sv, self.st, self.sq, self.sp

        with nc.Block("p2a") as blk:
            @blk.gpsimd
            def _(g):
                sg = self.sg
                sg.dinc(g.dma_start(
                    ckT[:, :].rearrange("p (k c) -> p k c", c=128)[:, :, 127:128],
                    ii["mem_kT"][:, :].unsqueeze(2),
                    allow_slow_non_contiguous=True))
                sg.dinc(g.dma_start(cvt[127:128, :], ii["mem_vb"][:, :]))
                sg.wait(g)

            @blk.vector
            def _(v):
                for kvh in range(KVH):
                    o = kvh * 2176
                    v.memset(rkp[:, o:o + 64], 0.0)
                    v.memset(rkp[:, o + 2112:o + 2176], 0.0)
                    v.memset(vtm[:, kvh * 18 * 128:(kvh * 18 + 1) * 128], 0.0)
                    v.memset(vtm[:, (kvh * 18 + 17) * 128:(kvh * 18 + 18) * 128], 0.0)
                sv.inc(v.drain())

            vfence = sv.n

            @blk.sync
            def _(s):
                s.wait_ge(sv.h, vfence)
                for kvh in range(KVH):
                    ss.dinc(s.dma_start(
                        rkp[:, kvh * 2176 + 64:kvh * 2176 + 2112],
                        sc["rk_scr"][:, kvh * 128:(kvh + 1) * 128],
                        transpose=True))
                    vt4 = vtm[:, :].rearrange("p (k b c) -> p k b c", b=18, c=128)
                    ss.dinc(s.dma_start(
                        vt4[:, kvh, 1:17, :],
                        sc["v_scr"][:, kvh * 128:(kvh + 1) * 128].rearrange(
                            "(n p) c -> p n c", p=128)))
                ss.dinc(s.dma_start(gts[:, :], sc["gates_scr"][:, :]))
                ss.wait(s)

        with (
            nc.sbuf_tensor("w1T", [128, 16 * 2048], BF16) as w1T,
            nc.sbuf_tensor("w2T", [128, 16 * 128], BF16) as w2T,
            nc.sbuf_tensor("kt", [128, 2 * 2048], BF16) as kt,
            nc.sbuf_tensor("kpk", [128, 2 * 2048], BF16) as kpk,
            nc.sbuf_tensor("h1t", [128, 2 * 16 * 128], BF16) as h1t,
            nc.sbuf_tensor("posb", [128, 2 * 16], F32) as posb,
            nc.sbuf_tensor("b1b", [128, 16], F32) as b1b,
            nc.sbuf_tensor("b2c", [128, 1], F32) as b2c,
            nc.sbuf_tensor("b2r", [128, 128], F32) as b2r,
            nc.psum_tensor("psh1", [128, 128], F32) as psh1,
            nc.psum_tensor("psh2", [128, 128], F32) as psh2,
            nc.psum_tensor("psck1", [128, 128], F32) as psck1,
            nc.psum_tensor("psck2", [128, 128], F32) as psck2,
        ):
            for which in ("k", "v"):
                w1_bf = sc["kw1_bf"] if which == "k" else sc["vw1_bf"]
                w2_bf = sc["kw2_bf"] if which == "k" else sc["vw2_bf"]
                with nc.Block(f"p2{which}w") as blk:
                    @blk.sync
                    def _(s):
                        s.wait_ge(self.sg.h, self.sg.n)
                        for ic in range(16):
                            ss.dinc(s.dma_start(
                                w1T[:, ic * 2048:(ic + 1) * 2048],
                                w1_bf[:, ic * 128:(ic + 1) * 128], transpose=True))
                        for oc in range(16):
                            ss.dinc(s.dma_start(
                                w2T[:, oc * 128:(oc + 1) * 128],
                                w2_bf[:, oc * 128:(oc + 1) * 128], transpose=True))
                        ss.dinc(s.dma_start(
                            b1b[:, :], ii["k_b1c" if which == "k" else "v_b1c"][:, :]))
                        if which == "k":
                            ss.dinc(s.dma_start(b2c[:, :], ii["k_b2c"][:, :]))
                        else:
                            ss.dinc(s.dma_start(b2r[:, :], ii["v_b2b"][:, :]))
                        ss.wait(s)

                fz = {}
                with nc.Block(f"p2{which}c") as blk:
                    for kvh in range(KVH):
                        par = kvh % 2
                        ktp = kt[:, par * 2048:(par + 1) * 2048]
                        kpkp = kpk[:, par * 2048:(par + 1) * 2048]
                        h1p = h1t[:, par * 2048:(par + 1) * 2048]
                        posp = posb[:, par * 16:(par + 1) * 16]
                        psck = psck1 if par == 0 else psck2

                        kt_free = fz.get(("kpk_done", kvh - 2))

                        skp = self.sk[par]

                        @blk.sync
                        def _(s):
                            if kt_free:
                                s.wait_ge(kt_free[0].h, kt_free[1])
                            src_ = sc["k_scr"] if which == "k" else sc["v_scr"]
                            skp.dinc(s.dma_start(
                                ktp, src_[:, kvh * 128:(kvh + 1) * 128],
                                transpose=True))
                            skp.dinc(s.dma_start(
                                posp,
                                ii["k_posT" if which == "k" else "v_posT"]
                                [:, kvh * 16:(kvh + 1) * 16]))

                        dmaf = skp.n
                        kpk_free = fz.get(("h1_done", kvh - 2))

                        @blk.vector
                        def _(v):
                            v.wait_ge(skp.h, dmaf)
                            if kpk_free:
                                v.wait_ge(kpk_free[0].h, kpk_free[1])
                            v.drain()
                            pb = posp.unsqueeze(1).to_broadcast([128, 128, 16])
                            v.tensor_tensor(
                                kpkp.rearrange("p (c t) -> p c t", t=16),
                                ktp.rearrange("p (c t) -> p c t", t=16),
                                pb, op=ALU.add)
                            sv.inc(v.drain())
                        fz[("kpk_done", kvh)] = (sv, sv.n)

                        vfen = sv.n
                        sq_base = sq.n
                        h1_free = fz.get(("ck_done", kvh - 2))

                        @blk.tensor
                        def _(t):
                            t.wait_ge(sv.h, vfen)
                            for oc in range(16):
                                psh = psh1 if oc % 2 == 0 else psh2
                                if oc >= 2:
                                    t.wait_ge(sq.h, sq_base + oc - 1)
                                last = None
                                for ic in range(16):
                                    last = t.matmul(
                                        psh[:, :],
                                        w1T[:, ic * 2048 + oc * 128:
                                            ic * 2048 + (oc + 1) * 128],
                                        kpkp[:, ic::16],
                                        start=(ic == 0), stop=(ic == 15))
                                st.inc(last)
                        fz[("h1_done", kvh)] = (st, st.n)

                        tbase = st.n - 16

                        @blk.scalar
                        def _(s2):
                            if h1_free:
                                s2.wait_ge(h1_free[0].h, h1_free[1])
                            for oc in range(16):
                                psh = psh1 if oc % 2 == 0 else psh2
                                s2.wait_ge(st.h, tbase + oc + 1)
                                sq.inc(s2.activation(
                                    h1p[:, oc * 128:(oc + 1) * 128], psh[:, :],
                                    ACT.Relu, bias=b1b[:, oc:oc + 1]))

                        sqf = sq.n
                        psck_free = fz.get(("evac_done", kvh - 2))

                        @blk.tensor
                        def _(t):
                            t.wait_ge(sq.h, sqf)
                            if psck_free:
                                t.wait_ge(psck_free[0].h, psck_free[1])
                            last = None
                            for oc in range(16):
                                if which == "k":
                                    last = t.matmul(
                                        psck[:, :], w2T[:, oc * 128:(oc + 1) * 128],
                                        h1p[:, oc * 128:(oc + 1) * 128],
                                        start=(oc == 0), stop=(oc == 15))
                                else:
                                    last = t.matmul(
                                        psck[:, :], h1p[:, oc * 128:(oc + 1) * 128],
                                        w2T[:, oc * 128:(oc + 1) * 128],
                                        start=(oc == 0), stop=(oc == 15))
                            st.inc(last)
                        fz[("ck_done", kvh)] = (st, st.n)

                        tf2 = st.n

                        @blk.vector
                        def _(v):
                            v.wait_ge(st.h, tf2)
                            v.drain()
                            if which == "k":
                                v.tensor_scalar(
                                    ckT[:, kvh * 128:kvh * 128 + 127],
                                    psck[:, 0:127], b2c[:, 0:1], None,
                                    op0=ALU.add)
                            else:
                                v.tensor_tensor(
                                    cvt[0:127, kvh * 128:(kvh + 1) * 128],
                                    psck[0:127, :], b2r[0:127, :], op=ALU.add)
                            sv.inc(v.drain())
                        fz[("evac_done", kvh)] = (sv, sv.n)

    def p3_attn(self, rkp, vtm, ckT, cvt, gts):
        nc, ii, sc = self.nc, self.inp, self.scr
        ss, sa, sv, st, sq, sp = self.ss, self.sa, self.sv, self.st, self.sq, self.sp
        self.f3 = {}

        from contextlib import ExitStack
        with ExitStack() as _es:
            E = _es.enter_context
            qh = E(nc.sbuf_tensor("qh", [128, G * 1024], BF16))
            rqh = E(nc.sbuf_tensor("rqh", [128, G * 1024], BF16))
            cmk = E(nc.sbuf_tensor("cmk", [128, NT * 128], BF16))
            omk = E(nc.sbuf_tensor("omk", [128, NT * 256], BF16))
            smk = E(nc.sbuf_tensor("smk", [128, NT * 384], BF16))
            cs = E(nc.sbuf_tensor("cs", [128, 2 * G * 128], F32))
            imp = E(nc.sbuf_tensor("imp", [128, 2 * 128], F32))
            impsm = E(nc.sbuf_tensor("impsm", [128, 2 * 128], F32))
            m8 = E(nc.sbuf_tensor("m8", [128, 2 * 8], F32))
            repl = E(nc.sbuf_tensor("repl", [128, 2 * 127], F32))
            fmask = E(nc.sbuf_tensor("fmask", [128, 2 * 128], F32))
            maskb = E(nc.sbuf_tensor("maskb", [128, 2 * 2048], BF16))
            pcc = E(nc.sbuf_tensor("pcc", [128, 2 * G * 128], BF16))
            pTc = E(nc.sbuf_tensor("pTc", [128, 2 * G * 128], BF16))
            couts = E(nc.sbuf_tensor("couts", [128, 2 * G * 128], F32))
            comb = E(nc.sbuf_tensor("comb", [128, 2 * G * 128], BF16))
            rmg = E(nc.sbuf_tensor("rmg", [128, 2 * 4], F32))
            rsg = E(nc.sbuf_tensor("rsg", [128, 2 * 4], F32))
            rrg = E(nc.sbuf_tensor("rrg", [128, 2 * 1], F32))
            r1 = E(nc.sbuf_tensor("r1", [128, 2 * 1], F32))
            scog = E(nc.sbuf_tensor("scog", [128, G * 2048], F32))
            slsco = E(nc.sbuf_tensor("slsco", [128, G * 384], F32))
            pbfg = E(nc.sbuf_tensor("pbfg", [128, G * 2048], BF16))
            pTg = E(nc.sbuf_tensor("pTg", [128, 4 * 128], BF16))
            slpg = E(nc.sbuf_tensor("slpg", [128, G * 512], BF16))
            rmF = E(nc.sbuf_tensor("rmF", [128, 4], F32))
            rsF = E(nc.sbuf_tensor("rsF", [128, 4], F32))
            rmS = E(nc.sbuf_tensor("rmS", [128, 4], F32))
            rsS = E(nc.sbuf_tensor("rsS", [128, 4], F32))
            prd = E(nc.sbuf_tensor("prd", [128, 12], F32))
            idb = E(nc.sbuf_tensor("idb", [128, 128], BF16))
            idf = E(nc.sbuf_tensor("idf", [128, 128], F32))
            psc0 = E(nc.psum_tensor("psc0", [128, 512], F32))
            psc1 = E(nc.psum_tensor("psc1", [128, 512], F32))
            psf0 = E(nc.psum_tensor("psf0", [128, 512], F32))
            psf1 = E(nc.psum_tensor("psf1", [128, 512], F32))
            pso = E(nc.psum_tensor("pso", [128, 1024], F32))
            ptr0 = E(nc.psum_tensor("ptr0", [128, 1024], BF16))
            ptr1 = E(nc.psum_tensor("ptr1", [128, 1024], BF16))

            bufs = dict(qh=qh, rqh=rqh, cmk=cmk, omk=omk, smk=smk, cs=cs,
                        imp=imp, impsm=impsm, m8=m8, repl=repl,
                        fmask=fmask, maskb=maskb, pcc=pcc, pTc=pTc,
                        couts=couts, comb=comb, rmg=rmg, rsg=rsg, rrg=rrg,
                        r1=r1, scog=scog, slsco=slsco, pbfg=pbfg, pTg=pTg,
                        slpg=slpg, rmF=rmF, rsF=rsF, rmS=rmS,
                        rsS=rsS, prd=prd, idb=idb,
                        psc=[psc0, psc1], psf=[psf0, psf1], pso=pso,
                        ptr=[ptr0[:, 0:128], ptr1[:, 0:128]],
                        rkp=rkp, vtm=vtm, ckT=ckT,
                        cvt=cvt, gts=gts)

            with nc.Block("p3m") as blk:
                @blk.gpsimd
                def _(g):
                    for dst, key in ((cmk, "cmask"), (omk, "ownmask"),
                                     (smk, "slmask")):
                        self.sg.dinc(g.dma_start(
                            dst[:, :].rearrange("p (n c) -> p n c", n=NT),
                            ii[key][:, :].rearrange("(n p) c -> p n c", p=128)))
                    self.sg.dinc(g.dma_start(idf[:, :], ii["ident"][:, :]))
                    self.sg.wait(g)

                @blk.vector
                def _(v):
                    self.sg.wait(v)
                    self.sv.inc(v.tensor_copy(idb[:, :], idf[:, :]))

            for kvh in range(KVH):
                with nc.Block(f"p3q{kvh}") as blk:
                    fw = self.f3.get("qh")

                    @blk.sync
                    def _(s):
                        if fw:
                            s.wait_ge(fw[0].h, fw[1])
                        for g in range(G):
                            hd = kvh * G + g
                            ss.dinc(s.dma_start(
                                qh[:, g * 1024:(g + 1) * 1024],
                                sc["q_scr"][:, hd * 128:(hd + 1) * 128],
                                transpose=True))
                            ss.dinc(s.dma_start(
                                rqh[:, g * 1024:(g + 1) * 1024],
                                sc["rq_scr"][:, hd * 128:(hd + 1) * 128],
                                transpose=True))
                    self.f3["qhload"] = (ss, ss.n)

                with nc.Block(f"ku{kvh}") as blk:
                    for t in range(NT + 1):
                        if t < NT:
                            self._p3_comp(blk, kvh, t, bufs)
                        if t >= 1:
                            self._p3_fine(blk, kvh, t - 1, bufs)
                self.f3["qh"] = (st, st.n)

    def _p3_comp(self, blk, kvh, tau, bb):
        ss, sa, sv, st, sq = self.ss, self.sa, self.sv, self.st, self.sq
        f3 = self.f3
        par = tau % 2
        psc = bb["psc"][par]
        cs = bb["cs"][:, par * 512:(par + 1) * 512]
        imp = bb["imp"][:, par * 128:(par + 1) * 128]
        impsm = bb["impsm"][:, par * 128:(par + 1) * 128]
        m8 = bb["m8"][:, par * 8:(par + 1) * 8]
        repl = bb["repl"][:, par * 127:(par + 1) * 127]
        fmask = bb["fmask"][:, par * 128:(par + 1) * 128]
        maskb = bb["maskb"][:, par * 2048:(par + 1) * 2048]
        pcc = bb["pcc"][:, par * 512:(par + 1) * 512]
        pTc = bb["pTc"][:, par * 512:(par + 1) * 512]
        couts = bb["couts"][:, par * 512:(par + 1) * 512]
        rmg = bb["rmg"][:, par * 4:(par + 1) * 4]
        rsg = bb["rsg"][:, par * 4:(par + 1) * 4]
        rrg = bb["rrg"][:, par * 1:(par + 1) * 1]
        r1 = bb["r1"][:, par * 1:(par + 1) * 1]
        qh, ckT, cvt, cmk, omk = (bb["qh"], bb["ckT"], bb["cvt"], bb["cmk"],
                                  bb["omk"])

        def FW(eng, *keys):
            for k in keys:
                pair = f3.get(k)
                if pair:
                    eng.wait_ge(pair[0].h, pair[1])

        @blk.tensor
        def _(t):
            FW(t, "qhload", f"psc{par}")
            last = None
            for g in range(G):
                last = t.matmul(
                    psc[:, g * 128:(g + 1) * 128],
                    qh[:, g * 1024 + tau * 128:g * 1024 + (tau + 1) * 128],
                    ckT[:, kvh * 128:(kvh + 1) * 128],
                    start=True, stop=True)
            st.inc(last)

        @blk.vector
        def _(v):
            st.wait(v)
            FW(v, f"cs{par}", f"imp{par}")
            v.drain()
            for g in range(G):
                v.scalar_tensor_tensor(
                    cs[:, g * 128:(g + 1) * 128],
                    psc[:, g * 128:(g + 1) * 128], SCALE,
                    cmk[:, tau * 128:(tau + 1) * 128],
                    op0=ALU.mult, op1=ALU.add)
            v.drain()
            v.tensor_tensor(imp, cs[:, 0:128], cs[:, 128:256], op=ALU.add)
            v.drain()
            v.tensor_tensor(imp, imp, cs[:, 256:384], op=ALU.add)
            v.drain()
            v.tensor_tensor(imp, imp, cs[:, 384:512], op=ALU.add)
            v.drain()
            v.memset(imp[:, 127:128], -4000.0)
            v.drain()
            v.tensor_reduce(r1, imp, axis=AX, op=ALU.max, negate=True)
            for g in range(G):
                v.tensor_reduce(rmg[:, g:g + 1], cs[:, g * 128:(g + 1) * 128],
                                axis=AX, op=ALU.max, negate=True)
            v.drain()
            v.tensor_scalar(r1, r1, 0.25, None, op0=ALU.mult)
            sv.inc(v.drain())
        f3[f"psc{par}"] = (sv, sv.n)

        @blk.scalar
        def _(s2):
            sv.wait(s2)
            FW(s2, f"impsm{par}", f"rsg{par}", f"pbfc{par}")
            for g in range(G):
                s2.activation(pcc[:, g * 128:(g + 1) * 128],
                              cs[:, g * 128:(g + 1) * 128], ACT.Exp,
                              bias=rmg[:, g:g + 1],
                              accum_out=rsg[:, g:g + 1])
            sq.inc(s2.activation(impsm, imp, ACT.Exp, scale=0.25,
                                 bias=r1[:, 0:1], accum_out=rrg[:, 0:1]))
        f3[f"cs{par}"] = (sq, sq.n)
        f3[f"imp{par}"] = (sq, sq.n)

        @blk.vector
        def _(v):
            sq.wait(v)
            FW(v, f"maskb{par}")
            v.reciprocal(rrg[:, 0:1], rrg[:, 0:1])
            v.reciprocal(rsg, rsg)
            v.drain()
            v.tensor_scalar(impsm, impsm, rrg[:, 0:1], None, op0=ALU.mult)
            v.drain()
            v.max(out=m8, in_=impsm[:, 0:127])
            v.drain()
            v.memset(m8[:, 4:8], -1.0)
            v.drain()
            v.match_replace(out=repl, in_to_replace=m8,
                            in_values=impsm[:, 0:127], imm_value=-1e30)
            v.drain()
            v.tensor_scalar(fmask[:, 0:127], repl, -1e29, None, op0=ALU.is_lt)
            v.drain()
            v.tensor_scalar(repl, impsm[:, 0:127], 1e-10, None, op0=ALU.is_gt)
            v.drain()
            v.tensor_tensor(fmask[:, 0:127], fmask[:, 0:127], repl,
                            op=ALU.mult)
            v.drain()
            v.tensor_scalar(fmask[:, 0:127], fmask[:, 0:127], 1e30, -1e30,
                            op0=ALU.mult, op1=ALU.add)
            v.memset(fmask[:, 127:128], NEG)
            v.drain()
            KT = 256 * (tau + 1)
            nb = KT // 16
            v.tensor_copy(
                maskb[:, 0:KT].rearrange("p (b s) -> p b s", s=16),
                fmask[:, 0:nb].unsqueeze(2).to_broadcast([128, nb, 16]))
            v.drain()
            v.tensor_tensor(maskb[:, KT - 256:KT], maskb[:, KT - 256:KT],
                            omk[:, tau * 256:(tau + 1) * 256], op=ALU.max)
            sv.inc(v.drain())
        f3[f"impsm{par}"] = (sv, sv.n)
        f3[f"maskb_rdy{par}"] = (sv, sv.n)

        @blk.sync
        def _(s):
            sv.wait(s)
            FW(s, f"pTc_rd{par}")
            for g in range(G):
                ss.dinc(s.dma_start(pTc[:, g * 128:(g + 1) * 128],
                                    pcc[:, g * 128:(g + 1) * 128],
                                    transpose=True))
        f3[f"pbfc{par}"] = (ss, ss.n)
        ss5 = ss.n

        @blk.tensor
        def _(t):
            t.wait_ge(ss.h, ss5)
            FW(t, f"couts{par}")
            last = None
            for g in range(G):
                last = t.matmul(
                    psc[:, g * 128:(g + 1) * 128],
                    pTc[:, g * 128:(g + 1) * 128],
                    cvt[:, kvh * 128:(kvh + 1) * 128],
                    start=True, stop=True)
            st.inc(last)
        f3[f"pTc_rd{par}"] = (st, st.n)

        @blk.vector
        def _(v):
            st.wait(v)
            v.drain()
            sv.inc(v.tensor_copy(couts, psc[:, 0:512]))
        f3[f"psc{par}"] = (sv, sv.n)

    def _p3_fine(self, blk, kvh, tau, bb):
        ss, sa, sv, st, sq = self.ss, self.sa, self.sv, self.st, self.sq
        sc = self.scr
        f3 = self.f3
        par = tau % 2
        KT = 256 * (tau + 1)
        NKC = KT // 128
        NSC = (KT + 511) // 512
        NCH = NSC + 1
        rko = kvh * 2176
        vto = kvh * 18 * 128
        maskb = bb["maskb"][:, par * 2048:par * 2048 + 2048]
        couts = bb["couts"][:, par * 512:(par + 1) * 512]
        comb = bb["comb"][:, par * 512:(par + 1) * 512]
        rsg = bb["rsg"][:, par * 4:(par + 1) * 4]
        rqh, smk, gts, rkp, vtm = (bb["rqh"], bb["smk"], bb["gts"],
                                   bb["rkp"], bb["vtm"])
        scog, slsco, pbfg, pTg = bb["scog"], bb["slsco"], bb["pbfg"], bb["pTg"]
        slpg = bb["slpg"]
        rmF, rsF, rmS, rsS, prd = (bb["rmF"], bb["rsF"], bb["rmS"],
                                   bb["rsS"], bb["prd"])
        psf, pso, ptrs, idb = bb["psf"], bb["pso"], bb["ptr"], bb["idb"]

        def FW(eng, *keys):
            for k in keys:
                pair = f3.get(k)
                if pair:
                    eng.wait_ge(pair[0].h, pair[1])

        sv_base = sv.n

        @blk.tensor
        def _(t):
            FW(t, "psf", f"maskb_rdy{par}")
            c = 0
            for g in range(G):
                lhs_q = rqh[:, g * 1024 + tau * 128:g * 1024 + (tau + 1) * 128]
                for i in range(NCH):
                    ps = psf[c % 2]
                    if c >= 2:
                        t.wait_ge(sv.h, sv_base + (c - 1))
                    if i < NSC:
                        nn = min(512, KT - i * 512)
                        st.inc(t.matmul(
                            ps[:, 0:nn], lhs_q,
                            rkp[:, rko + 64 + i * 512:rko + 64 + i * 512 + nn],
                            start=True, stop=True))
                    else:
                        st.inc(t.matmul(
                            ps[:, 0:384], lhs_q,
                            rkp[:, rko + 2 * tau * 128:rko + 2 * tau * 128 + 384],
                            start=True, stop=True))
                    c += 1

        tb8 = st.n - G * NCH

        @blk.vector
        def _(v):
            FW(v, "scog", "slsco")
            v.drain()
            c = 0
            for g in range(G):
                for i in range(NCH):
                    ps = psf[c % 2]
                    v.wait_ge(st.h, tb8 + c + 1)
                    if i < NSC:
                        nn = min(512, KT - i * 512)
                        sv.inc(v.scalar_tensor_tensor(
                            scog[:, g * 2048 + i * 512:g * 2048 + i * 512 + nn],
                            ps[:, 0:nn], SCALE,
                            maskb[:, i * 512:i * 512 + nn],
                            op0=ALU.mult, op1=ALU.add))
                    else:
                        sv.inc(v.scalar_tensor_tensor(
                            slsco[:, g * 384:(g + 1) * 384],
                            ps[:, 0:384], SCALE,
                            smk[:, tau * 384:(tau + 1) * 384],
                            op0=ALU.mult, op1=ALU.add))
                    c += 1
            v.drain()
            for g in range(G):
                v.tensor_reduce(rmF[:, g:g + 1],
                                scog[:, g * 2048:g * 2048 + KT],
                                axis=AX, op=ALU.max, negate=True)
                v.tensor_reduce(rmS[:, g:g + 1],
                                slsco[:, g * 384:(g + 1) * 384],
                                axis=AX, op=ALU.max, negate=True)
            sv.inc(v.drain())
        f3["psf"] = (sv, sv.n)
        f3[f"maskb{par}"] = (sv, sv.n)
        sv9 = sv.n

        @blk.scalar
        def _(s2):
            s2.wait_ge(sv.h, sv9)
            FW(s2, "pbf_rd", "slp_rd", "rsFS")
            for g in range(G):
                s2.activation(pbfg[:, g * 2048:g * 2048 + KT],
                              scog[:, g * 2048:g * 2048 + KT], ACT.Exp,
                              bias=rmF[:, g:g + 1], accum_out=rsF[:, g:g + 1])
                s2.drain()
                sq.inc(s2.activation(
                    slpg[:, g * 512 + 64:g * 512 + 448],
                    slsco[:, g * 384:(g + 1) * 384], ACT.Exp,
                    bias=rmS[:, g:g + 1], accum_out=rsS[:, g:g + 1]))
        f3["scog"] = (sq, sq.n)
        f3["slsco"] = (sq, sq.n)

        if tau == 0 and kvh == 0:
            @blk.vector
            def _(v):
                for g in range(G):
                    v.memset(slpg[:, g * 512:g * 512 + 64], 0.0)
                    v.memset(slpg[:, g * 512 + 448:(g + 1) * 512], 0.0)
                sv.inc(v.drain())

        sq10 = sq.n - G
        sqe_base = sq.n
        tb12 = st.n
        svB = sv.n

        NTOT = NKC + 4
        tb13s = []

        def inc_T(i):
            return 1 if i == 0 else 2 * i

        def inc_P(i):
            return 2 * i + 3 if i < NTOT - 1 else 2 * NTOT

        @blk.scalar
        def _(s2):
            for g in range(G):
                bg = tb12 + g * 2 * NTOT
                for i in range(NTOT):
                    s2.wait_ge(st.h, bg + inc_T(i))
                    if i >= 4:
                        s2.wait_ge(st.h, bg + inc_P(i - 4))
                    elif g >= 1:
                        s2.wait_ge(st.h, bg - 2 * NTOT + inc_P(NTOT - 4 + i))
                    sq.inc(s2.copy(pTg[:, (i % 4) * 128:(i % 4 + 1) * 128],
                                   ptrs[i % 2][:, :]))

        @blk.tensor
        def _(t):
            FW(t, "pso", "pT_rd")
            t.wait_ge(sv.h, svB)
            for g in range(G):
                seg = (g % 2) * 512
                sqe_g = sqe_base + g * NTOT
                t.wait_ge(sq.h, sq10 + g + 1)
                if g >= 2:
                    t.wait_ge(sv.h, svB + g - 1)

                def srcp_of(i):
                    if i < NKC:
                        return pbfg[:, g * 2048 + i * 128:g * 2048 + (i + 1) * 128]
                    i_ = i - NKC
                    return slpg[:, g * 512 + i_ * 128:g * 512 + (i_ + 1) * 128]

                def pv(i):
                    if i < NKC:
                        return t.matmul(
                            pso[:, seg:seg + 128],
                            pTg[:, (i % 4) * 128:(i % 4 + 1) * 128],
                            vtm[:, vto + (1 + i) * 128:vto + (2 + i) * 128],
                            start=(i == 0), stop=(i == NKC - 1))
                    i_ = i - NKC
                    bi = 2 * tau + i_
                    return t.matmul(
                        pso[:, seg + 128:seg + 256],
                        pTg[:, (i % 4) * 128:(i % 4 + 1) * 128],
                        vtm[:, vto + bi * 128:vto + (bi + 1) * 128],
                        start=(i_ == 0), stop=(i_ == 3))

                for i in range(NTOT):
                    if i >= 2:
                        t.wait_ge(sq.h, sqe_g + i - 1)
                    st.inc(t.transpose(ptrs[i % 2][:, :], srcp_of(i), idb[:, :]))
                    if i >= 1:
                        t.wait_ge(sq.h, sqe_g + i)
                        st.inc(pv(i - 1))
                t.wait_ge(sq.h, sqe_g + NTOT)
                st.inc(pv(NTOT - 1))
        f3["pT_rd"] = (st, st.n)
        f3["pbf_rd"] = (st, st.n)
        f3["slp_rd"] = (st, st.n)
        tb13 = st.n

        @blk.vector
        def _(v):
            FW(v, f"comb{par}")
            v.wait_ge(sq.h, sqe_base)
            v.reciprocal(rsF[:, :], rsF[:, :])
            v.reciprocal(rsS[:, :], rsS[:, :])
            v.drain()
            for g in range(G):
                hd3 = 3 * (kvh * G + g)
                gc = lambda b: gts[:, tau * 96 + hd3 + b:tau * 96 + hd3 + b + 1]
                v.tensor_tensor(prd[:, g:g + 1], gc(0), rsg[:, g:g + 1],
                                op=ALU.mult)
                v.tensor_tensor(prd[:, 4 + g:5 + g], gc(1), rsF[:, g:g + 1],
                                op=ALU.mult)
                v.tensor_tensor(prd[:, 8 + g:9 + g], gc(2), rsS[:, g:g + 1],
                                op=ALU.mult)
            v.drain()
            for g in range(G):
                seg = (g % 2) * 512
                v.wait_ge(st.h, tb13 - (G - 1 - g) * 2 * NTOT)
                v.tensor_scalar(comb[:, g * 128:(g + 1) * 128],
                                couts[:, g * 128:(g + 1) * 128],
                                prd[:, g:g + 1], None, op0=ALU.mult)
                v.drain()
                v.scalar_tensor_tensor(comb[:, g * 128:(g + 1) * 128],
                                       pso[:, seg:seg + 128],
                                       prd[:, 4 + g:5 + g],
                                       comb[:, g * 128:(g + 1) * 128],
                                       op0=ALU.mult, op1=ALU.add)
                v.drain()
                sv.inc(v.scalar_tensor_tensor(
                    comb[:, g * 128:(g + 1) * 128],
                    pso[:, seg + 128:seg + 256], prd[:, 8 + g:9 + g],
                    comb[:, g * 128:(g + 1) * 128],
                    op0=ALU.mult, op1=ALU.add))
                v.drain()
        f3["pso"] = (sv, sv.n)
        f3[f"couts{par}"] = (sv, sv.n)
        f3[f"rsg{par}"] = (sv, sv.n)
        f3["rsFS"] = (sv, sv.n)

        @blk.scalar
        def _(s2):
            sv.wait(s2)
            sa.dinc(s2.dma_start(
                sc["comb_scr"][tau * 128:(tau + 1) * 128,
                               kvh * 512:(kvh + 1) * 512], comb))
            sa.wait(s2)
        f3[f"comb{par}"] = (sa, sa.n)

    def emit_ln_tc(self, blk, make_ybuf, lb, out32_ap, out16_ap):
        sv, sq, sa = self.sv, self.sq, self.sa
        ybuf, sqs = lb["ybuf"], lb["sqs"]
        m, vv, sd, rs = lb["m"], lb["vv"], lb["sd"], lb["rs"]
        ot32, ot16 = lb["ot32"], lb["ot16"]

        @blk.vector
        def _(v):
            make_ybuf(v)
            v.drain()
            v.tensor_reduce(m[:, :], ybuf[:, :], axis=AX, op=ALU.add)
            sv.inc(v.drain())

        @blk.scalar
        def _(s2):
            sv.wait(s2)
            sq.inc(s2.activation(sqs[:, :], ybuf[:, :], ACT.Square,
                                 accum_out=vv[:, 0:1]))

        @blk.vector
        def _(v):
            sq.wait(v)
            v.tensor_scalar(m[:, :], m[:, :], 1.0 / D, None, op0=ALU.mult)
            v.tensor_scalar(vv[:, :], vv[:, :], 1.0 / D, None, op0=ALU.mult)
            v.drain()
            v.tensor_tensor(sd[:, :], m[:, :], m[:, :], op=ALU.mult)
            v.drain()
            v.tensor_tensor(vv[:, :], vv[:, :], sd[:, :], op=ALU.subtract)
            v.drain()
            v.tensor_scalar(vv[:, :], vv[:, :], EPS, None, op0=ALU.add)
            sv.inc(v.drain())

        @blk.scalar
        def _(s2):
            sv.wait(s2)
            sq.inc(s2.activation(sd[:, :], vv[:, :], ACT.Sqrt))

        @blk.vector
        def _(v):
            sq.wait(v)
            v.reciprocal(rs[:, :], sd[:, :])
            v.drain()
            v.tensor_tensor(sd[:, :], m[:, :], rs[:, :], op=ALU.mult)
            v.drain()
            v.tensor_scalar(sd[:, :], sd[:, :], -1.0, None, op0=ALU.mult)
            if out16_ap is not None:
                v.drain()
                v.tensor_scalar(ot16[:, :], ybuf[:, :], m[:, 0:1], rs[:, 0:1],
                                op0=ALU.subtract, op1=ALU.mult)
            sv.inc(v.drain())

        @blk.scalar
        def _(s2):
            sv.wait(s2)
            s2.activation(ot32[:, :], ybuf[:, :], ACT.Identity,
                          scale=rs[:, 0:1], bias=sd[:, 0:1])
            s2.drain()
            sa.dinc(s2.dma_start(out32_ap, ot32[:, :]))
            sa.wait(s2)
            if out16_ap is not None:
                sa.dinc(s2.dma_start(out16_ap, ot16[:, :]))
                sa.wait(s2)

    _ln_ctr = [0]

    def _ln_bufs(self, es):
        nc = self.nc
        self._ln_ctr[0] += 1
        u = f"_{self._ln_ctr[0]}"
        return dict(
            ybuf=es.enter_context(nc.sbuf_tensor("lnyb" + u, [128, D], F32)),
            sqs=es.enter_context(nc.sbuf_tensor("lnsq" + u, [128, D], F32)),
            ot32=es.enter_context(nc.sbuf_tensor("lno3" + u, [128, D], F32)),
            ot16=es.enter_context(nc.sbuf_tensor("lno1" + u, [128, D], BF16)),
            yh=es.enter_context(nc.sbuf_tensor("lnyh" + u, [128, D // 2], F32)),
            xo=es.enter_context(nc.sbuf_tensor("lnxo" + u, [128, D], F32)),
            m=es.enter_context(nc.sbuf_tensor("lnm" + u, [128, 1], F32)),
            vv=es.enter_context(nc.sbuf_tensor("lnv" + u, [128, 1], F32)),
            sd=es.enter_context(nc.sbuf_tensor("lnsd" + u, [128, 1], F32)),
            rs=es.enter_context(nc.sbuf_tensor("lnrs" + u, [128, 1], F32)),
        )

    def p_linear_block(self, tag, xsrc, wsrc, resid, out32_ap, out16_ap,
                       pre_t=False, rhs_cb=None, bias_dram=None,
                       scale_sb=None):
        nc, sc = self.nc, self.scr
        ss, sv, sp, sa = self.ss, self.sv, self.sp, self.sa
        from contextlib import ExitStack
        with ExitStack() as es:
            yacc = es.enter_context(
                nc.sbuf_tensor(f"{tag}ya", [128, NT * 2048], F32))
            with ExitStack() as es2:
                xg = es2.enter_context(
                    nc.sbuf_tensor(f"{tag}xg", [128, 16 * TOK], BF16))
                wgb = es2.enter_context(
                    nc.sbuf_tensor(f"{tag}wg", [128, 16 * 2048], BF16))
                psA = es2.enter_context(
                    nc.psum_tensor(f"{tag}pA", [128, 2048], F32))
                psB = es2.enter_context(
                    nc.psum_tensor(f"{tag}pB", [128, 2048], F32))
                self.tm_gemm(f"{tag}0", xsrc, TOK, wsrc, 0, 2048, yacc,
                             (xg, wgb, psA, psB), pre_t=pre_t, rhs_cb=rhs_cb)
                with nc.Block(f"{tag}dump") as blk:
                    @blk.sync
                    def _(s):
                        sp.wait(s)
                        ss.dinc(s.dma_start(
                            sc["yh_scr"][:, :].rearrange("(n p) c -> p n c", p=128),
                            yacc[:, :].rearrange("p (n c) -> p n c", c=2048)))
                        ss.wait(s)
                self.tm_gemm(f"{tag}1", xsrc, TOK, wsrc, 2048, 2048, yacc,
                             (xg, wgb, psA, psB), pre_t=pre_t, rhs_cb=rhs_cb)

            lb = self._ln_bufs(es)
            bias_sb = None
            if bias_dram is not None:
                bias_sb = es.enter_context(
                    nc.sbuf_tensor(f"{tag}bi", [128, D], F32))
                with nc.Block(f"{tag}bl") as blk:
                    @blk.sync
                    def _(s):
                        ss.dinc(s.dma_start(bias_sb[:, :], bias_dram))
                        ss.wait(s)
            for tc in range(NT):
                with nc.Block(f"{tag}e{tc}") as blk:
                    @blk.sync
                    def _(s):
                        sv.wait(s)
                        self.sq.wait(s)
                        ss.dinc(s.dma_start(
                            lb["ybuf"][:, 0:2048],
                            sc["yh_scr"][tc * 128:(tc + 1) * 128, :]))
                        ss.dinc(s.dma_start(
                            lb["xo"][:, :], resid[tc * 128:(tc + 1) * 128, :]))
                        ss.wait(s)

                    spf = sp.n
                    ssf = ss.n

                    def make_ybuf(v, tc=tc):
                        v.wait_ge(ss.h, ssf)
                        v.wait_ge(sp.h, spf)
                        v.drain()
                        ybuf = lb["ybuf"]
                        v.tensor_copy(ybuf[:, 2048:4096],
                                      yacc[:, tc * 2048:(tc + 1) * 2048])
                        v.drain()
                        if scale_sb is not None:
                            v.tensor_tensor(ybuf[:, :], ybuf[:, :],
                                            scale_sb[:, :], op=ALU.mult)
                            v.drain()
                        if bias_sb is not None:
                            v.tensor_tensor(ybuf[:, :], ybuf[:, :],
                                            bias_sb[:, :], op=ALU.add)
                            v.drain()
                        v.tensor_tensor(ybuf[:, :], ybuf[:, :], lb["xo"][:, :],
                                        op=ALU.add)

                    self.emit_ln_tc(
                        blk, make_ybuf, lb,
                        out32_ap[tc * 128:(tc + 1) * 128, :],
                        None if out16_ap is None
                        else out16_ap[tc * 128:(tc + 1) * 128, :])

    def p5_ffn(self):
        nc, ii, sc = self.nc, self.inp, self.scr
        sg, ss, sa, sv, st, sq, sp = (self.sg, self.ss, self.sa, self.sv,
                                      self.st, self.sq, self.sp)
        from contextlib import ExitStack
        with ExitStack() as es:
            zbf = es.enter_context(nc.sbuf_tensor("f5zb", [128, 8 * TOK], BF16))
            b1s = es.enter_context(nc.sbuf_tensor("f5b1", [128, 2 * NF], F32))
            pfm = es.enter_context(nc.sbuf_tensor("f5p", [128, 8 * TOK], BF16))

            with ExitStack() as es2:
                xfm = es2.enter_context(nc.sbuf_tensor("f5xf", [128, KC * TOK], BF16))
                u1s = es2.enter_context(nc.sbuf_tensor("f5u1", [128, KC * 128], BF16))
                psA = es2.enter_context(nc.psum_tensor("f5pA", [128, 512], F32))
                psB = es2.enter_context(nc.psum_tensor("f5pB", [128, 512], F32))
                with nc.Block("p5ax") as blk:
                    @blk.sync
                    def _(s):
                        for kc in range(KC):
                            ss.dinc(s.dma_start(
                                xfm[:, kc * TOK:(kc + 1) * TOK],
                                sc["resbf"][:, kc * 128:(kc + 1) * 128],
                                transpose=True))
                        ss.dinc(s.dma_start(b1s[:, :], ii["b1c"][:, :]))
                        ss.wait(s)
                ps_base = sp.n
                for rc in range(8):
                    with nc.Block(f"p5ar{rc}") as blk:
                        @blk.gpsimd
                        def _(g):
                            sg.dinc(g.dma_start(
                                u1s[:, :].rearrange("p (k c) -> p k c", c=128),
                                ii["U1"][:, rc * 128:(rc + 1) * 128].rearrange(
                                    "(k p) c -> p k c", p=128)))
                            sg.wait(g)

                        sgf = self.sg.n
                        ssf = ss.n

                        @blk.tensor
                        def _(t):
                            t.wait_ge(sg.h, sgf)
                            t.wait_ge(ss.h, ssf)
                            for th in range(2):
                                r = rc * 2 + th
                                ps = psA if r % 2 == 0 else psB
                                t.wait_ge(sp.h, ps_base + max(0, r - 1))
                                last = None
                                for kc in range(KC):
                                    last = t.matmul(
                                        ps[:, :],
                                        u1s[:, kc * 128:(kc + 1) * 128],
                                        xfm[:, kc * TOK + th * 512:
                                            kc * TOK + (th + 1) * 512],
                                        start=(kc == 0), stop=(kc == KC - 1))
                                st.inc(last)

                        tb = st.n - 2

                        @blk.vector
                        def _(v):
                            v.drain()
                            for th in range(2):
                                r = rc * 2 + th
                                ps = psA if r % 2 == 0 else psB
                                v.wait_ge(st.h, tb + th + 1)
                                sp.inc(v.tensor_copy(
                                    pfm[:, rc * TOK + th * 512:
                                        rc * TOK + (th + 1) * 512], ps[:, :]))

            GSZ = 8
            groups = [list(range(a, min(a + GSZ, NF))) for a in range(0, NF, GSZ)]
            with ExitStack() as es2:
                zsb = es2.enter_context(nc.sbuf_tensor("f5z", [128, 8 * TOK], F32))
                v1G = es2.enter_context(
                    nc.sbuf_tensor("f5v1", [128, 2 * GSZ * 16 * 128], BF16))
                u2G = es2.enter_context(
                    nc.sbuf_tensor("f5u2", [128, 2 * GSZ * 1024], BF16))
                sgb = es2.enter_context(nc.sbuf_tensor("f5sg", [128, 2 * 256], F32))
                upb = es2.enter_context(nc.sbuf_tensor("f5up", [128, 256], F32))
                abf = es2.enter_context(nc.sbuf_tensor("f5ab", [128, 2 * 256], BF16))
                hgP = [es2.enter_context(nc.psum_tensor("f5hA", [128, 256], F32)),
                       es2.enter_context(nc.psum_tensor("f5hC", [128, 256], F32))]
                huP = [es2.enter_context(nc.psum_tensor("f5hB", [128, 256], F32)),
                       es2.enter_context(nc.psum_tensor("f5hD", [128, 256], F32))]
                zps = [es2.enter_context(nc.psum_tensor(f"f5z{b}", [128, 512], F32))
                       for b in range(4)]

                def load_group(g, gi):
                    par = gi % 2
                    swp = self.sw[par]
                    for fj, f in enumerate(groups[gi]):
                        base = par * GSZ * 16 * 128 + fj * 16 * 128
                        swp.dinc(g.dma_start(
                            v1G[:, base:base + 8 * 128].rearrange(
                                "p (k c) -> p k c", c=128),
                            ii["V1"][:, f * 128:(f + 1) * 128].rearrange(
                                "(k p) c -> p k c", p=128)))
                        swp.dinc(g.dma_start(
                            v1G[:, base + 8 * 128:base + 16 * 128].rearrange(
                                "p (k c) -> p k c", c=128),
                            ii["V1"][:, INTER + f * 128:INTER + (f + 1) * 128]
                            .rearrange("(k p) c -> p k c", p=128)))
                        ub = par * GSZ * 1024 + fj * 1024
                        swp.dinc(g.dma_start(
                            u2G[:, ub:ub + 1024],
                            ii["U2"][f * 128:(f + 1) * 128, :]))

                with nc.Block("p5bw") as blk:
                    @blk.gpsimd
                    def _(g):
                        load_group(g, 0)

                for gi, grp in enumerate(groups):
                    ng = len(grp)
                    N = 4 * ng
                    par = gi % 2
                    with nc.Block(f"p5b{gi}") as blk:
                        if gi + 1 < len(groups):
                            stf_pre = st.n

                            @blk.gpsimd
                            def _(g):
                                g.wait_ge(st.h, stf_pre)
                                load_group(g, gi + 1)

                        sw_t = self.sw[par].n
                        st_base = st.n
                        sv_base = sv.n
                        sq_base = sq.n
                        sv_pre = sv.n

                        def inc_H(c):
                            return 1 if c == 0 else 2 * c

                        def inc_Z(c):
                            return 2 * c + 3 if c < N - 1 else 2 * N

                        def act_inc(c):
                            return c + (c // ng) + 1

                        @blk.tensor
                        def _(t):
                            t.wait_ge(self.sw[par].h, sw_t)
                            t.wait_ge(sv.h, sv_pre)

                            def emit_H(c):
                                tt, fj = c // ng, c % ng
                                if c >= 2:
                                    t.wait_ge(sq.h, sq_base + c - 1)
                                    t.wait_ge(sv.h, sv_base + act_inc(c - 2))
                                base = par * GSZ * 16 * 128 + fj * 16 * 128
                                lastm = None
                                for rc in range(8):
                                    lastm = t.matmul(
                                        hgP[c % 2][:, :],
                                        v1G[:, base + rc * 128:base + (rc + 1) * 128],
                                        pfm[:, rc * TOK + tt * 256:
                                            rc * TOK + (tt + 1) * 256],
                                        start=(rc == 0), stop=(rc == 7))
                                for rc in range(8):
                                    lastm = t.matmul(
                                        huP[c % 2][:, :],
                                        v1G[:, base + (8 + rc) * 128:
                                            base + (9 + rc) * 128],
                                        pfm[:, rc * TOK + tt * 256:
                                            rc * TOK + (tt + 1) * 256],
                                        start=(rc == 0), stop=(rc == 7))
                                st.inc(lastm)

                            def emit_Z(c):
                                tt, fj = c // ng, c % ng
                                t.wait_ge(sv.h, sv_base + act_inc(c))
                                ub = par * GSZ * 1024 + (c % ng) * 1024
                                lastm = None
                                for rc in range(8):
                                    b, half = rc // 2, rc % 2
                                    lastm = t.matmul(
                                        zps[b][:, half * 256:(half + 1) * 256],
                                        u2G[:, ub + rc * 128:ub + (rc + 1) * 128],
                                        abf[:, (c % 2) * 256:(c % 2) * 256 + 256],
                                        start=(fj == 0 and half == 0),
                                        stop=(fj == ng - 1 and half == 1))
                                st.inc(lastm)

                            for c in range(N):
                                emit_H(c)
                                if c >= 1:
                                    emit_Z(c - 1)
                            emit_Z(N - 1)

                        @blk.scalar
                        def _(s2):
                            for c in range(N):
                                tt, fj = c // ng, c % ng
                                f = grp[fj]
                                s2.wait_ge(st.h, st_base + inc_H(c))
                                if c >= 2:
                                    s2.wait_ge(sv.h, sv_base + act_inc(c - 2))
                                sq.inc(s2.activation(
                                    sgb[:, (c % 2) * 256:(c % 2) * 256 + 256],
                                    hgP[c % 2][:, :], ACT.Silu,
                                    bias=b1s[:, f:f + 1]))

                        @blk.vector
                        def _(v):
                            v.drain()
                            for c in range(N):
                                tt, fj = c // ng, c % ng
                                f = grp[fj]
                                v.wait_ge(st.h, st_base + inc_H(c))
                                v.wait_ge(sq.h, sq_base + c + 1)
                                if c >= 2:
                                    v.wait_ge(st.h, st_base + inc_Z(c - 2))
                                v.tensor_scalar(upb[:, :], huP[c % 2][:, :],
                                                b1s[:, NF + f:NF + f + 1],
                                                None, op0=ALU.add)
                                v.drain()
                                sv.inc(v.tensor_tensor(
                                    abf[:, (c % 2) * 256:(c % 2) * 256 + 256],
                                    sgb[:, (c % 2) * 256:(c % 2) * 256 + 256],
                                    upb[:, :], op=ALU.mult))
                                v.drain()
                                if fj == ng - 1:
                                    tt_ = tt
                                    v.wait_ge(st.h, st_base + inc_Z(c))
                                    for b in range(4):
                                        for half in range(2):
                                            rc = b * 2 + half
                                            dst = zsb[:, rc * TOK + tt_ * 256:
                                                      rc * TOK + (tt_ + 1) * 256]
                                            srcp = zps[b][:, half * 256:
                                                          (half + 1) * 256]
                                            if gi == 0:
                                                v.tensor_copy(dst, srcp)
                                            else:
                                                v.tensor_tensor(dst, dst, srcp,
                                                                op=ALU.add)
                                    sv.inc(v.drain())

                with nc.Block("p5zc") as blk:
                    svf = sv.n

                    @blk.vector
                    def _(v):
                        v.wait_ge(sv.h, svf)
                        v.drain()
                        v.tensor_copy(zbf[:, :], zsb[:, :])
                        sv.inc(v.drain())

            with ExitStack() as es2:
                v2s = es2.enter_context(nc.sbuf_tensor("f5v2", [128, 8 * D], BF16))
                b2s = es2.enter_context(nc.sbuf_tensor("f5b2", [128, D], F32))
                ps8 = es2.enter_context(nc.psum_tensor("f5p8", [128, D], F32))
                lb = self._ln_bufs(es2)
                with nc.Block("p5cl") as blk:
                    @blk.gpsimd
                    def _(g):
                        for rc in range(8):
                            sg.dinc(g.dma_start(
                                v2s[:, rc * D:(rc + 1) * D],
                                ii["V2"][rc * 128:(rc + 1) * 128, :]))
                        sg.wait(g)

                    @blk.sync
                    def _(s):
                        ss.dinc(s.dma_start(b2s[:, :], ii["b2b"][:, :]))
                        ss.wait(s)

                for tc in range(NT):
                    with nc.Block(f"p5ct{tc}") as blk:
                        svf = sv.n
                        sgf = sg.n

                        @blk.sync
                        def _(s):
                            s.wait_ge(sv.h, svf)
                            ss.dinc(s.dma_start(
                                lb["xo"][:, :],
                                sc["res32"][tc * 128:(tc + 1) * 128, :]))
                            ss.wait(s)

                        ssf = ss.n

                        @blk.tensor
                        def _(t):
                            t.wait_ge(sv.h, svf)
                            t.wait_ge(sg.h, sgf)
                            t.wait_ge(ss.h, ssf)
                            last = None
                            for rc in range(8):
                                for nt_ in range(8):
                                    last = t.matmul(
                                        ps8[:, nt_ * 512:(nt_ + 1) * 512],
                                        zbf[:, rc * TOK + tc * 128:
                                            rc * TOK + (tc + 1) * 128],
                                        v2s[:, rc * D + nt_ * 512:
                                            rc * D + (nt_ + 1) * 512],
                                        start=(rc == 0), stop=(rc == 7))
                            st.inc(last)

                        stf = st.n

                        def make_ybuf(v, tc=tc, stf=stf):
                            v.wait_ge(st.h, stf)
                            v.drain()
                            v.tensor_tensor(lb["ybuf"][:, :], ps8[:, :],
                                            b2s[:, :], op=ALU.add)
                            v.drain()
                            v.tensor_tensor(lb["ybuf"][:, :], lb["ybuf"][:, :],
                                            lb["xo"][:, :], op=ALU.add)

                        self.emit_ln_tc(
                            blk, make_ybuf, lb,
                            sc["res32"][tc * 128:(tc + 1) * 128, :],
                            sc["resbf"][tc * 128:(tc + 1) * 128, :])

    def _p5_fchunk(self, blk, gi, tt, fj, f, ng, pfm, v1s, u2s, b1s,
                   sgb, upb, abf, hgA, huA, zps):
        nc, ii, sc = self.nc, self.inp, self.scr
        sg, ss, sv, st, sq = self.sg, self.ss, self.sv, self.st, self.sq

        @blk.gpsimd
        def _(g):
            st.wait(g)
            sg.dinc(g.dma_start(
                v1s[:, 0:8 * 128].rearrange("p (k c) -> p k c", c=128),
                ii["V1"][:, f * 128:(f + 1) * 128].rearrange(
                    "(k p) c -> p k c", p=128)))
            sg.dinc(g.dma_start(
                v1s[:, 8 * 128:16 * 128].rearrange("p (k c) -> p k c", c=128),
                ii["V1"][:, INTER + f * 128:INTER + (f + 1) * 128].rearrange(
                    "(k p) c -> p k c", p=128)))
            sg.dinc(g.dma_start(u2s[:, :], ii["U2"][f * 128:(f + 1) * 128, :]))
            sg.wait(g)

        sgf = sg.n
        sqf_pre = sq.n
        svf_pre0 = sv.n

        @blk.tensor
        def _(t):
            t.wait_ge(sg.h, sgf)
            t.wait_ge(sq.h, sqf_pre)
            t.wait_ge(sv.h, svf_pre0)
            last = None
            for rc in range(8):
                last = t.matmul(hgA[:, :], v1s[:, rc * 128:(rc + 1) * 128],
                                pfm[:, rc * TOK + tt * 256:rc * TOK + (tt + 1) * 256],
                                start=(rc == 0), stop=(rc == 7))
            for rc in range(8):
                last = t.matmul(huA[:, :], v1s[:, (8 + rc) * 128:(9 + rc) * 128],
                                pfm[:, rc * TOK + tt * 256:rc * TOK + (tt + 1) * 256],
                                start=(rc == 0), stop=(rc == 7))
            st.inc(last)

        stf = st.n

        @blk.scalar
        def _(s2):
            st.wait(s2)
            s2.activation(sgb[:, :], hgA[:, :], ACT.Silu,
                          bias=b1s[:, f:f + 1])
            sq.inc(s2.drain())

        sqf = sq.n
        svf_pre = sv.n

        @blk.vector
        def _(v):
            v.wait_ge(st.h, stf)
            v.wait_ge(sq.h, sqf)
            v.drain()
            v.tensor_scalar(upb[:, :], huA[:, :], b1s[:, NF + f:NF + f + 1],
                            None, op0=ALU.add)
            v.drain()
            v.tensor_tensor(abf[:, :], sgb[:, :], upb[:, :], op=ALU.mult)
            sv.inc(v.drain())

        svf = sv.n

        @blk.tensor
        def _(t):
            t.wait_ge(sv.h, svf)
            last = None
            for rc in range(8):
                b, half = rc // 2, rc % 2
                last = t.matmul(
                    zps[b][:, half * 256:(half + 1) * 256],
                    u2s[:, rc * 128:(rc + 1) * 128], abf[:, :],
                    start=(fj == 0 and half == 0),
                    stop=(fj == ng - 1 and half == 1))
            st.inc(last)

    def p6_maskprep(self, bmx):
        nc, ii, sc = self.nc, self.inp, self.scr
        ss, sv, st, sa = self.ss, self.sv, self.st, self.sa
        from contextlib import ExitStack
        with ExitStack() as es:
            msb = es.enter_context(nc.sbuf_tensor("m6i", [128, D], I32))
            rsf = es.enter_context(nc.sbuf_tensor("m6g", [128, 128], F32))
            rsb = es.enter_context(nc.sbuf_tensor("m6b", [128, 128], BF16))
            rsbT = es.enter_context(nc.sbuf_tensor("m6t", [128, 128], BF16))
            bmT = es.enter_context(nc.sbuf_tensor("m6m", [128, 128], F32))

            for i in range(32):
                with nc.Block(f"p6mt{i}") as blk:
                    svf = sv.n
                    ssf0 = ss.n

                    @blk.sync
                    def _(s):
                        s.wait_ge(sv.h, svf)
                        ss.dinc(s.dma_start(
                            msb[:, :], ii["l1_mask"][i * 128:(i + 1) * 128, :]))
                        ss.wait(s)

                    ssf = ss.n

                    @blk.vector
                    def _(v):
                        v.wait_ge(ss.h, ssf)
                        v.drain()
                        v.tensor_reduce(
                            rsf[:, :],
                            msb[:, :].rearrange("p (a b) -> p a b", b=32),
                            axis=AX, op=ALU.add)
                        v.drain()
                        v.tensor_copy(rsb[:, :], rsf[:, :])
                        sv.inc(v.drain())

                    @blk.sync
                    def _(s):
                        sv.wait(s)
                        ss.dinc(s.dma_start(rsbT[:, :], rsb[:, :],
                                            transpose=True))
                        ss.wait(s)

                    ssf2 = ss.n

                    @blk.vector
                    def _(v):
                        v.wait_ge(ss.h, ssf2)
                        v.drain()
                        v.tensor_reduce(
                            bmT[:, i * 4:(i + 1) * 4],
                            rsbT[:, :].rearrange("p (a b) -> p a b", b=32),
                            axis=AX, op=ALU.add)
                        sv.inc(v.drain())

            with nc.Block("p6mf") as blk:
                svf = sv.n

                @blk.vector
                def _(v):
                    v.wait_ge(sv.h, svf)
                    v.drain()
                    v.tensor_scalar(bmT[:, :], bmT[:, :], 0.0, None,
                                    op0=ALU.is_gt)
                    sv.inc(v.drain())

                @blk.sync
                def _(s):
                    sv.wait(s)
                    ss.dinc(s.dma_start(sc["bm_scr"][0:128, 0:128], bmT[:, :]))
                    ss.wait(s)
                    for kc in range(KC):
                        for pa in range(4):
                            kb = kc * 4 + pa
                            dst = bmx[pa * 32:(pa + 1) * 32,
                                      kc * 128:(kc + 1) * 128]
                            srcr = sc["bm_scr"][kb:kb + 1, 0:128]
                            ss.dinc(s.dma_start(
                                dst, srcr.to_broadcast([32, 128])))
                    ss.wait(s)

    def p7_prep(self, s_bcast):
        nc, ii, sc = self.nc, self.inp, self.scr
        sg, ss, sv, st, sq, sa = (self.sg, self.ss, self.sv, self.st,
                                  self.sq, self.sa)
        from contextlib import ExitStack
        with ExitStack() as es:
            a_sb = es.enter_context(nc.sbuf_tensor("d7a", [32, D], BF16))
            bT = es.enter_context(nc.sbuf_tensor("d7b", [128, D], BF16))
            wt = es.enter_context(nc.sbuf_tensor("d7w", [128, D], BF16))
            et = es.enter_context(nc.sbuf_tensor("d7e", [128, D], BF16))
            sqb = es.enter_context(nc.sbuf_tensor("d7s", [128, D], BF16))
            ones = es.enter_context(nc.sbuf_tensor("d7o", [128, 1], BF16))
            cnr = es.enter_context(nc.sbuf_tensor("d7c", [1, D], F32))
            mag = es.enter_context(nc.sbuf_tensor("d7m", [1, D], F32))

            with nc.Block("p7a") as blk:
                @blk.gpsimd
                def _(g):
                    sg.dinc(g.dma_start(a_sb[:, :], ii["lf_A"][:, :]))
                    sg.wait(g)

                saf = sa.n + 16

                @blk.sync
                def _(s):
                    s.wait_ge(sa.h, saf)
                    ss.dinc(s.dma_start(bT[:, :], sc["lfB_bf"][:, :],
                                        transpose=True))
                    ss.dinc(s.dma_start(mag[:, :], ii["lf_magb"][:, :]))
                    ss.wait(s)

                @blk.vector
                def _(v):
                    v.memset(ones[:, :], 1.0)
                    v.memset(sqb[:, 0:96], 0.0)
                    sv.inc(v.drain())

                @blk.scalar
                def _(s2):
                    sv.wait(s2)
                    sa.dinc(s2.dma_start(
                        sc["lfB_bf"][:, 32:128].rearrange(
                            "(a p) c -> p a c", p=128),
                        sqb[:, 0:96].unsqueeze(1).to_broadcast([128, 32, 96])))
                    sa.wait(s2)

            es1 = ExitStack()
            ps8 = es1.enter_context(nc.psum_tensor("d7p", [128, D], F32))
            for kc in range(KC):
                with nc.Block(f"p7b{kc}") as blk:
                    svf = sv.n

                    @blk.sync
                    def _(s):
                        s.wait_ge(sv.h, svf)
                        ss.dinc(s.dma_start(
                            wt[:, :], sc["lfW_bf"][:, kc * 128:(kc + 1) * 128],
                            transpose=True))
                        ss.wait(s)

                    ssf = ss.n
                    sgf = sg.n

                    @blk.tensor
                    def _(t):
                        t.wait_ge(ss.h, ssf)
                        t.wait_ge(sg.h, sgf)
                        t.wait_ge(sv.h, svf)
                        last = None
                        for nt_ in range(8):
                            last = t.matmul(
                                ps8[:, nt_ * 512:(nt_ + 1) * 512],
                                a_sb[:, kc * 128:(kc + 1) * 128],
                                bT[0:32, nt_ * 512:(nt_ + 1) * 512],
                                start=True, stop=True)
                        st.inc(last)

                    stf = st.n

                    @blk.vector
                    def _(v):
                        v.wait_ge(st.h, stf)
                        v.drain()
                        v.tensor_tensor(et[:, :], wt[:, :], ps8[:, 0:D],
                                        op=ALU.add)
                        sv.inc(v.drain())

                    @blk.scalar
                    def _(s2):
                        sv.wait(s2)
                        sa.dinc(s2.dma_start(
                            sc["effW_scr"][kc * 128:(kc + 1) * 128, :], et[:, :]))
                        sa.wait(s2)

            es1.close()
            with ExitStack() as es2:
                pscn = es2.enter_context(nc.psum_tensor("d7n", [1, D], F32))
                for kc in range(KC):
                    with nc.Block(f"p7c{kc}") as blk:
                        svf = sv.n

                        @blk.sync
                        def _(s):
                            s.wait_ge(sv.h, svf)
                            ss.dinc(s.dma_start(
                                et[:, :],
                                sc["effW_scr"][kc * 128:(kc + 1) * 128, :]))
                            ss.wait(s)

                        ssf = ss.n

                        @blk.vector
                        def _(v):
                            v.wait_ge(ss.h, ssf)
                            v.drain()
                            v.tensor_tensor(sqb[:, :], et[:, :], et[:, :],
                                            op=ALU.mult)
                            sv.inc(v.drain())

                        svf2 = sv.n

                        @blk.tensor
                        def _(t):
                            t.wait_ge(sv.h, svf2)
                            last = None
                            for nt_ in range(8):
                                last = t.matmul(
                                    pscn[:, nt_ * 512:(nt_ + 1) * 512],
                                    ones[:, :],
                                    sqb[:, nt_ * 512:(nt_ + 1) * 512],
                                    start=(kc == 0), stop=(kc == KC - 1))
                            st.inc(last)

                with nc.Block("p7d") as blk:
                    stf = st.n

                    @blk.vector
                    def _(v):
                        v.wait_ge(st.h, stf)
                        v.drain()
                        v.tensor_copy(cnr[:, :], pscn[:, 0:D])
                        sv.inc(v.drain())

                    @blk.scalar
                    def _(s2):
                        sv.wait(s2)
                        sq.inc(s2.activation(cnr[:, :], cnr[:, :], ACT.Sqrt))

                    @blk.vector
                    def _(v):
                        sq.wait(v)
                        v.reciprocal(cnr[:, :], cnr[:, :])
                        v.drain()
                        v.tensor_tensor(cnr[:, :], cnr[:, :], mag[:, :],
                                        op=ALU.mult)
                        sv.inc(v.drain())

                    @blk.sync
                    def _(s):
                        sv.wait(s)
                        ss.dinc(s.dma_start(sc["row_scr"][:, :], cnr[:, :]))
                        ss.wait(s)
                        ss.dinc(s.dma_start(
                            s_bcast[:, :],
                            sc["row_scr"][0:1, :].to_broadcast([128, D])))
                        ss.wait(s)


def _build(upto=100):
    p = P(upto)
    nc = p.nc
    sc = p.scr
    p.p0_casts()
    if upto >= 1:
        p.p1_qkv()
    if upto >= 2:
        with (
            nc.sbuf_tensor("rkp", [128, KVH * 2176], BF16) as rkp,
            nc.sbuf_tensor("vtm", [128, KVH * 18 * 128], BF16) as vtm,
            nc.sbuf_tensor("ckT", [128, KVH * 128], BF16) as ckT,
            nc.sbuf_tensor("cvt", [128, KVH * 128], BF16) as cvt,
            nc.sbuf_tensor("gts", [128, NT * 96], F32) as gts,
        ):
            p.p2_prep(rkp, vtm, ckT, cvt, gts)
            if upto >= 3:
                p.p3_attn(rkp, vtm, ckT, cvt, gts)
    if upto >= 4:
        p.p_linear_block("p4", sc["comb_scr"], sc["wo_bf"],
                         p.inp["x_own"][:, :], sc["res32"], sc["resbf"])
    if upto >= 5:
        p.p5_ffn()
    if upto >= 6:
        with nc.sbuf_tensor("bmx", [128, KC * 128], F32) as bmx:
            p.p6_maskprep(bmx)

            def mask_cb(v, wgb, j, kc, n0, bmx=bmx):
                nb0 = n0 // 32
                nbs = 2048 // 32
                sl = wgb[:, j * 2048:(j + 1) * 2048].rearrange(
                    "p (a b) -> p a b", b=32)
                mk = bmx[:, kc * 128 + nb0:kc * 128 + nb0 + nbs]
                v.tensor_tensor(sl, sl, mk.unsqueeze(2).to_broadcast(
                    [128, nbs, 32]), op=ALU.mult)

            p.p_linear_block("p6", sc["resbf"], sc["l1w_bf"],
                             sc["res32"][:, :], sc["res32"], sc["resbf"],
                             rhs_cb=mask_cb, bias_dram=p.inp["l1_biasb"][:, :])
    if upto >= 7:
        with nc.sbuf_tensor("sbc", [128, D], F32) as s_bcast:
            p.p7_prep(s_bcast)
            p.p_linear_block("p7", sc["resbf"], sc["effW_scr"],
                             sc["res32"][:, :], p.y_out, None,
                             pre_t=True, scale_sb=s_bcast)
    return p


def _stage_inputs(inputs):
    x = np.ascontiguousarray(inputs["L0_attn_hidden_states"])
    g = lambda n: np.ascontiguousarray(inputs[n])
    mem_kv = g("L0_attn_mem_kv")
    shared = {
        "w_qkv": g("L0_attn_w_qkv"), "w_o": g("L0_attn_w_o"),
        "k_posT": np.ascontiguousarray(
            inputs["L0_attn_k_pos"].transpose(2, 0, 1).reshape(DH, KVH * CB)),
        "v_posT": np.ascontiguousarray(
            inputs["L0_attn_v_pos"].transpose(2, 0, 1).reshape(DH, KVH * CB)),
        "k_w1": g("L0_attn_k_w1"),
        "k_b1c": np.ascontiguousarray(
            inputs["L0_attn_k_b1"].reshape(16, 128).T),
        "k_w2": g("L0_attn_k_w2"),
        "k_b2c": np.ascontiguousarray(inputs["L0_attn_k_b2"].reshape(128, 1)),
        "v_w1": g("L0_attn_v_w1"),
        "v_b1c": np.ascontiguousarray(
            inputs["L0_attn_v_b1"].reshape(16, 128).T),
        "v_w2": g("L0_attn_v_w2"),
        "v_b2b": np.ascontiguousarray(np.broadcast_to(
            inputs["L0_attn_v_b2"][None, :], (128, DH))),
        "mem_kT": np.ascontiguousarray(mem_kv[0, :, 0, :].T),
        "mem_vb": np.ascontiguousarray(mem_kv[1, :, 0, :].reshape(1, KVH * DH)),
        "w_gate": g("L0_attn_w_gate"),
        "b_gateb": np.ascontiguousarray(np.broadcast_to(
            inputs["L0_attn_b_gate"][None, :], (128, 96))),
        "U1": g("L0_ffn_U1"), "V1": g("L0_ffn_V1"),
        "U2": g("L0_ffn_U2"), "V2": g("L0_ffn_V2"),
        "b1c": np.ascontiguousarray(
            inputs["L0_ffn_b1"].reshape(2 * NF, 128).T),
        "b2b": np.ascontiguousarray(np.broadcast_to(
            inputs["L0_ffn_b2"][None, :], (128, D))),
        "l1_mask": g("L1_attn_mask"), "l1_w": g("L1_attn_weight"),
        "l1_biasb": np.ascontiguousarray(np.broadcast_to(
            inputs["L1_attn_bias"][None, :], (128, D))),
        "lf_W": g("L1_ffn_W"), "lf_A": g("L1_ffn_lora_A"),
        "lf_B": g("L1_ffn_lora_B"),
        "lf_magb": np.ascontiguousarray(
            inputs["L1_ffn_magnitude"].reshape(1, D)),
        "e4": np.ascontiguousarray(
            (np.arange(128)[:, None] // 32 == np.arange(4)[None, :])
            .astype(np.float32)),
        "ident": np.eye(128, dtype=np.float32),
    }
    tabs = [_host_tables(0), _host_tables(1)]
    in_maps = []
    own_rows = []
    for c in range(N_CORES):
        b, h = c // 2, c % 2
        rows = np.concatenate(
            [np.arange(j * 128, (j + 1) * 128) for j in range(h, NTF, 2)])
        own_rows.append(rows)
        m = dict(shared)
        m["x_full"] = np.ascontiguousarray(x[b])
        m["x_own"] = np.ascontiguousarray(x[b][rows])
        for k, v in tabs[h].items():
            m[k] = v
        in_maps.append(m)
    return in_maps, own_rows


def kernel(**inputs):
    in_maps, own_rows = _stage_inputs(inputs)
    p = _build()
    res = run_bass_kernel_spmd(p.nc, in_maps, core_ids=list(range(N_CORES)))
    out = np.zeros((B, S, D), dtype=np.float32)
    for c in range(N_CORES):
        out[c // 2, own_rows[c]] = res.results[c]["y_out"]
    return out


# version 14
# speedup vs baseline: 1.0054x; 1.0054x over previous
import numpy as np
import concourse.bass as bass
import concourse.mybir as mybir
from concourse.bass_utils import run_bass_kernel_spmd

F32 = mybir.dt.float32
BF16 = mybir.dt.bfloat16
I32 = mybir.dt.int32
AX = mybir.AxisListType.X
ALU = mybir.AluOpType
ACT = mybir.ActivationFunctionType

NEG = -1.0e30
B, S, D = 4, 2048, 4096
H, KVH, DH = 32, 8, 128
G = H // KVH
CB = 16
WIN = 64
TOK = 1024
NT = TOK // 128
NTF = S // 128
KC = D // 128
INTER = 11008
RANK = 1024
NF = INTER // 128
SCALE = float(DH) ** -0.5
EPS = 1e-5
N_CORES = 8



def _host_tables(h):
    t_abs = np.zeros(TOK, dtype=np.int64)
    for tau in range(NT):
        t_abs[tau * 128:(tau + 1) * 128] = (2 * tau + h) * 128 + np.arange(128)

    inv = 1.0 / (10000.0 ** (np.arange(0, DH, 2, dtype=np.float64) / DH))
    ang_own = t_abs[:, None].astype(np.float64) * inv[None, :]
    ang_all = np.arange(S, dtype=np.float64)[:, None] * inv[None, :]

    cm = np.full((TOK, 128), NEG, dtype=np.float32)
    cm[:, 127] = 0.0
    ends = np.arange(127) * CB + CB - 1
    cm[:, 0:127] = np.where(ends[None, :] < t_abs[:, None], 0.0, NEG)

    own = np.full((TOK, 256), NEG, dtype=np.float32)
    for i in range(TOK):
        a = int(t_abs[i])
        tau = i // 128
        base = 2 * tau * 128
        b0 = (a // CB) * CB
        own[i, b0 - base:a - base + 1] = 0.0

    sl = np.full((TOK, 384), NEG, dtype=np.float32)
    ar = np.arange(384)
    for i in range(TOK):
        a = int(t_abs[i])
        tau = i // 128
        ks = 2 * tau * 128 - 64 + ar
        vis = (ks >= a - WIN) & (ks <= a) & (ks >= 0)
        sl[i, vis] = 0.0

    return dict(
        cosq=np.cos(ang_own).astype(np.float32),
        sinq=np.sin(ang_own).astype(np.float32),
        cosk=np.cos(ang_all).astype(np.float32),
        sink=np.sin(ang_all).astype(np.float32),
        cmask=cm, ownmask=own, slmask=sl)



class Sem:
    def __init__(self, nc, name):
        self.h = nc.semaphore(name).__enter__()
        self.n = 0

    def inc(self, instr, k=1):
        instr.then_inc(self.h, k)
        self.n += k
        return instr

    def dinc(self, instr):
        return self.inc(instr, 16)

    def wait(self, eng, val=None):
        eng.wait_ge(self.h, self.n if val is None else val)


class P:

    def __init__(self, upto=100):
        self.upto = upto
        nc = bass.Bass("TRN2", target_bir_lowering=False,
                       detect_race_conditions=False)
        self.nc = nc
        self.inp = {}
        self.scr = {}
        self._declare()
        self.sg = Sem(nc, "sg")
        self.ss = Sem(nc, "ss")
        self.sa = Sem(nc, "sa")
        self.st = Sem(nc, "st")
        self.sv = Sem(nc, "sv")
        self.sq = Sem(nc, "sq")
        self.sp = Sem(nc, "sp")
        self.sw = [Sem(nc, "sw0"), Sem(nc, "sw1")]
        self.sgp = Sem(nc, "sgp")
        self.sk = [Sem(nc, "sk0"), Sem(nc, "sk1")]
        self.sx = [Sem(nc, "sx0"), Sem(nc, "sx1")]

    def _declare(self):
        nc = self.nc
        ii = self.inp

        def inp(name, shape, dtype=F32):
            ii[name] = nc.dram_tensor(name, list(shape), dtype,
                                      kind="ExternalInput").ap()

        inp("x_full", (S, D)); inp("x_own", (TOK, D))
        inp("w_qkv", (6144, D)); inp("w_o", (D, D))
        inp("k_posT", (DH, KVH * CB)); inp("v_posT", (DH, KVH * CB))
        inp("k_w1", (2048, 2048)); inp("k_b1c", (128, 16))
        inp("k_w2", (DH, 2048)); inp("k_b2c", (128, 1))
        inp("v_w1", (2048, 2048)); inp("v_b1c", (128, 16))
        inp("v_w2", (DH, 2048)); inp("v_b2b", (128, DH))
        inp("mem_kT", (DH, KVH)); inp("mem_vb", (1, KVH * DH))
        inp("w_gate", (96, D)); inp("b_gateb", (128, 96))
        inp("U1", (D, RANK)); inp("V1", (RANK, 2 * INTER))
        inp("U2", (INTER, RANK)); inp("V2", (RANK, D))
        inp("b1c", (128, 2 * NF)); inp("b2b", (128, D))
        inp("l1_mask", (D, D), I32); inp("l1_w", (D, D))
        inp("l1_biasb", (128, D))
        inp("lf_W", (D, D)); inp("lf_A", (32, D)); inp("lf_B", (D, 32))
        inp("lf_magb", (1, D)); inp("e4", (128, 4)); inp("ident", (128, 128))
        inp("cosq", (TOK, 64)); inp("sinq", (TOK, 64))
        inp("cosk", (S, 64)); inp("sink", (S, 64))
        inp("cmask", (TOK, 128)); inp("ownmask", (TOK, 256))
        inp("slmask", (TOK, 384))

        self.y_out = nc.dram_tensor("y_out", [TOK, D], F32,
                                    kind="ExternalOutput").ap()

        sc = self.scr

        def scratch(name, shape, dtype=BF16):
            sc[name] = nc.dram_tensor(name, list(shape), dtype,
                                      kind="Internal").ap()

        scratch("wq_bf", (6144, D)); scratch("wo_bf", (D, D))
        scratch("kw1_bf", (2048, 2048)); scratch("vw1_bf", (2048, 2048))
        scratch("kw2_bf", (DH, 2048)); scratch("vw2_bf", (DH, 2048))
        scratch("wg_bf", (96, D)); scratch("l1w_bf", (D, D))
        scratch("lfW_bf", (D, D)); scratch("lfB_bf", (D, 128))
        scratch("x_bf", (S, D)); scratch("xown_bf", (TOK, D))
        scratch("k_scr", (S, KVH * DH)); scratch("rk_scr", (S, KVH * DH))
        scratch("v_scr", (S, KVH * DH))
        scratch("q_scr", (TOK, D)); scratch("rq_scr", (TOK, D))
        scratch("comb_scr", (TOK, D))
        scratch("yh_scr", (TOK, D // 2), F32)
        scratch("res32", (TOK, D), F32)
        scratch("resbf", (TOK, D))
        scratch("effW_scr", (D, D))
        scratch("bm_scr", (128, 128), F32)
        scratch("row_scr", (1, D), F32)
        scratch("gates_scr", (128, NT * 96), F32)

    def p0_casts(self):
        nc, ii, sc = self.nc, self.inp, self.scr
        sg = self.sg
        ss, sa, sv = self.ss, self.sa, self.sv
        sw0 = self.sw[0]
        with (
            nc.sbuf_tensor("p0xa", [128, 2 * D], F32) as xstg,
            nc.sbuf_tensor("p0xb", [128, 2 * D], BF16) as xbst,
            nc.Block("p0") as blk,
        ):
            @blk.gpsimd
            def _(g):
                sg.dinc(g.dma_start(sc["wq_bf"][4096:6144, :],
                                    ii["w_qkv"][4096:6144, :]))
                sg.wait(g)

            def _waves23(g):
                sw0.dinc(g.dma_start(sc["xown_bf"][:, :], ii["x_own"][:, :]))
                sw0.dinc(g.dma_start(sc["wq_bf"][0:4096, :],
                                     ii["w_qkv"][0:4096, :]))
                sw0.dinc(g.dma_start(sc["wg_bf"][:, :], ii["w_gate"][:, :]))
                self._p0_wave2 = sw0.n
                for dst, src in [
                    ("wo_bf", "w_o"), ("kw1_bf", "k_w1"), ("vw1_bf", "v_w1"),
                    ("kw2_bf", "k_w2"), ("vw2_bf", "v_w2"),
                    ("l1w_bf", "l1_w"), ("lfW_bf", "lf_W"),
                ]:
                    sg.dinc(g.dma_start(sc[dst][:, :], ii[src][:, :]))
                sg.dinc(g.dma_start(sc["lfB_bf"][:, 0:32], ii["lf_B"][:, :]))
                self._p0_all = sg.n
            self._waves23 = _waves23

            sv_b = sv.n
            sk_b = [self.sk[0].n, self.sk[1].n]
            sx_b = [self.sx[0].n, self.sx[1].n]

            @blk.sync
            def _(s):
                for tc in range(NTF):
                    par = tc % 2
                    if tc >= 2:
                        s.wait_ge(sv.h, sv_b + tc - 1)
                    self.sk[par].dinc(s.dma_start(
                        xstg[:, par * D:(par + 1) * D],
                        ii["x_full"][tc * 128:(tc + 1) * 128, :]))

            @blk.vector
            def _(v):
                for tc in range(NTF):
                    par = tc % 2
                    v.wait_ge(self.sk[par].h, sk_b[par] + 16 * (tc // 2 + 1))
                    if tc >= 2:
                        v.wait_ge(self.sx[par].h, sx_b[par] + 16 * (tc // 2))
                    sv.inc(v.tensor_copy(xbst[:, par * D:(par + 1) * D],
                                         xstg[:, par * D:(par + 1) * D]))

            @blk.scalar
            def _(s2):
                for tc in range(NTF):
                    par = tc % 2
                    s2.wait_ge(sv.h, sv_b + tc + 1)
                    self.sx[par].dinc(s2.dma_start(
                        sc["x_bf"][tc * 128:(tc + 1) * 128, :],
                        xbst[:, par * D:(par + 1) * D]))
                self.sx[0].wait(s2)
                self.sx[1].wait(s2)

    def tm_gemm(self, tag, xsrc, M, wsrc, n0, Nsec, yacc, bufs,
                pre_t=False, rhs_cb=None, gp_hook=None):
        nc = self.nc
        ss, st, sp, sv = self.ss, self.st, self.sp, self.sv
        xg, wgb, psA, psB = bufs
        NTC = M // 128
        GK = 16
        ps_base = sp.n
        for grp in range(2):
            kcs = list(range(grp * GK, (grp + 1) * GK))
            with nc.Block(f"{tag}_g{grp}") as blk:
                if grp == 0 and gp_hook is not None:
                    blk.gpsimd(gp_hook)
                @blk.sync
                def _(s):
                    for j, kc in enumerate(kcs):
                        ss.dinc(s.dma_start(
                            xg[:, j * M:(j + 1) * M],
                            xsrc[:, kc * 128:(kc + 1) * 128], transpose=True))
                        if pre_t:
                            ss.dinc(s.dma_start(
                                wgb[:, j * Nsec:(j + 1) * Nsec],
                                wsrc[kc * 128:(kc + 1) * 128, n0:n0 + Nsec]))
                        else:
                            ss.dinc(s.dma_start(
                                wgb[:, j * Nsec:(j + 1) * Nsec],
                                wsrc[n0:n0 + Nsec, kc * 128:(kc + 1) * 128],
                                transpose=True))
                    ss.wait(s)

                if rhs_cb is not None:
                    @blk.vector
                    def _(v):
                        ss.wait(v)
                        for j, kc in enumerate(kcs):
                            rhs_cb(v, wgb, j, kc, n0)
                        sv.inc(v.drain())

                rdy_v = sv.n
                rdy_s = ss.n

                @blk.tensor
                def _(t):
                    t.wait_ge(ss.h, rdy_s)
                    if rhs_cb is not None:
                        t.wait_ge(sv.h, rdy_v)
                    for tc in range(NTC):
                        r = grp * NTC + tc
                        ps = psA if r % 2 == 0 else psB
                        t.wait_ge(sp.h, ps_base + max(0, r - 1))
                        last = None
                        for j in range(GK):
                            lhs = xg[:, j * M + tc * 128:j * M + (tc + 1) * 128]
                            for nn0 in range(0, Nsec, 512):
                                nn = min(512, Nsec - nn0)
                                last = t.matmul(
                                    ps[:, nn0:nn0 + nn], lhs,
                                    wgb[:, j * Nsec + nn0:j * Nsec + nn0 + nn],
                                    start=(j == 0), stop=(j == GK - 1))
                        st.inc(last)

                base = st.n - NTC

                @blk.vector
                def _(v):
                    v.drain()
                    for tc in range(NTC):
                        r = grp * NTC + tc
                        ps = psA if r % 2 == 0 else psB
                        v.wait_ge(st.h, base + tc + 1)
                        dst = yacc[:, tc * Nsec:(tc + 1) * Nsec]
                        if grp == 0:
                            sp.inc(v.tensor_copy(dst, ps[:, 0:Nsec]))
                        else:
                            sp.inc(v.tensor_tensor(dst, dst, ps[:, 0:Nsec],
                                                   op=ALU.add))

    def rope_tm(self, v, out, src, cos_ap, sin_ap, nh, tmp):
        s3 = src.rearrange("p (h d two) -> p h d two", h=nh, two=2)
        o3 = out.rearrange("p (h d two) -> p h d two", h=nh, two=2)
        t3 = tmp.rearrange("p (h d) -> p h d", h=nh)
        cb = cos_ap.unsqueeze(1).to_broadcast([128, nh, 64])
        sb = sin_ap.unsqueeze(1).to_broadcast([128, nh, 64])
        v.tensor_tensor(t3, s3[:, :, :, 1], sb, op=ALU.mult)
        v.tensor_tensor(o3[:, :, :, 0], s3[:, :, :, 0], cb, op=ALU.mult)
        v.drain()
        v.tensor_tensor(o3[:, :, :, 0], o3[:, :, :, 0], t3, op=ALU.subtract)
        v.drain()
        v.tensor_tensor(t3, s3[:, :, :, 0], sb, op=ALU.mult)
        v.tensor_tensor(o3[:, :, :, 1], s3[:, :, :, 1], cb, op=ALU.mult)
        v.drain()
        v.tensor_tensor(o3[:, :, :, 1], o3[:, :, :, 1], t3, op=ALU.add)

    def p1_qkv(self):
        nc, ii, sc = self.nc, self.inp, self.scr
        ss, sv, sa = self.ss, self.sv, self.sa

        with (
            nc.sbuf_tensor("xg1", [128, 16 * S], BF16) as xg,
            nc.sbuf_tensor("wgb1", [128, 16 * 1024], BF16) as wgb,
            nc.sbuf_tensor("yacc1", [128, NTF * 1024], F32) as yacc,
            nc.psum_tensor("psA1", [128, 1024], F32) as psA,
            nc.psum_tensor("psB1", [128, 1024], F32) as psB,
            nc.sbuf_tensor("ctab", [128, NTF * 64], F32) as ctab,
            nc.sbuf_tensor("stab", [128, NTF * 64], F32) as stab,
            nc.sbuf_tensor("stg", [128, 2 * 1024], BF16) as stg,
            nc.sbuf_tensor("stg2", [128, 2 * 1024], BF16) as stg2,
            nc.sbuf_tensor("rtmp", [128, KVH * 64], F32) as rtmp,
        ):
            for which in ("k", "v"):
                row0 = 4096 if which == "k" else 5120
                self.tm_gemm(f"p1{which}", sc["x_bf"], S, sc["wq_bf"], row0,
                             1024, yacc, (xg, wgb, psA, psB),
                             gp_hook=(self._waves23 if which == "k" else None))
                self._waves23 = None
                ndma = 2 if which == "k" else 1
                with nc.Block(f"p1{which}p") as blk:
                    @blk.sync
                    def _(s):
                        if which == "k":
                            ss.dinc(s.dma_start(
                                ctab[:, :].rearrange("p (n d) -> p n d", d=64),
                                ii["cosk"][:, :].rearrange("(n p) d -> p n d", p=128)))
                            ss.dinc(s.dma_start(
                                stab[:, :].rearrange("p (n d) -> p n d", d=64),
                                ii["sink"][:, :].rearrange("(n p) d -> p n d", p=128)))
                        ss.wait(s)

                    dma_rdy = self.ss.n
                    ps_rdy = self.sp.n
                    sx_b = [self.sx[0].n, self.sx[1].n]

                    @blk.vector
                    def _(v):
                        v.wait_ge(ss.h, dma_rdy)
                        v.wait_ge(self.sp.h, ps_rdy)
                        v.drain()
                        for tc in range(NTF):
                            if tc >= 2:
                                v.wait_ge(self.sx[tc % 2].h,
                                          sx_b[tc % 2] + 16 * ndma * (tc // 2))
                            hb = 1024 * (tc % 2)
                            ych = yacc[:, tc * 1024:(tc + 1) * 1024]
                            v.tensor_copy(stg[:, hb:hb + 1024], ych)
                            if which == "k":
                                self.rope_tm(v, stg2[:, hb:hb + 1024], ych,
                                             ctab[:, tc * 64:(tc + 1) * 64],
                                             stab[:, tc * 64:(tc + 1) * 64],
                                             KVH, rtmp)
                            sv.inc(v.drain())

                    vbase = sv.n - NTF

                    @blk.scalar
                    def _(s2):
                        for tc in range(NTF):
                            s2.wait_ge(sv.h, vbase + tc + 1)
                            hb = 1024 * (tc % 2)
                            dst = sc["k_scr"] if which == "k" else sc["v_scr"]
                            self.sx[tc % 2].dinc(s2.dma_start(
                                dst[tc * 128:(tc + 1) * 128, :],
                                stg[:, hb:hb + 1024]))
                            if which == "k":
                                self.sx[tc % 2].dinc(s2.dma_start(
                                    sc["rk_scr"][tc * 128:(tc + 1) * 128, :],
                                    stg2[:, hb:hb + 1024]))
                        self.sx[0].wait(s2)
                        self.sx[1].wait(s2)

        with (
            nc.sbuf_tensor("xg2", [128, 16 * TOK], BF16) as xg,
            nc.sbuf_tensor("wgb2", [128, 16 * 2048], BF16) as wgb,
            nc.sbuf_tensor("yacc2", [128, NT * 2048], F32) as yacc,
            nc.psum_tensor("psA2", [128, 2048], F32) as psA,
            nc.psum_tensor("psB2", [128, 2048], F32) as psB,
            nc.sbuf_tensor("ctab2", [128, NT * 64], F32) as ctab,
            nc.sbuf_tensor("stab2", [128, NT * 64], F32) as stab,
            nc.sbuf_tensor("stg3", [128, 2 * 2048], BF16) as stg,
            nc.sbuf_tensor("stg4", [128, 2 * 2048], BF16) as stg2,
            nc.sbuf_tensor("rtmp2", [128, 16 * 64], F32) as rtmp,
        ):
            with nc.Block("p1qt") as blk:
                @blk.sync
                def _(s):
                    s.wait_ge(self.sw[0].h, self._p0_wave2)
                    ss.dinc(s.dma_start(
                        ctab[:, :].rearrange("p (n d) -> p n d", d=64),
                        ii["cosq"][:, :].rearrange("(n p) d -> p n d", p=128)))
                    ss.dinc(s.dma_start(
                        stab[:, :].rearrange("p (n d) -> p n d", d=64),
                        ii["sinq"][:, :].rearrange("(n p) d -> p n d", p=128)))
                    ss.wait(s)

            for half in range(2):
                self.tm_gemm(f"p1q{half}", sc["xown_bf"], TOK, sc["wq_bf"],
                             half * 2048, 2048, yacc, (xg, wgb, psA, psB))
                with nc.Block(f"p1q{half}p") as blk:
                    ps_rdy = self.sp.n
                    sx_b = [self.sx[0].n, self.sx[1].n]

                    @blk.vector
                    def _(v):
                        v.wait_ge(self.sp.h, ps_rdy)
                        v.drain()
                        for tc in range(NT):
                            if tc >= 2:
                                v.wait_ge(self.sx[tc % 2].h,
                                          sx_b[tc % 2] + 32 * (tc // 2))
                            hb = 2048 * (tc % 2)
                            ych = yacc[:, tc * 2048:(tc + 1) * 2048]
                            v.tensor_copy(stg[:, hb:hb + 2048], ych)
                            self.rope_tm(v, stg2[:, hb:hb + 2048], ych,
                                         ctab[:, tc * 64:(tc + 1) * 64],
                                         stab[:, tc * 64:(tc + 1) * 64],
                                         16, rtmp)
                            sv.inc(v.drain())

                    vbase = sv.n - NT

                    @blk.scalar
                    def _(s2):
                        for tc in range(NT):
                            s2.wait_ge(sv.h, vbase + tc + 1)
                            hb = 2048 * (tc % 2)
                            cs = slice(half * 2048, (half + 1) * 2048)
                            self.sx[tc % 2].dinc(s2.dma_start(
                                sc["q_scr"][tc * 128:(tc + 1) * 128, cs],
                                stg[:, hb:hb + 2048]))
                            self.sx[tc % 2].dinc(s2.dma_start(
                                sc["rq_scr"][tc * 128:(tc + 1) * 128, cs],
                                stg2[:, hb:hb + 2048]))
                        self.sx[0].wait(s2)
                        self.sx[1].wait(s2)

        with (
            nc.sbuf_tensor("xg3", [128, 16 * TOK], BF16) as xg,
            nc.sbuf_tensor("wgb3", [128, 16 * 96], BF16) as wgb,
            nc.psum_tensor("psA3", [128, 96], F32) as psA,
            nc.psum_tensor("psB3", [128, 96], F32) as psB,
            nc.sbuf_tensor("bgt", [128, 96], F32) as bgt,
            nc.sbuf_tensor("gac", [128, NT * 96], F32) as gac,
        ):
            self.tm_gemm("p1g", sc["xown_bf"], TOK, sc["wg_bf"], 0, 96,
                         gac, (xg, wgb, psA, psB))
            with nc.Block("p1gp") as blk:
                @blk.sync
                def _(s):
                    ss.dinc(s.dma_start(bgt[:, :], ii["b_gateb"][:, :]))
                    ss.wait(s)

                ps_rdy = self.sp.n
                dma_rdy = ss.n

                @blk.vector
                def _(v):
                    v.wait_ge(self.sp.h, ps_rdy)
                    v.wait_ge(ss.h, dma_rdy)
                    v.drain()
                    bb = bgt[:, :].unsqueeze(1).to_broadcast([128, NT, 96])
                    v.tensor_tensor(
                        gac[:, :].rearrange("p (n g) -> p n g", g=96),
                        gac[:, :].rearrange("p (n g) -> p n g", g=96),
                        bb, op=ALU.add)
                    sv.inc(v.drain())

                @blk.scalar
                def _(s2):
                    self.sv.wait(s2)
                    self.sq.inc(s2.activation(gac[:, :], gac[:, :], ACT.Sigmoid))
                    self.sq.wait(s2)
                    sa.dinc(s2.dma_start(sc["gates_scr"][:, :], gac[:, :]))
                    sa.wait(s2)


    def p2_prep(self, rkp, vtm, ckT, cvt, gts):
        nc, ii, sc = self.nc, self.inp, self.scr
        ss, sa, sv, st, sq, sp = self.ss, self.sa, self.sv, self.st, self.sq, self.sp

        with nc.Block("p2a") as blk:
            @blk.gpsimd
            def _(g):
                sg = self.sg
                sg.dinc(g.dma_start(
                    ckT[:, :].rearrange("p (k c) -> p k c", c=128)[:, :, 127:128],
                    ii["mem_kT"][:, :].unsqueeze(2),
                    allow_slow_non_contiguous=True))
                sg.dinc(g.dma_start(cvt[127:128, :], ii["mem_vb"][:, :]))
                sg.wait(g)

            @blk.vector
            def _(v):
                for kvh in range(KVH):
                    o = kvh * 2176
                    v.memset(rkp[:, o:o + 64], 0.0)
                    v.memset(rkp[:, o + 2112:o + 2176], 0.0)
                    v.memset(vtm[:, kvh * 18 * 128:(kvh * 18 + 1) * 128], 0.0)
                    v.memset(vtm[:, (kvh * 18 + 17) * 128:(kvh * 18 + 18) * 128], 0.0)
                sv.inc(v.drain())

            vfence = sv.n

            @blk.sync
            def _(s):
                s.wait_ge(sv.h, vfence)
                for kvh in range(KVH):
                    ss.dinc(s.dma_start(
                        rkp[:, kvh * 2176 + 64:kvh * 2176 + 2112],
                        sc["rk_scr"][:, kvh * 128:(kvh + 1) * 128],
                        transpose=True))
                    vt4 = vtm[:, :].rearrange("p (k b c) -> p k b c", b=18, c=128)
                    ss.dinc(s.dma_start(
                        vt4[:, kvh, 1:17, :],
                        sc["v_scr"][:, kvh * 128:(kvh + 1) * 128].rearrange(
                            "(n p) c -> p n c", p=128)))
                ss.dinc(s.dma_start(gts[:, :], sc["gates_scr"][:, :]))
                ss.wait(s)

        with (
            nc.sbuf_tensor("w1T", [128, 16 * 2048], BF16) as w1T,
            nc.sbuf_tensor("w2T", [128, 16 * 128], BF16) as w2T,
            nc.sbuf_tensor("kt", [128, 2 * 2048], BF16) as kt,
            nc.sbuf_tensor("kpk", [128, 2 * 2048], BF16) as kpk,
            nc.sbuf_tensor("h1t", [128, 2 * 16 * 128], BF16) as h1t,
            nc.sbuf_tensor("posb", [128, 2 * 16], F32) as posb,
            nc.sbuf_tensor("b1b", [128, 16], F32) as b1b,
            nc.sbuf_tensor("b2c", [128, 1], F32) as b2c,
            nc.sbuf_tensor("b2r", [128, 128], F32) as b2r,
            nc.psum_tensor("psh1", [128, 128], F32) as psh1,
            nc.psum_tensor("psh2", [128, 128], F32) as psh2,
            nc.psum_tensor("psck1", [128, 128], F32) as psck1,
            nc.psum_tensor("psck2", [128, 128], F32) as psck2,
        ):
            for which in ("k", "v"):
                w1_bf = sc["kw1_bf"] if which == "k" else sc["vw1_bf"]
                w2_bf = sc["kw2_bf"] if which == "k" else sc["vw2_bf"]
                with nc.Block(f"p2{which}w") as blk:
                    @blk.sync
                    def _(s):
                        s.wait_ge(self.sg.h, self.sg.n)
                        for ic in range(16):
                            ss.dinc(s.dma_start(
                                w1T[:, ic * 2048:(ic + 1) * 2048],
                                w1_bf[:, ic * 128:(ic + 1) * 128], transpose=True))
                        for oc in range(16):
                            ss.dinc(s.dma_start(
                                w2T[:, oc * 128:(oc + 1) * 128],
                                w2_bf[:, oc * 128:(oc + 1) * 128], transpose=True))
                        ss.dinc(s.dma_start(
                            b1b[:, :], ii["k_b1c" if which == "k" else "v_b1c"][:, :]))
                        if which == "k":
                            ss.dinc(s.dma_start(b2c[:, :], ii["k_b2c"][:, :]))
                        else:
                            ss.dinc(s.dma_start(b2r[:, :], ii["v_b2b"][:, :]))
                        ss.wait(s)

                fz = {}
                with nc.Block(f"p2{which}c") as blk:
                    for kvh in range(KVH):
                        par = kvh % 2
                        ktp = kt[:, par * 2048:(par + 1) * 2048]
                        kpkp = kpk[:, par * 2048:(par + 1) * 2048]
                        h1p = h1t[:, par * 2048:(par + 1) * 2048]
                        posp = posb[:, par * 16:(par + 1) * 16]
                        psck = psck1 if par == 0 else psck2

                        kt_free = fz.get(("kpk_done", kvh - 2))

                        skp = self.sk[par]

                        @blk.sync
                        def _(s):
                            if kt_free:
                                s.wait_ge(kt_free[0].h, kt_free[1])
                            src_ = sc["k_scr"] if which == "k" else sc["v_scr"]
                            skp.dinc(s.dma_start(
                                ktp, src_[:, kvh * 128:(kvh + 1) * 128],
                                transpose=True))
                            skp.dinc(s.dma_start(
                                posp,
                                ii["k_posT" if which == "k" else "v_posT"]
                                [:, kvh * 16:(kvh + 1) * 16]))

                        dmaf = skp.n
                        kpk_free = fz.get(("h1_done", kvh - 2))

                        @blk.vector
                        def _(v):
                            v.wait_ge(skp.h, dmaf)
                            if kpk_free:
                                v.wait_ge(kpk_free[0].h, kpk_free[1])
                            v.drain()
                            pb = posp.unsqueeze(1).to_broadcast([128, 128, 16])
                            v.tensor_tensor(
                                kpkp.rearrange("p (c t) -> p c t", t=16),
                                ktp.rearrange("p (c t) -> p c t", t=16),
                                pb, op=ALU.add)
                            sv.inc(v.drain())
                        fz[("kpk_done", kvh)] = (sv, sv.n)

                        vfen = sv.n
                        sq_base = sq.n
                        h1_free = fz.get(("ck_done", kvh - 2))

                        @blk.tensor
                        def _(t):
                            t.wait_ge(sv.h, vfen)
                            for oc in range(16):
                                psh = psh1 if oc % 2 == 0 else psh2
                                if oc >= 2:
                                    t.wait_ge(sq.h, sq_base + oc - 1)
                                last = None
                                for ic in range(16):
                                    last = t.matmul(
                                        psh[:, :],
                                        w1T[:, ic * 2048 + oc * 128:
                                            ic * 2048 + (oc + 1) * 128],
                                        kpkp[:, ic::16],
                                        start=(ic == 0), stop=(ic == 15))
                                st.inc(last)
                        fz[("h1_done", kvh)] = (st, st.n)

                        tbase = st.n - 16

                        @blk.scalar
                        def _(s2):
                            if h1_free:
                                s2.wait_ge(h1_free[0].h, h1_free[1])
                            for oc in range(16):
                                psh = psh1 if oc % 2 == 0 else psh2
                                s2.wait_ge(st.h, tbase + oc + 1)
                                sq.inc(s2.activation(
                                    h1p[:, oc * 128:(oc + 1) * 128], psh[:, :],
                                    ACT.Relu, bias=b1b[:, oc:oc + 1]))

                        sqf = sq.n
                        psck_free = fz.get(("evac_done", kvh - 2))

                        @blk.tensor
                        def _(t):
                            t.wait_ge(sq.h, sqf)
                            if psck_free:
                                t.wait_ge(psck_free[0].h, psck_free[1])
                            last = None
                            for oc in range(16):
                                if which == "k":
                                    last = t.matmul(
                                        psck[:, :], w2T[:, oc * 128:(oc + 1) * 128],
                                        h1p[:, oc * 128:(oc + 1) * 128],
                                        start=(oc == 0), stop=(oc == 15))
                                else:
                                    last = t.matmul(
                                        psck[:, :], h1p[:, oc * 128:(oc + 1) * 128],
                                        w2T[:, oc * 128:(oc + 1) * 128],
                                        start=(oc == 0), stop=(oc == 15))
                            st.inc(last)
                        fz[("ck_done", kvh)] = (st, st.n)

                        tf2 = st.n

                        @blk.vector
                        def _(v):
                            v.wait_ge(st.h, tf2)
                            v.drain()
                            if which == "k":
                                v.tensor_scalar(
                                    ckT[:, kvh * 128:kvh * 128 + 127],
                                    psck[:, 0:127], b2c[:, 0:1], None,
                                    op0=ALU.add)
                            else:
                                v.tensor_tensor(
                                    cvt[0:127, kvh * 128:(kvh + 1) * 128],
                                    psck[0:127, :], b2r[0:127, :], op=ALU.add)
                            sv.inc(v.drain())
                        fz[("evac_done", kvh)] = (sv, sv.n)

    def p3_attn(self, rkp, vtm, ckT, cvt, gts):
        nc, ii, sc = self.nc, self.inp, self.scr
        ss, sa, sv, st, sq, sp = self.ss, self.sa, self.sv, self.st, self.sq, self.sp
        self.f3 = {}

        from contextlib import ExitStack
        with ExitStack() as _es:
            E = _es.enter_context
            qh = E(nc.sbuf_tensor("qh", [128, G * 1024], BF16))
            rqh = E(nc.sbuf_tensor("rqh", [128, G * 1024], BF16))
            cmk = E(nc.sbuf_tensor("cmk", [128, NT * 128], BF16))
            omk = E(nc.sbuf_tensor("omk", [128, NT * 256], BF16))
            smk = E(nc.sbuf_tensor("smk", [128, NT * 384], BF16))
            cs = E(nc.sbuf_tensor("cs", [128, 2 * G * 128], F32))
            imp = E(nc.sbuf_tensor("imp", [128, 2 * 128], F32))
            impsm = E(nc.sbuf_tensor("impsm", [128, 2 * 128], F32))
            m8 = E(nc.sbuf_tensor("m8", [128, 2 * 8], F32))
            repl = E(nc.sbuf_tensor("repl", [128, 2 * 127], F32))
            fmask = E(nc.sbuf_tensor("fmask", [128, 2 * 128], F32))
            maskb = E(nc.sbuf_tensor("maskb", [128, 2 * 2048], BF16))
            pcc = E(nc.sbuf_tensor("pcc", [128, 2 * G * 128], BF16))
            pTc = E(nc.sbuf_tensor("pTc", [128, 2 * G * 128], BF16))
            couts = E(nc.sbuf_tensor("couts", [128, 2 * G * 128], F32))
            comb = E(nc.sbuf_tensor("comb", [128, 2 * G * 128], BF16))
            rmg = E(nc.sbuf_tensor("rmg", [128, 2 * 4], F32))
            rsg = E(nc.sbuf_tensor("rsg", [128, 2 * 4], F32))
            rrg = E(nc.sbuf_tensor("rrg", [128, 2 * 1], F32))
            r1 = E(nc.sbuf_tensor("r1", [128, 2 * 1], F32))
            scog = E(nc.sbuf_tensor("scog", [128, G * 2048], F32))
            slsco = E(nc.sbuf_tensor("slsco", [128, G * 384], F32))
            pbfg = E(nc.sbuf_tensor("pbfg", [128, G * 2048], BF16))
            pTg = E(nc.sbuf_tensor("pTg", [128, 4 * 128], BF16))
            slpg = E(nc.sbuf_tensor("slpg", [128, G * 512], BF16))
            rmF = E(nc.sbuf_tensor("rmF", [128, 4], F32))
            rsF = E(nc.sbuf_tensor("rsF", [128, 4], F32))
            rmS = E(nc.sbuf_tensor("rmS", [128, 4], F32))
            rsS = E(nc.sbuf_tensor("rsS", [128, 4], F32))
            prd = E(nc.sbuf_tensor("prd", [128, 12], F32))
            idb = E(nc.sbuf_tensor("idb", [128, 128], BF16))
            idf = E(nc.sbuf_tensor("idf", [128, 128], F32))
            psc0 = E(nc.psum_tensor("psc0", [128, 512], F32))
            psc1 = E(nc.psum_tensor("psc1", [128, 512], F32))
            psf0 = E(nc.psum_tensor("psf0", [128, 512], F32))
            psf1 = E(nc.psum_tensor("psf1", [128, 512], F32))
            pso = E(nc.psum_tensor("pso", [128, 1024], F32))
            ptr0 = E(nc.psum_tensor("ptr0", [128, 1024], BF16))
            ptr1 = E(nc.psum_tensor("ptr1", [128, 1024], BF16))

            bufs = dict(qh=qh, rqh=rqh, cmk=cmk, omk=omk, smk=smk, cs=cs,
                        imp=imp, impsm=impsm, m8=m8, repl=repl,
                        fmask=fmask, maskb=maskb, pcc=pcc, pTc=pTc,
                        couts=couts, comb=comb, rmg=rmg, rsg=rsg, rrg=rrg,
                        r1=r1, scog=scog, slsco=slsco, pbfg=pbfg, pTg=pTg,
                        slpg=slpg, rmF=rmF, rsF=rsF, rmS=rmS,
                        rsS=rsS, prd=prd, idb=idb,
                        psc=[psc0, psc1], psf=[psf0, psf1], pso=pso,
                        ptr=[ptr0[:, 0:128], ptr1[:, 0:128]],
                        rkp=rkp, vtm=vtm, ckT=ckT,
                        cvt=cvt, gts=gts)

            with nc.Block("p3m") as blk:
                @blk.gpsimd
                def _(g):
                    for dst, key in ((cmk, "cmask"), (omk, "ownmask"),
                                     (smk, "slmask")):
                        self.sg.dinc(g.dma_start(
                            dst[:, :].rearrange("p (n c) -> p n c", n=NT),
                            ii[key][:, :].rearrange("(n p) c -> p n c", p=128)))
                    self.sg.dinc(g.dma_start(idf[:, :], ii["ident"][:, :]))
                    self.sg.wait(g)

                @blk.vector
                def _(v):
                    self.sg.wait(v)
                    self.sv.inc(v.tensor_copy(idb[:, :], idf[:, :]))

            for kvh in range(KVH):
                with nc.Block(f"p3q{kvh}") as blk:
                    fw = self.f3.get("qh")

                    @blk.sync
                    def _(s):
                        if fw:
                            s.wait_ge(fw[0].h, fw[1])
                        for g in range(G):
                            hd = kvh * G + g
                            ss.dinc(s.dma_start(
                                qh[:, g * 1024:(g + 1) * 1024],
                                sc["q_scr"][:, hd * 128:(hd + 1) * 128],
                                transpose=True))
                            ss.dinc(s.dma_start(
                                rqh[:, g * 1024:(g + 1) * 1024],
                                sc["rq_scr"][:, hd * 128:(hd + 1) * 128],
                                transpose=True))
                    self.f3["qhload"] = (ss, ss.n)

                with nc.Block(f"ku{kvh}") as blk:
                    for t in range(NT + 1):
                        if t < NT:
                            self._p3_comp(blk, kvh, t, bufs)
                        if t >= 1:
                            self._p3_fine(blk, kvh, t - 1, bufs)
                self.f3["qh"] = (st, st.n)

    def _p3_comp(self, blk, kvh, tau, bb):
        ss, sa, sv, st, sq = self.ss, self.sa, self.sv, self.st, self.sq
        f3 = self.f3
        par = tau % 2
        psc = bb["psc"][par]
        cs = bb["cs"][:, par * 512:(par + 1) * 512]
        imp = bb["imp"][:, par * 128:(par + 1) * 128]
        impsm = bb["impsm"][:, par * 128:(par + 1) * 128]
        m8 = bb["m8"][:, par * 8:(par + 1) * 8]
        repl = bb["repl"][:, par * 127:(par + 1) * 127]
        fmask = bb["fmask"][:, par * 128:(par + 1) * 128]
        maskb = bb["maskb"][:, par * 2048:(par + 1) * 2048]
        pcc = bb["pcc"][:, par * 512:(par + 1) * 512]
        pTc = bb["pTc"][:, par * 512:(par + 1) * 512]
        couts = bb["couts"][:, par * 512:(par + 1) * 512]
        rmg = bb["rmg"][:, par * 4:(par + 1) * 4]
        rsg = bb["rsg"][:, par * 4:(par + 1) * 4]
        rrg = bb["rrg"][:, par * 1:(par + 1) * 1]
        r1 = bb["r1"][:, par * 1:(par + 1) * 1]
        qh, ckT, cvt, cmk, omk = (bb["qh"], bb["ckT"], bb["cvt"], bb["cmk"],
                                  bb["omk"])

        def FW(eng, *keys):
            for k in keys:
                pair = f3.get(k)
                if pair:
                    eng.wait_ge(pair[0].h, pair[1])

        @blk.tensor
        def _(t):
            FW(t, "qhload", f"psc{par}")
            last = None
            for g in range(G):
                last = t.matmul(
                    psc[:, g * 128:(g + 1) * 128],
                    qh[:, g * 1024 + tau * 128:g * 1024 + (tau + 1) * 128],
                    ckT[:, kvh * 128:(kvh + 1) * 128],
                    start=True, stop=True)
            st.inc(last)

        @blk.vector
        def _(v):
            st.wait(v)
            FW(v, f"cs{par}", f"imp{par}")
            v.drain()
            for g in range(G):
                v.scalar_tensor_tensor(
                    cs[:, g * 128:(g + 1) * 128],
                    psc[:, g * 128:(g + 1) * 128], SCALE,
                    cmk[:, tau * 128:(tau + 1) * 128],
                    op0=ALU.mult, op1=ALU.add)
            v.drain()
            v.tensor_tensor(imp, cs[:, 0:128], cs[:, 128:256], op=ALU.add)
            v.drain()
            v.tensor_tensor(imp, imp, cs[:, 256:384], op=ALU.add)
            v.drain()
            v.tensor_tensor(imp, imp, cs[:, 384:512], op=ALU.add)
            v.drain()
            v.memset(imp[:, 127:128], -4000.0)
            v.drain()
            v.tensor_reduce(r1, imp, axis=AX, op=ALU.max, negate=True)
            for g in range(G):
                v.tensor_reduce(rmg[:, g:g + 1], cs[:, g * 128:(g + 1) * 128],
                                axis=AX, op=ALU.max, negate=True)
            v.drain()
            v.tensor_scalar(r1, r1, 0.25, None, op0=ALU.mult)
            sv.inc(v.drain())
        f3[f"psc{par}"] = (sv, sv.n)

        @blk.scalar
        def _(s2):
            sv.wait(s2)
            FW(s2, f"impsm{par}", f"rsg{par}", f"pbfc{par}")
            for g in range(G):
                s2.activation(pcc[:, g * 128:(g + 1) * 128],
                              cs[:, g * 128:(g + 1) * 128], ACT.Exp,
                              bias=rmg[:, g:g + 1],
                              accum_out=rsg[:, g:g + 1])
            sq.inc(s2.activation(impsm, imp, ACT.Exp, scale=0.25,
                                 bias=r1[:, 0:1], accum_out=rrg[:, 0:1]))
        f3[f"cs{par}"] = (sq, sq.n)
        f3[f"imp{par}"] = (sq, sq.n)

        @blk.vector
        def _(v):
            sq.wait(v)
            FW(v, f"maskb{par}")
            v.reciprocal(rrg[:, 0:1], rrg[:, 0:1])
            v.reciprocal(rsg, rsg)
            v.drain()
            v.tensor_scalar(impsm, impsm, rrg[:, 0:1], None, op0=ALU.mult)
            v.drain()
            v.max(out=m8, in_=impsm[:, 0:127])
            v.drain()
            v.memset(m8[:, 4:8], -1.0)
            v.drain()
            v.match_replace(out=repl, in_to_replace=m8,
                            in_values=impsm[:, 0:127], imm_value=-1e30)
            v.drain()
            v.tensor_scalar(fmask[:, 0:127], repl, -1e29, None, op0=ALU.is_lt)
            v.drain()
            v.tensor_scalar(repl, impsm[:, 0:127], 1e-10, None, op0=ALU.is_gt)
            v.drain()
            v.tensor_tensor(fmask[:, 0:127], fmask[:, 0:127], repl,
                            op=ALU.mult)
            v.drain()
            v.tensor_scalar(fmask[:, 0:127], fmask[:, 0:127], 1e30, -1e30,
                            op0=ALU.mult, op1=ALU.add)
            v.memset(fmask[:, 127:128], NEG)
            v.drain()
            KT = 256 * (tau + 1)
            nb = KT // 16
            v.tensor_copy(
                maskb[:, 0:KT].rearrange("p (b s) -> p b s", s=16),
                fmask[:, 0:nb].unsqueeze(2).to_broadcast([128, nb, 16]))
            v.drain()
            v.tensor_tensor(maskb[:, KT - 256:KT], maskb[:, KT - 256:KT],
                            omk[:, tau * 256:(tau + 1) * 256], op=ALU.max)
            sv.inc(v.drain())
        f3[f"impsm{par}"] = (sv, sv.n)
        f3[f"maskb_rdy{par}"] = (sv, sv.n)

        @blk.sync
        def _(s):
            sv.wait(s)
            FW(s, f"pTc_rd{par}")
            for g in range(G):
                ss.dinc(s.dma_start(pTc[:, g * 128:(g + 1) * 128],
                                    pcc[:, g * 128:(g + 1) * 128],
                                    transpose=True))
        f3[f"pbfc{par}"] = (ss, ss.n)
        ss5 = ss.n

        @blk.tensor
        def _(t):
            t.wait_ge(ss.h, ss5)
            FW(t, f"couts{par}")
            last = None
            for g in range(G):
                last = t.matmul(
                    psc[:, g * 128:(g + 1) * 128],
                    pTc[:, g * 128:(g + 1) * 128],
                    cvt[:, kvh * 128:(kvh + 1) * 128],
                    start=True, stop=True)
            st.inc(last)
        f3[f"pTc_rd{par}"] = (st, st.n)

        @blk.vector
        def _(v):
            st.wait(v)
            v.drain()
            sv.inc(v.tensor_copy(couts, psc[:, 0:512]))
        f3[f"psc{par}"] = (sv, sv.n)

    def _p3_fine(self, blk, kvh, tau, bb):
        ss, sa, sv, st, sq = self.ss, self.sa, self.sv, self.st, self.sq
        sc = self.scr
        f3 = self.f3
        par = tau % 2
        KT = 256 * (tau + 1)
        NKC = KT // 128
        NSC = (KT + 511) // 512
        NCH = NSC + 1
        rko = kvh * 2176
        vto = kvh * 18 * 128
        maskb = bb["maskb"][:, par * 2048:par * 2048 + 2048]
        couts = bb["couts"][:, par * 512:(par + 1) * 512]
        comb = bb["comb"][:, par * 512:(par + 1) * 512]
        rsg = bb["rsg"][:, par * 4:(par + 1) * 4]
        rqh, smk, gts, rkp, vtm = (bb["rqh"], bb["smk"], bb["gts"],
                                   bb["rkp"], bb["vtm"])
        scog, slsco, pbfg, pTg = bb["scog"], bb["slsco"], bb["pbfg"], bb["pTg"]
        slpg = bb["slpg"]
        rmF, rsF, rmS, rsS, prd = (bb["rmF"], bb["rsF"], bb["rmS"],
                                   bb["rsS"], bb["prd"])
        psf, pso, ptrs, idb = bb["psf"], bb["pso"], bb["ptr"], bb["idb"]

        def FW(eng, *keys):
            for k in keys:
                pair = f3.get(k)
                if pair:
                    eng.wait_ge(pair[0].h, pair[1])

        sv_base = sv.n

        @blk.tensor
        def _(t):
            FW(t, "psf", f"maskb_rdy{par}")
            c = 0
            for g in range(G):
                lhs_q = rqh[:, g * 1024 + tau * 128:g * 1024 + (tau + 1) * 128]
                for i in range(NCH):
                    ps = psf[c % 2]
                    if c >= 2:
                        t.wait_ge(sv.h, sv_base + (c - 1))
                    if i < NSC:
                        nn = min(512, KT - i * 512)
                        st.inc(t.matmul(
                            ps[:, 0:nn], lhs_q,
                            rkp[:, rko + 64 + i * 512:rko + 64 + i * 512 + nn],
                            start=True, stop=True))
                    else:
                        st.inc(t.matmul(
                            ps[:, 0:384], lhs_q,
                            rkp[:, rko + 2 * tau * 128:rko + 2 * tau * 128 + 384],
                            start=True, stop=True))
                    c += 1

        tb8 = st.n - G * NCH

        @blk.vector
        def _(v):
            FW(v, "scog", "slsco")
            v.drain()
            c = 0
            for g in range(G):
                for i in range(NCH):
                    ps = psf[c % 2]
                    v.wait_ge(st.h, tb8 + c + 1)
                    if i < NSC:
                        nn = min(512, KT - i * 512)
                        sv.inc(v.scalar_tensor_tensor(
                            scog[:, g * 2048 + i * 512:g * 2048 + i * 512 + nn],
                            ps[:, 0:nn], SCALE,
                            maskb[:, i * 512:i * 512 + nn],
                            op0=ALU.mult, op1=ALU.add))
                    else:
                        sv.inc(v.scalar_tensor_tensor(
                            slsco[:, g * 384:(g + 1) * 384],
                            ps[:, 0:384], SCALE,
                            smk[:, tau * 384:(tau + 1) * 384],
                            op0=ALU.mult, op1=ALU.add))
                    c += 1
            v.drain()
            for g in range(G):
                v.tensor_reduce(rmF[:, g:g + 1],
                                scog[:, g * 2048:g * 2048 + KT],
                                axis=AX, op=ALU.max, negate=True)
                v.tensor_reduce(rmS[:, g:g + 1],
                                slsco[:, g * 384:(g + 1) * 384],
                                axis=AX, op=ALU.max, negate=True)
            sv.inc(v.drain())
        f3["psf"] = (sv, sv.n)
        f3[f"maskb{par}"] = (sv, sv.n)
        sv9 = sv.n

        @blk.scalar
        def _(s2):
            s2.wait_ge(sv.h, sv9)
            FW(s2, "pbf_rd", "slp_rd", "rsFS")
            for g in range(G):
                s2.activation(pbfg[:, g * 2048:g * 2048 + KT],
                              scog[:, g * 2048:g * 2048 + KT], ACT.Exp,
                              bias=rmF[:, g:g + 1], accum_out=rsF[:, g:g + 1])
                s2.drain()
                sq.inc(s2.activation(
                    slpg[:, g * 512 + 64:g * 512 + 448],
                    slsco[:, g * 384:(g + 1) * 384], ACT.Exp,
                    bias=rmS[:, g:g + 1], accum_out=rsS[:, g:g + 1]))
        f3["scog"] = (sq, sq.n)
        f3["slsco"] = (sq, sq.n)

        if tau == 0 and kvh == 0:
            @blk.vector
            def _(v):
                for g in range(G):
                    v.memset(slpg[:, g * 512:g * 512 + 64], 0.0)
                    v.memset(slpg[:, g * 512 + 448:(g + 1) * 512], 0.0)
                sv.inc(v.drain())

        sq10 = sq.n - G
        sqe_base = sq.n
        tb12 = st.n
        svB = sv.n

        NTOT = NKC + 4
        tb13s = []

        def inc_T(i):
            return 1 if i == 0 else 2 * i

        def inc_P(i):
            return 2 * i + 3 if i < NTOT - 1 else 2 * NTOT

        @blk.scalar
        def _(s2):
            for g in range(G):
                bg = tb12 + g * 2 * NTOT
                for i in range(NTOT):
                    s2.wait_ge(st.h, bg + inc_T(i))
                    if i >= 4:
                        s2.wait_ge(st.h, bg + inc_P(i - 4))
                    elif g >= 1:
                        s2.wait_ge(st.h, bg - 2 * NTOT + inc_P(NTOT - 4 + i))
                    sq.inc(s2.copy(pTg[:, (i % 4) * 128:(i % 4 + 1) * 128],
                                   ptrs[i % 2][:, :]))

        @blk.tensor
        def _(t):
            FW(t, "pso", "pT_rd")
            t.wait_ge(sv.h, svB)
            for g in range(G):
                seg = (g % 2) * 512
                sqe_g = sqe_base + g * NTOT
                t.wait_ge(sq.h, sq10 + g + 1)
                if g >= 2:
                    t.wait_ge(sv.h, svB + g - 1)

                def srcp_of(i):
                    if i < NKC:
                        return pbfg[:, g * 2048 + i * 128:g * 2048 + (i + 1) * 128]
                    i_ = i - NKC
                    return slpg[:, g * 512 + i_ * 128:g * 512 + (i_ + 1) * 128]

                def pv(i):
                    if i < NKC:
                        return t.matmul(
                            pso[:, seg:seg + 128],
                            pTg[:, (i % 4) * 128:(i % 4 + 1) * 128],
                            vtm[:, vto + (1 + i) * 128:vto + (2 + i) * 128],
                            start=(i == 0), stop=(i == NKC - 1))
                    i_ = i - NKC
                    bi = 2 * tau + i_
                    return t.matmul(
                        pso[:, seg + 128:seg + 256],
                        pTg[:, (i % 4) * 128:(i % 4 + 1) * 128],
                        vtm[:, vto + bi * 128:vto + (bi + 1) * 128],
                        start=(i_ == 0), stop=(i_ == 3))

                for i in range(NTOT):
                    if i >= 2:
                        t.wait_ge(sq.h, sqe_g + i - 1)
                    st.inc(t.transpose(ptrs[i % 2][:, :], srcp_of(i), idb[:, :]))
                    if i >= 1:
                        t.wait_ge(sq.h, sqe_g + i)
                        st.inc(pv(i - 1))
                t.wait_ge(sq.h, sqe_g + NTOT)
                st.inc(pv(NTOT - 1))
        f3["pT_rd"] = (st, st.n)
        f3["pbf_rd"] = (st, st.n)
        f3["slp_rd"] = (st, st.n)
        tb13 = st.n

        @blk.vector
        def _(v):
            FW(v, f"comb{par}")
            v.wait_ge(sq.h, sqe_base)
            v.reciprocal(rsF[:, :], rsF[:, :])
            v.reciprocal(rsS[:, :], rsS[:, :])
            v.drain()
            for g in range(G):
                hd3 = 3 * (kvh * G + g)
                gc = lambda b: gts[:, tau * 96 + hd3 + b:tau * 96 + hd3 + b + 1]
                v.tensor_tensor(prd[:, g:g + 1], gc(0), rsg[:, g:g + 1],
                                op=ALU.mult)
                v.tensor_tensor(prd[:, 4 + g:5 + g], gc(1), rsF[:, g:g + 1],
                                op=ALU.mult)
                v.tensor_tensor(prd[:, 8 + g:9 + g], gc(2), rsS[:, g:g + 1],
                                op=ALU.mult)
            v.drain()
            for g in range(G):
                seg = (g % 2) * 512
                v.wait_ge(st.h, tb13 - (G - 1 - g) * 2 * NTOT)
                v.tensor_scalar(comb[:, g * 128:(g + 1) * 128],
                                couts[:, g * 128:(g + 1) * 128],
                                prd[:, g:g + 1], None, op0=ALU.mult)
                v.drain()
                v.scalar_tensor_tensor(comb[:, g * 128:(g + 1) * 128],
                                       pso[:, seg:seg + 128],
                                       prd[:, 4 + g:5 + g],
                                       comb[:, g * 128:(g + 1) * 128],
                                       op0=ALU.mult, op1=ALU.add)
                v.drain()
                sv.inc(v.scalar_tensor_tensor(
                    comb[:, g * 128:(g + 1) * 128],
                    pso[:, seg + 128:seg + 256], prd[:, 8 + g:9 + g],
                    comb[:, g * 128:(g + 1) * 128],
                    op0=ALU.mult, op1=ALU.add))
                v.drain()
        f3["pso"] = (sv, sv.n)
        f3[f"couts{par}"] = (sv, sv.n)
        f3[f"rsg{par}"] = (sv, sv.n)
        f3["rsFS"] = (sv, sv.n)

        @blk.scalar
        def _(s2):
            sv.wait(s2)
            sa.dinc(s2.dma_start(
                sc["comb_scr"][tau * 128:(tau + 1) * 128,
                               kvh * 512:(kvh + 1) * 512], comb))
            sa.wait(s2)
        f3[f"comb{par}"] = (sa, sa.n)

    def emit_ln_tc(self, blk, make_ybuf, lb, out32_ap, out16_ap):
        sv, sq, sa = self.sv, self.sq, self.sa
        ybuf, sqs = lb["ybuf"], lb["sqs"]
        m, vv, sd, rs = lb["m"], lb["vv"], lb["sd"], lb["rs"]
        ot32, ot16 = lb["ot32"], lb["ot16"]

        @blk.vector
        def _(v):
            make_ybuf(v)
            v.drain()
            v.tensor_reduce(m[:, :], ybuf[:, :], axis=AX, op=ALU.add)
            sv.inc(v.drain())

        @blk.scalar
        def _(s2):
            sv.wait(s2)
            sq.inc(s2.activation(sqs[:, :], ybuf[:, :], ACT.Square,
                                 accum_out=vv[:, 0:1]))

        @blk.vector
        def _(v):
            sq.wait(v)
            v.tensor_scalar(m[:, :], m[:, :], 1.0 / D, None, op0=ALU.mult)
            v.tensor_scalar(vv[:, :], vv[:, :], 1.0 / D, None, op0=ALU.mult)
            v.drain()
            v.tensor_tensor(sd[:, :], m[:, :], m[:, :], op=ALU.mult)
            v.drain()
            v.tensor_tensor(vv[:, :], vv[:, :], sd[:, :], op=ALU.subtract)
            v.drain()
            v.tensor_scalar(vv[:, :], vv[:, :], EPS, None, op0=ALU.add)
            sv.inc(v.drain())

        @blk.scalar
        def _(s2):
            sv.wait(s2)
            sq.inc(s2.activation(sd[:, :], vv[:, :], ACT.Sqrt))

        @blk.vector
        def _(v):
            sq.wait(v)
            v.reciprocal(rs[:, :], sd[:, :])
            v.drain()
            v.tensor_tensor(sd[:, :], m[:, :], rs[:, :], op=ALU.mult)
            v.drain()
            v.tensor_scalar(sd[:, :], sd[:, :], -1.0, None, op0=ALU.mult)
            if out16_ap is not None:
                v.drain()
                v.tensor_scalar(ot16[:, :], ybuf[:, :], m[:, 0:1], rs[:, 0:1],
                                op0=ALU.subtract, op1=ALU.mult)
            sv.inc(v.drain())

        @blk.scalar
        def _(s2):
            sv.wait(s2)
            s2.activation(ot32[:, :], ybuf[:, :], ACT.Identity,
                          scale=rs[:, 0:1], bias=sd[:, 0:1])
            s2.drain()
            sa.dinc(s2.dma_start(out32_ap, ot32[:, :]))
            sa.wait(s2)
            if out16_ap is not None:
                sa.dinc(s2.dma_start(out16_ap, ot16[:, :]))
                sa.wait(s2)

    _ln_ctr = [0]

    def _ln_bufs(self, es):
        nc = self.nc
        self._ln_ctr[0] += 1
        u = f"_{self._ln_ctr[0]}"
        return dict(
            ybuf=es.enter_context(nc.sbuf_tensor("lnyb" + u, [128, D], F32)),
            sqs=es.enter_context(nc.sbuf_tensor("lnsq" + u, [128, D], F32)),
            ot32=es.enter_context(nc.sbuf_tensor("lno3" + u, [128, D], F32)),
            ot16=es.enter_context(nc.sbuf_tensor("lno1" + u, [128, D], BF16)),
            yh=es.enter_context(nc.sbuf_tensor("lnyh" + u, [128, D // 2], F32)),
            xo=es.enter_context(nc.sbuf_tensor("lnxo" + u, [128, D], F32)),
            m=es.enter_context(nc.sbuf_tensor("lnm" + u, [128, 1], F32)),
            vv=es.enter_context(nc.sbuf_tensor("lnv" + u, [128, 1], F32)),
            sd=es.enter_context(nc.sbuf_tensor("lnsd" + u, [128, 1], F32)),
            rs=es.enter_context(nc.sbuf_tensor("lnrs" + u, [128, 1], F32)),
        )

    def p_linear_block(self, tag, xsrc, wsrc, resid, out32_ap, out16_ap,
                       pre_t=False, rhs_cb=None, bias_dram=None,
                       scale_sb=None):
        nc, sc = self.nc, self.scr
        ss, sv, sp, sa = self.ss, self.sv, self.sp, self.sa
        from contextlib import ExitStack
        with ExitStack() as es:
            yacc = es.enter_context(
                nc.sbuf_tensor(f"{tag}ya", [128, NT * 2048], F32))
            with ExitStack() as es2:
                xg = es2.enter_context(
                    nc.sbuf_tensor(f"{tag}xg", [128, 16 * TOK], BF16))
                wgb = es2.enter_context(
                    nc.sbuf_tensor(f"{tag}wg", [128, 16 * 2048], BF16))
                psA = es2.enter_context(
                    nc.psum_tensor(f"{tag}pA", [128, 2048], F32))
                psB = es2.enter_context(
                    nc.psum_tensor(f"{tag}pB", [128, 2048], F32))
                self.tm_gemm(f"{tag}0", xsrc, TOK, wsrc, 0, 2048, yacc,
                             (xg, wgb, psA, psB), pre_t=pre_t, rhs_cb=rhs_cb)
                with nc.Block(f"{tag}dump") as blk:
                    @blk.sync
                    def _(s):
                        sp.wait(s)
                        ss.dinc(s.dma_start(
                            sc["yh_scr"][:, :].rearrange("(n p) c -> p n c", p=128),
                            yacc[:, :].rearrange("p (n c) -> p n c", c=2048)))
                        ss.wait(s)
                self.tm_gemm(f"{tag}1", xsrc, TOK, wsrc, 2048, 2048, yacc,
                             (xg, wgb, psA, psB), pre_t=pre_t, rhs_cb=rhs_cb)

            lb = self._ln_bufs(es)
            bias_sb = None
            if bias_dram is not None:
                bias_sb = es.enter_context(
                    nc.sbuf_tensor(f"{tag}bi", [128, D], F32))
                with nc.Block(f"{tag}bl") as blk:
                    @blk.sync
                    def _(s):
                        ss.dinc(s.dma_start(bias_sb[:, :], bias_dram))
                        ss.wait(s)
            for tc in range(NT):
                with nc.Block(f"{tag}e{tc}") as blk:
                    @blk.sync
                    def _(s):
                        sv.wait(s)
                        self.sq.wait(s)
                        ss.dinc(s.dma_start(
                            lb["ybuf"][:, 0:2048],
                            sc["yh_scr"][tc * 128:(tc + 1) * 128, :]))
                        ss.dinc(s.dma_start(
                            lb["xo"][:, :], resid[tc * 128:(tc + 1) * 128, :]))
                        ss.wait(s)

                    spf = sp.n
                    ssf = ss.n

                    def make_ybuf(v, tc=tc):
                        v.wait_ge(ss.h, ssf)
                        v.wait_ge(sp.h, spf)
                        v.drain()
                        ybuf = lb["ybuf"]
                        v.tensor_copy(ybuf[:, 2048:4096],
                                      yacc[:, tc * 2048:(tc + 1) * 2048])
                        v.drain()
                        if scale_sb is not None:
                            v.tensor_tensor(ybuf[:, :], ybuf[:, :],
                                            scale_sb[:, :], op=ALU.mult)
                            v.drain()
                        if bias_sb is not None:
                            v.tensor_tensor(ybuf[:, :], ybuf[:, :],
                                            bias_sb[:, :], op=ALU.add)
                            v.drain()
                        v.tensor_tensor(ybuf[:, :], ybuf[:, :], lb["xo"][:, :],
                                        op=ALU.add)

                    self.emit_ln_tc(
                        blk, make_ybuf, lb,
                        out32_ap[tc * 128:(tc + 1) * 128, :],
                        None if out16_ap is None
                        else out16_ap[tc * 128:(tc + 1) * 128, :])

    def p5_ffn(self):
        nc, ii, sc = self.nc, self.inp, self.scr
        sg, ss, sa, sv, st, sq, sp = (self.sg, self.ss, self.sa, self.sv,
                                      self.st, self.sq, self.sp)
        from contextlib import ExitStack
        with ExitStack() as es:
            zbf = es.enter_context(nc.sbuf_tensor("f5zb", [128, 8 * TOK], BF16))
            b1s = es.enter_context(nc.sbuf_tensor("f5b1", [128, 2 * NF], F32))
            pfm = es.enter_context(nc.sbuf_tensor("f5p", [128, 8 * TOK], BF16))

            with ExitStack() as es2:
                xfm = es2.enter_context(nc.sbuf_tensor("f5xf", [128, KC * TOK], BF16))
                u1s = es2.enter_context(nc.sbuf_tensor("f5u1", [128, KC * 128], BF16))
                psA = es2.enter_context(nc.psum_tensor("f5pA", [128, 512], F32))
                psB = es2.enter_context(nc.psum_tensor("f5pB", [128, 512], F32))
                with nc.Block("p5ax") as blk:
                    @blk.sync
                    def _(s):
                        for kc in range(KC):
                            ss.dinc(s.dma_start(
                                xfm[:, kc * TOK:(kc + 1) * TOK],
                                sc["resbf"][:, kc * 128:(kc + 1) * 128],
                                transpose=True))
                        ss.dinc(s.dma_start(b1s[:, :], ii["b1c"][:, :]))
                        ss.wait(s)
                ps_base = sp.n
                for rc in range(8):
                    with nc.Block(f"p5ar{rc}") as blk:
                        @blk.gpsimd
                        def _(g):
                            sg.dinc(g.dma_start(
                                u1s[:, :].rearrange("p (k c) -> p k c", c=128),
                                ii["U1"][:, rc * 128:(rc + 1) * 128].rearrange(
                                    "(k p) c -> p k c", p=128)))
                            sg.wait(g)

                        sgf = self.sg.n
                        ssf = ss.n

                        @blk.tensor
                        def _(t):
                            t.wait_ge(sg.h, sgf)
                            t.wait_ge(ss.h, ssf)
                            for th in range(2):
                                r = rc * 2 + th
                                ps = psA if r % 2 == 0 else psB
                                t.wait_ge(sp.h, ps_base + max(0, r - 1))
                                last = None
                                for kc in range(KC):
                                    last = t.matmul(
                                        ps[:, :],
                                        u1s[:, kc * 128:(kc + 1) * 128],
                                        xfm[:, kc * TOK + th * 512:
                                            kc * TOK + (th + 1) * 512],
                                        start=(kc == 0), stop=(kc == KC - 1))
                                st.inc(last)

                        tb = st.n - 2

                        @blk.vector
                        def _(v):
                            v.drain()
                            for th in range(2):
                                r = rc * 2 + th
                                ps = psA if r % 2 == 0 else psB
                                v.wait_ge(st.h, tb + th + 1)
                                sp.inc(v.tensor_copy(
                                    pfm[:, rc * TOK + th * 512:
                                        rc * TOK + (th + 1) * 512], ps[:, :]))

            GSZ = 8
            groups = [list(range(a, min(a + GSZ, NF))) for a in range(0, NF, GSZ)]
            with ExitStack() as es2:
                zsb = es2.enter_context(nc.sbuf_tensor("f5z", [128, 8 * TOK], F32))
                v1G = es2.enter_context(
                    nc.sbuf_tensor("f5v1", [128, 2 * GSZ * 16 * 128], BF16))
                u2G = es2.enter_context(
                    nc.sbuf_tensor("f5u2", [128, 2 * GSZ * 1024], BF16))
                sgb = es2.enter_context(nc.sbuf_tensor("f5sg", [128, 2 * 256], F32))
                upb = es2.enter_context(nc.sbuf_tensor("f5up", [128, 256], F32))
                abf = es2.enter_context(nc.sbuf_tensor("f5ab", [128, 2 * 256], BF16))
                hgP = [es2.enter_context(nc.psum_tensor("f5hA", [128, 256], F32)),
                       es2.enter_context(nc.psum_tensor("f5hC", [128, 256], F32))]
                huP = [es2.enter_context(nc.psum_tensor("f5hB", [128, 256], F32)),
                       es2.enter_context(nc.psum_tensor("f5hD", [128, 256], F32))]
                zps = [es2.enter_context(nc.psum_tensor(f"f5z{b}", [128, 512], F32))
                       for b in range(4)]

                def load_group(g, gi):
                    par = gi % 2
                    swp = self.sw[par]
                    for fj, f in enumerate(groups[gi]):
                        base = par * GSZ * 16 * 128 + fj * 16 * 128
                        swp.dinc(g.dma_start(
                            v1G[:, base:base + 8 * 128].rearrange(
                                "p (k c) -> p k c", c=128),
                            ii["V1"][:, f * 128:(f + 1) * 128].rearrange(
                                "(k p) c -> p k c", p=128)))
                        swp.dinc(g.dma_start(
                            v1G[:, base + 8 * 128:base + 16 * 128].rearrange(
                                "p (k c) -> p k c", c=128),
                            ii["V1"][:, INTER + f * 128:INTER + (f + 1) * 128]
                            .rearrange("(k p) c -> p k c", p=128)))
                        ub = par * GSZ * 1024 + fj * 1024
                        swp.dinc(g.dma_start(
                            u2G[:, ub:ub + 1024],
                            ii["U2"][f * 128:(f + 1) * 128, :]))

                with nc.Block("p5bw") as blk:
                    @blk.gpsimd
                    def _(g):
                        load_group(g, 0)

                for gi, grp in enumerate(groups):
                    ng = len(grp)
                    N = 4 * ng
                    par = gi % 2
                    with nc.Block(f"p5b{gi}") as blk:
                        if gi + 1 < len(groups):
                            stf_pre = st.n

                            @blk.gpsimd
                            def _(g):
                                g.wait_ge(st.h, stf_pre)
                                load_group(g, gi + 1)

                        sw_t = self.sw[par].n
                        st_base = st.n
                        sv_base = sv.n
                        sq_base = sq.n
                        sv_pre = sv.n

                        def inc_H(c):
                            return 1 if c == 0 else 2 * c

                        def inc_Z(c):
                            return 2 * c + 3 if c < N - 1 else 2 * N

                        def act_inc(c):
                            return c + (c // ng) + 1

                        @blk.tensor
                        def _(t):
                            t.wait_ge(self.sw[par].h, sw_t)
                            t.wait_ge(sv.h, sv_pre)

                            def emit_H(c):
                                tt, fj = c // ng, c % ng
                                if c >= 2:
                                    t.wait_ge(sq.h, sq_base + c - 1)
                                    t.wait_ge(sv.h, sv_base + act_inc(c - 2))
                                base = par * GSZ * 16 * 128 + fj * 16 * 128
                                lastm = None
                                for rc in range(8):
                                    lastm = t.matmul(
                                        hgP[c % 2][:, :],
                                        v1G[:, base + rc * 128:base + (rc + 1) * 128],
                                        pfm[:, rc * TOK + tt * 256:
                                            rc * TOK + (tt + 1) * 256],
                                        start=(rc == 0), stop=(rc == 7))
                                for rc in range(8):
                                    lastm = t.matmul(
                                        huP[c % 2][:, :],
                                        v1G[:, base + (8 + rc) * 128:
                                            base + (9 + rc) * 128],
                                        pfm[:, rc * TOK + tt * 256:
                                            rc * TOK + (tt + 1) * 256],
                                        start=(rc == 0), stop=(rc == 7))
                                st.inc(lastm)

                            def emit_Z(c):
                                tt, fj = c // ng, c % ng
                                t.wait_ge(sv.h, sv_base + act_inc(c))
                                ub = par * GSZ * 1024 + (c % ng) * 1024
                                lastm = None
                                for rc in range(8):
                                    b, half = rc // 2, rc % 2
                                    lastm = t.matmul(
                                        zps[b][:, half * 256:(half + 1) * 256],
                                        u2G[:, ub + rc * 128:ub + (rc + 1) * 128],
                                        abf[:, (c % 2) * 256:(c % 2) * 256 + 256],
                                        start=(fj == 0 and half == 0),
                                        stop=(fj == ng - 1 and half == 1))
                                st.inc(lastm)

                            for c in range(N):
                                emit_H(c)
                                if c >= 1:
                                    emit_Z(c - 1)
                            emit_Z(N - 1)

                        @blk.scalar
                        def _(s2):
                            for c in range(N):
                                tt, fj = c // ng, c % ng
                                f = grp[fj]
                                s2.wait_ge(st.h, st_base + inc_H(c))
                                if c >= 2:
                                    s2.wait_ge(sv.h, sv_base + act_inc(c - 2))
                                sq.inc(s2.activation(
                                    sgb[:, (c % 2) * 256:(c % 2) * 256 + 256],
                                    hgP[c % 2][:, :], ACT.Silu,
                                    bias=b1s[:, f:f + 1]))

                        @blk.vector
                        def _(v):
                            v.drain()
                            for c in range(N):
                                tt, fj = c // ng, c % ng
                                f = grp[fj]
                                v.wait_ge(st.h, st_base + inc_H(c))
                                v.wait_ge(sq.h, sq_base + c + 1)
                                if c >= 2:
                                    v.wait_ge(st.h, st_base + inc_Z(c - 2))
                                v.tensor_scalar(upb[:, :], huP[c % 2][:, :],
                                                b1s[:, NF + f:NF + f + 1],
                                                None, op0=ALU.add)
                                v.drain()
                                sv.inc(v.tensor_tensor(
                                    abf[:, (c % 2) * 256:(c % 2) * 256 + 256],
                                    sgb[:, (c % 2) * 256:(c % 2) * 256 + 256],
                                    upb[:, :], op=ALU.mult))
                                v.drain()
                                if fj == ng - 1:
                                    tt_ = tt
                                    v.wait_ge(st.h, st_base + inc_Z(c))
                                    for b in range(4):
                                        for half in range(2):
                                            rc = b * 2 + half
                                            dst = zsb[:, rc * TOK + tt_ * 256:
                                                      rc * TOK + (tt_ + 1) * 256]
                                            srcp = zps[b][:, half * 256:
                                                          (half + 1) * 256]
                                            if gi == 0:
                                                v.tensor_copy(dst, srcp)
                                            else:
                                                v.tensor_tensor(dst, dst, srcp,
                                                                op=ALU.add)
                                    sv.inc(v.drain())

                with nc.Block("p5zc") as blk:
                    svf = sv.n

                    @blk.vector
                    def _(v):
                        v.wait_ge(sv.h, svf)
                        v.drain()
                        v.tensor_copy(zbf[:, :], zsb[:, :])
                        sv.inc(v.drain())

            with ExitStack() as es2:
                v2s = es2.enter_context(nc.sbuf_tensor("f5v2", [128, 8 * D], BF16))
                b2s = es2.enter_context(nc.sbuf_tensor("f5b2", [128, D], F32))
                ps8 = es2.enter_context(nc.psum_tensor("f5p8", [128, D], F32))
                lb = self._ln_bufs(es2)
                with nc.Block("p5cl") as blk:
                    @blk.gpsimd
                    def _(g):
                        for rc in range(8):
                            sg.dinc(g.dma_start(
                                v2s[:, rc * D:(rc + 1) * D],
                                ii["V2"][rc * 128:(rc + 1) * 128, :]))
                        sg.wait(g)

                    @blk.sync
                    def _(s):
                        ss.dinc(s.dma_start(b2s[:, :], ii["b2b"][:, :]))
                        ss.wait(s)

                for tc in range(NT):
                    with nc.Block(f"p5ct{tc}") as blk:
                        svf = sv.n
                        sgf = sg.n

                        @blk.sync
                        def _(s):
                            s.wait_ge(sv.h, svf)
                            ss.dinc(s.dma_start(
                                lb["xo"][:, :],
                                sc["res32"][tc * 128:(tc + 1) * 128, :]))
                            ss.wait(s)

                        ssf = ss.n

                        @blk.tensor
                        def _(t):
                            t.wait_ge(sv.h, svf)
                            t.wait_ge(sg.h, sgf)
                            t.wait_ge(ss.h, ssf)
                            last = None
                            for rc in range(8):
                                for nt_ in range(8):
                                    last = t.matmul(
                                        ps8[:, nt_ * 512:(nt_ + 1) * 512],
                                        zbf[:, rc * TOK + tc * 128:
                                            rc * TOK + (tc + 1) * 128],
                                        v2s[:, rc * D + nt_ * 512:
                                            rc * D + (nt_ + 1) * 512],
                                        start=(rc == 0), stop=(rc == 7))
                            st.inc(last)

                        stf = st.n

                        def make_ybuf(v, tc=tc, stf=stf):
                            v.wait_ge(st.h, stf)
                            v.drain()
                            v.tensor_tensor(lb["ybuf"][:, :], ps8[:, :],
                                            b2s[:, :], op=ALU.add)
                            v.drain()
                            v.tensor_tensor(lb["ybuf"][:, :], lb["ybuf"][:, :],
                                            lb["xo"][:, :], op=ALU.add)

                        self.emit_ln_tc(
                            blk, make_ybuf, lb,
                            sc["res32"][tc * 128:(tc + 1) * 128, :],
                            sc["resbf"][tc * 128:(tc + 1) * 128, :])

    def _p5_fchunk(self, blk, gi, tt, fj, f, ng, pfm, v1s, u2s, b1s,
                   sgb, upb, abf, hgA, huA, zps):
        nc, ii, sc = self.nc, self.inp, self.scr
        sg, ss, sv, st, sq = self.sg, self.ss, self.sv, self.st, self.sq

        @blk.gpsimd
        def _(g):
            st.wait(g)
            sg.dinc(g.dma_start(
                v1s[:, 0:8 * 128].rearrange("p (k c) -> p k c", c=128),
                ii["V1"][:, f * 128:(f + 1) * 128].rearrange(
                    "(k p) c -> p k c", p=128)))
            sg.dinc(g.dma_start(
                v1s[:, 8 * 128:16 * 128].rearrange("p (k c) -> p k c", c=128),
                ii["V1"][:, INTER + f * 128:INTER + (f + 1) * 128].rearrange(
                    "(k p) c -> p k c", p=128)))
            sg.dinc(g.dma_start(u2s[:, :], ii["U2"][f * 128:(f + 1) * 128, :]))
            sg.wait(g)

        sgf = sg.n
        sqf_pre = sq.n
        svf_pre0 = sv.n

        @blk.tensor
        def _(t):
            t.wait_ge(sg.h, sgf)
            t.wait_ge(sq.h, sqf_pre)
            t.wait_ge(sv.h, svf_pre0)
            last = None
            for rc in range(8):
                last = t.matmul(hgA[:, :], v1s[:, rc * 128:(rc + 1) * 128],
                                pfm[:, rc * TOK + tt * 256:rc * TOK + (tt + 1) * 256],
                                start=(rc == 0), stop=(rc == 7))
            for rc in range(8):
                last = t.matmul(huA[:, :], v1s[:, (8 + rc) * 128:(9 + rc) * 128],
                                pfm[:, rc * TOK + tt * 256:rc * TOK + (tt + 1) * 256],
                                start=(rc == 0), stop=(rc == 7))
            st.inc(last)

        stf = st.n

        @blk.scalar
        def _(s2):
            st.wait(s2)
            s2.activation(sgb[:, :], hgA[:, :], ACT.Silu,
                          bias=b1s[:, f:f + 1])
            sq.inc(s2.drain())

        sqf = sq.n
        svf_pre = sv.n

        @blk.vector
        def _(v):
            v.wait_ge(st.h, stf)
            v.wait_ge(sq.h, sqf)
            v.drain()
            v.tensor_scalar(upb[:, :], huA[:, :], b1s[:, NF + f:NF + f + 1],
                            None, op0=ALU.add)
            v.drain()
            v.tensor_tensor(abf[:, :], sgb[:, :], upb[:, :], op=ALU.mult)
            sv.inc(v.drain())

        svf = sv.n

        @blk.tensor
        def _(t):
            t.wait_ge(sv.h, svf)
            last = None
            for rc in range(8):
                b, half = rc // 2, rc % 2
                last = t.matmul(
                    zps[b][:, half * 256:(half + 1) * 256],
                    u2s[:, rc * 128:(rc + 1) * 128], abf[:, :],
                    start=(fj == 0 and half == 0),
                    stop=(fj == ng - 1 and half == 1))
            st.inc(last)

    def p6_maskprep(self, bmx):
        nc, ii, sc = self.nc, self.inp, self.scr
        ss, sv, st, sa = self.ss, self.sv, self.st, self.sa
        from contextlib import ExitStack
        with ExitStack() as es:
            msb = es.enter_context(nc.sbuf_tensor("m6i", [128, D], I32))
            rsf = es.enter_context(nc.sbuf_tensor("m6g", [128, 128], F32))
            rsb = es.enter_context(nc.sbuf_tensor("m6b", [128, 128], BF16))
            rsbT = es.enter_context(nc.sbuf_tensor("m6t", [128, 128], BF16))
            bmT = es.enter_context(nc.sbuf_tensor("m6m", [128, 128], F32))

            for i in range(32):
                with nc.Block(f"p6mt{i}") as blk:
                    svf = sv.n
                    ssf0 = ss.n

                    @blk.sync
                    def _(s):
                        s.wait_ge(sv.h, svf)
                        ss.dinc(s.dma_start(
                            msb[:, :], ii["l1_mask"][i * 128:(i + 1) * 128, :]))
                        ss.wait(s)

                    ssf = ss.n

                    @blk.vector
                    def _(v):
                        v.wait_ge(ss.h, ssf)
                        v.drain()
                        v.tensor_reduce(
                            rsf[:, :],
                            msb[:, :].rearrange("p (a b) -> p a b", b=32),
                            axis=AX, op=ALU.add)
                        v.drain()
                        v.tensor_copy(rsb[:, :], rsf[:, :])
                        sv.inc(v.drain())

                    @blk.sync
                    def _(s):
                        sv.wait(s)
                        ss.dinc(s.dma_start(rsbT[:, :], rsb[:, :],
                                            transpose=True))
                        ss.wait(s)

                    ssf2 = ss.n

                    @blk.vector
                    def _(v):
                        v.wait_ge(ss.h, ssf2)
                        v.drain()
                        v.tensor_reduce(
                            bmT[:, i * 4:(i + 1) * 4],
                            rsbT[:, :].rearrange("p (a b) -> p a b", b=32),
                            axis=AX, op=ALU.add)
                        sv.inc(v.drain())

            with nc.Block("p6mf") as blk:
                svf = sv.n

                @blk.vector
                def _(v):
                    v.wait_ge(sv.h, svf)
                    v.drain()
                    v.tensor_scalar(bmT[:, :], bmT[:, :], 0.0, None,
                                    op0=ALU.is_gt)
                    sv.inc(v.drain())

                @blk.sync
                def _(s):
                    sv.wait(s)
                    ss.dinc(s.dma_start(sc["bm_scr"][0:128, 0:128], bmT[:, :]))
                    ss.wait(s)
                    for kc in range(KC):
                        for pa in range(4):
                            kb = kc * 4 + pa
                            dst = bmx[pa * 32:(pa + 1) * 32,
                                      kc * 128:(kc + 1) * 128]
                            srcr = sc["bm_scr"][kb:kb + 1, 0:128]
                            ss.dinc(s.dma_start(
                                dst, srcr.to_broadcast([32, 128])))
                    ss.wait(s)

    def p7_prep(self, s_bcast):
        nc, ii, sc = self.nc, self.inp, self.scr
        sg, ss, sv, st, sq, sa = (self.sg, self.ss, self.sv, self.st,
                                  self.sq, self.sa)
        from contextlib import ExitStack
        with ExitStack() as es:
            a_sb = es.enter_context(nc.sbuf_tensor("d7a", [32, D], BF16))
            bT = es.enter_context(nc.sbuf_tensor("d7b", [128, D], BF16))
            wt = es.enter_context(nc.sbuf_tensor("d7w", [128, D], BF16))
            et = es.enter_context(nc.sbuf_tensor("d7e", [128, D], BF16))
            sqb = es.enter_context(nc.sbuf_tensor("d7s", [128, D], BF16))
            ones = es.enter_context(nc.sbuf_tensor("d7o", [128, 1], BF16))
            cnr = es.enter_context(nc.sbuf_tensor("d7c", [1, D], F32))
            mag = es.enter_context(nc.sbuf_tensor("d7m", [1, D], F32))

            with nc.Block("p7a") as blk:
                @blk.gpsimd
                def _(g):
                    sg.dinc(g.dma_start(a_sb[:, :], ii["lf_A"][:, :]))
                    sg.wait(g)

                saf = sa.n + 16

                @blk.sync
                def _(s):
                    s.wait_ge(sa.h, saf)
                    ss.dinc(s.dma_start(bT[:, :], sc["lfB_bf"][:, :],
                                        transpose=True))
                    ss.dinc(s.dma_start(mag[:, :], ii["lf_magb"][:, :]))
                    ss.wait(s)

                @blk.vector
                def _(v):
                    v.memset(ones[:, :], 1.0)
                    v.memset(sqb[:, 0:96], 0.0)
                    sv.inc(v.drain())

                @blk.scalar
                def _(s2):
                    sv.wait(s2)
                    sa.dinc(s2.dma_start(
                        sc["lfB_bf"][:, 32:128].rearrange(
                            "(a p) c -> p a c", p=128),
                        sqb[:, 0:96].unsqueeze(1).to_broadcast([128, 32, 96])))
                    sa.wait(s2)

            es1 = ExitStack()
            ps8 = es1.enter_context(nc.psum_tensor("d7p", [128, D], F32))
            for kc in range(KC):
                with nc.Block(f"p7b{kc}") as blk:
                    svf = sv.n

                    @blk.sync
                    def _(s):
                        s.wait_ge(sv.h, svf)
                        ss.dinc(s.dma_start(
                            wt[:, :], sc["lfW_bf"][:, kc * 128:(kc + 1) * 128],
                            transpose=True))
                        ss.wait(s)

                    ssf = ss.n
                    sgf = sg.n

                    @blk.tensor
                    def _(t):
                        t.wait_ge(ss.h, ssf)
                        t.wait_ge(sg.h, sgf)
                        t.wait_ge(sv.h, svf)
                        last = None
                        for nt_ in range(8):
                            last = t.matmul(
                                ps8[:, nt_ * 512:(nt_ + 1) * 512],
                                a_sb[:, kc * 128:(kc + 1) * 128],
                                bT[0:32, nt_ * 512:(nt_ + 1) * 512],
                                start=True, stop=True)
                        st.inc(last)

                    stf = st.n

                    @blk.vector
                    def _(v):
                        v.wait_ge(st.h, stf)
                        v.drain()
                        v.tensor_tensor(et[:, :], wt[:, :], ps8[:, 0:D],
                                        op=ALU.add)
                        sv.inc(v.drain())

                    @blk.scalar
                    def _(s2):
                        sv.wait(s2)
                        sa.dinc(s2.dma_start(
                            sc["effW_scr"][kc * 128:(kc + 1) * 128, :], et[:, :]))
                        sa.wait(s2)

            es1.close()
            with ExitStack() as es2:
                pscn = es2.enter_context(nc.psum_tensor("d7n", [1, D], F32))
                for kc in range(KC):
                    with nc.Block(f"p7c{kc}") as blk:
                        svf = sv.n

                        @blk.sync
                        def _(s):
                            s.wait_ge(sv.h, svf)
                            ss.dinc(s.dma_start(
                                et[:, :],
                                sc["effW_scr"][kc * 128:(kc + 1) * 128, :]))
                            ss.wait(s)

                        ssf = ss.n

                        @blk.vector
                        def _(v):
                            v.wait_ge(ss.h, ssf)
                            v.drain()
                            v.tensor_tensor(sqb[:, :], et[:, :], et[:, :],
                                            op=ALU.mult)
                            sv.inc(v.drain())

                        svf2 = sv.n

                        @blk.tensor
                        def _(t):
                            t.wait_ge(sv.h, svf2)
                            last = None
                            for nt_ in range(8):
                                last = t.matmul(
                                    pscn[:, nt_ * 512:(nt_ + 1) * 512],
                                    ones[:, :],
                                    sqb[:, nt_ * 512:(nt_ + 1) * 512],
                                    start=(kc == 0), stop=(kc == KC - 1))
                            st.inc(last)

                with nc.Block("p7d") as blk:
                    stf = st.n

                    @blk.vector
                    def _(v):
                        v.wait_ge(st.h, stf)
                        v.drain()
                        v.tensor_copy(cnr[:, :], pscn[:, 0:D])
                        sv.inc(v.drain())

                    @blk.scalar
                    def _(s2):
                        sv.wait(s2)
                        sq.inc(s2.activation(cnr[:, :], cnr[:, :], ACT.Sqrt))

                    @blk.vector
                    def _(v):
                        sq.wait(v)
                        v.reciprocal(cnr[:, :], cnr[:, :])
                        v.drain()
                        v.tensor_tensor(cnr[:, :], cnr[:, :], mag[:, :],
                                        op=ALU.mult)
                        sv.inc(v.drain())

                    @blk.sync
                    def _(s):
                        sv.wait(s)
                        ss.dinc(s.dma_start(sc["row_scr"][:, :], cnr[:, :]))
                        ss.wait(s)
                        ss.dinc(s.dma_start(
                            s_bcast[:, :],
                            sc["row_scr"][0:1, :].to_broadcast([128, D])))
                        ss.wait(s)


def _build(upto=100):
    p = P(upto)
    nc = p.nc
    sc = p.scr
    p.p0_casts()
    if upto >= 1:
        p.p1_qkv()
    if upto >= 2:
        with (
            nc.sbuf_tensor("rkp", [128, KVH * 2176], BF16) as rkp,
            nc.sbuf_tensor("vtm", [128, KVH * 18 * 128], BF16) as vtm,
            nc.sbuf_tensor("ckT", [128, KVH * 128], BF16) as ckT,
            nc.sbuf_tensor("cvt", [128, KVH * 128], BF16) as cvt,
            nc.sbuf_tensor("gts", [128, NT * 96], F32) as gts,
        ):
            p.p2_prep(rkp, vtm, ckT, cvt, gts)
            if upto >= 3:
                p.p3_attn(rkp, vtm, ckT, cvt, gts)
    if upto >= 4:
        p.p_linear_block("p4", sc["comb_scr"], sc["wo_bf"],
                         p.inp["x_own"][:, :], sc["res32"], sc["resbf"])
    if upto >= 5:
        p.p5_ffn()
    if upto >= 6:
        with nc.sbuf_tensor("bmx", [128, KC * 128], F32) as bmx:
            p.p6_maskprep(bmx)

            def mask_cb(v, wgb, j, kc, n0, bmx=bmx):
                nb0 = n0 // 32
                nbs = 2048 // 32
                sl = wgb[:, j * 2048:(j + 1) * 2048].rearrange(
                    "p (a b) -> p a b", b=32)
                mk = bmx[:, kc * 128 + nb0:kc * 128 + nb0 + nbs]
                v.tensor_tensor(sl, sl, mk.unsqueeze(2).to_broadcast(
                    [128, nbs, 32]), op=ALU.mult)

            p.p_linear_block("p6", sc["resbf"], sc["l1w_bf"],
                             sc["res32"][:, :], sc["res32"], sc["resbf"],
                             rhs_cb=mask_cb, bias_dram=p.inp["l1_biasb"][:, :])
    if upto >= 7:
        with nc.sbuf_tensor("sbc", [128, D], F32) as s_bcast:
            p.p7_prep(s_bcast)
            p.p_linear_block("p7", sc["resbf"], sc["effW_scr"],
                             sc["res32"][:, :], p.y_out, None,
                             pre_t=True, scale_sb=s_bcast)
    return p


def _stage_inputs(inputs):
    x = np.ascontiguousarray(inputs["L0_attn_hidden_states"])
    g = lambda n: np.ascontiguousarray(inputs[n])
    mem_kv = g("L0_attn_mem_kv")
    shared = {
        "w_qkv": g("L0_attn_w_qkv"), "w_o": g("L0_attn_w_o"),
        "k_posT": np.ascontiguousarray(
            inputs["L0_attn_k_pos"].transpose(2, 0, 1).reshape(DH, KVH * CB)),
        "v_posT": np.ascontiguousarray(
            inputs["L0_attn_v_pos"].transpose(2, 0, 1).reshape(DH, KVH * CB)),
        "k_w1": g("L0_attn_k_w1"),
        "k_b1c": np.ascontiguousarray(
            inputs["L0_attn_k_b1"].reshape(16, 128).T),
        "k_w2": g("L0_attn_k_w2"),
        "k_b2c": np.ascontiguousarray(inputs["L0_attn_k_b2"].reshape(128, 1)),
        "v_w1": g("L0_attn_v_w1"),
        "v_b1c": np.ascontiguousarray(
            inputs["L0_attn_v_b1"].reshape(16, 128).T),
        "v_w2": g("L0_attn_v_w2"),
        "v_b2b": np.ascontiguousarray(np.broadcast_to(
            inputs["L0_attn_v_b2"][None, :], (128, DH))),
        "mem_kT": np.ascontiguousarray(mem_kv[0, :, 0, :].T),
        "mem_vb": np.ascontiguousarray(mem_kv[1, :, 0, :].reshape(1, KVH * DH)),
        "w_gate": g("L0_attn_w_gate"),
        "b_gateb": np.ascontiguousarray(np.broadcast_to(
            inputs["L0_attn_b_gate"][None, :], (128, 96))),
        "U1": g("L0_ffn_U1"), "V1": g("L0_ffn_V1"),
        "U2": g("L0_ffn_U2"), "V2": g("L0_ffn_V2"),
        "b1c": np.ascontiguousarray(
            inputs["L0_ffn_b1"].reshape(2 * NF, 128).T),
        "b2b": np.ascontiguousarray(np.broadcast_to(
            inputs["L0_ffn_b2"][None, :], (128, D))),
        "l1_mask": g("L1_attn_mask"), "l1_w": g("L1_attn_weight"),
        "l1_biasb": np.ascontiguousarray(np.broadcast_to(
            inputs["L1_attn_bias"][None, :], (128, D))),
        "lf_W": g("L1_ffn_W"), "lf_A": g("L1_ffn_lora_A"),
        "lf_B": g("L1_ffn_lora_B"),
        "lf_magb": np.ascontiguousarray(
            inputs["L1_ffn_magnitude"].reshape(1, D)),
        "e4": np.ascontiguousarray(
            (np.arange(128)[:, None] // 32 == np.arange(4)[None, :])
            .astype(np.float32)),
        "ident": np.eye(128, dtype=np.float32),
    }
    tabs = [_host_tables(0), _host_tables(1)]
    in_maps = []
    own_rows = []
    for c in range(N_CORES):
        b, h = c // 2, c % 2
        rows = np.concatenate(
            [np.arange(j * 128, (j + 1) * 128) for j in range(h, NTF, 2)])
        own_rows.append(rows)
        m = dict(shared)
        m["x_full"] = np.ascontiguousarray(x[b])
        m["x_own"] = np.ascontiguousarray(x[b][rows])
        for k, v in tabs[h].items():
            m[k] = v
        in_maps.append(m)
    return in_maps, own_rows


def kernel(**inputs):
    in_maps, own_rows = _stage_inputs(inputs)
    p = _build()
    res = run_bass_kernel_spmd(p.nc, in_maps, core_ids=list(range(N_CORES)))
    out = np.zeros((B, S, D), dtype=np.float32)
    for c in range(N_CORES):
        out[c // 2, own_rows[c]] = res.results[c]["y_out"]
    return out


# version 15 (final)
# speedup vs baseline: 1.0268x; 1.0213x over previous
import numpy as np
import concourse.bass as bass
import concourse.mybir as mybir
from concourse.bass_utils import run_bass_kernel_spmd

F32 = mybir.dt.float32
BF16 = mybir.dt.bfloat16
I32 = mybir.dt.int32
AX = mybir.AxisListType.X
ALU = mybir.AluOpType
ACT = mybir.ActivationFunctionType

NEG = -1.0e30
B, S, D = 4, 2048, 4096
H, KVH, DH = 32, 8, 128
G = H // KVH
CB = 16
WIN = 64
TOK = 1024
NT = TOK // 128
NTF = S // 128
KC = D // 128
INTER = 11008
RANK = 1024
NF = INTER // 128
SCALE = float(DH) ** -0.5
EPS = 1e-5
N_CORES = 8



def _host_tables(h):
    t_abs = np.zeros(TOK, dtype=np.int64)
    for tau in range(NT):
        t_abs[tau * 128:(tau + 1) * 128] = (2 * tau + h) * 128 + np.arange(128)

    inv = 1.0 / (10000.0 ** (np.arange(0, DH, 2, dtype=np.float64) / DH))
    ang_own = t_abs[:, None].astype(np.float64) * inv[None, :]
    ang_all = np.arange(S, dtype=np.float64)[:, None] * inv[None, :]

    cm = np.full((TOK, 128), NEG, dtype=np.float32)
    cm[:, 127] = 0.0
    ends = np.arange(127) * CB + CB - 1
    cm[:, 0:127] = np.where(ends[None, :] < t_abs[:, None], 0.0, NEG)

    own = np.full((TOK, 256), NEG, dtype=np.float32)
    for i in range(TOK):
        a = int(t_abs[i])
        tau = i // 128
        base = 2 * tau * 128
        b0 = (a // CB) * CB
        own[i, b0 - base:a - base + 1] = 0.0

    sl = np.full((TOK, 384), NEG, dtype=np.float32)
    ar = np.arange(384)
    for i in range(TOK):
        a = int(t_abs[i])
        tau = i // 128
        ks = 2 * tau * 128 - 64 + ar
        vis = (ks >= a - WIN) & (ks <= a) & (ks >= 0)
        sl[i, vis] = 0.0

    return dict(
        cosq=np.cos(ang_own).astype(np.float32),
        sinq=np.sin(ang_own).astype(np.float32),
        cosk=np.cos(ang_all).astype(np.float32),
        sink=np.sin(ang_all).astype(np.float32),
        cmask=cm, ownmask=own, slmask=sl)



class Sem:
    def __init__(self, nc, name):
        self.h = nc.semaphore(name).__enter__()
        self.n = 0

    def inc(self, instr, k=1):
        instr.then_inc(self.h, k)
        self.n += k
        return instr

    def dinc(self, instr):
        return self.inc(instr, 16)

    def wait(self, eng, val=None):
        eng.wait_ge(self.h, self.n if val is None else val)


class P:

    def __init__(self, upto=100):
        self.upto = upto
        nc = bass.Bass("TRN2", target_bir_lowering=False,
                       detect_race_conditions=False)
        self.nc = nc
        self.inp = {}
        self.scr = {}
        self._declare()
        self.sg = Sem(nc, "sg")
        self.ss = Sem(nc, "ss")
        self.sa = Sem(nc, "sa")
        self.st = Sem(nc, "st")
        self.sv = Sem(nc, "sv")
        self.sq = Sem(nc, "sq")
        self.sp = Sem(nc, "sp")
        self.sw = [Sem(nc, "sw0"), Sem(nc, "sw1")]
        self.sgp = Sem(nc, "sgp")
        self.sk = [Sem(nc, "sk0"), Sem(nc, "sk1")]
        self.sx = [Sem(nc, "sx0"), Sem(nc, "sx1")]

    def _declare(self):
        nc = self.nc
        ii = self.inp

        def inp(name, shape, dtype=F32):
            ii[name] = nc.dram_tensor(name, list(shape), dtype,
                                      kind="ExternalInput").ap()

        inp("x_full", (S, D)); inp("x_own", (TOK, D))
        inp("w_qkv", (6144, D)); inp("w_o", (D, D))
        inp("k_posT", (DH, KVH * CB)); inp("v_posT", (DH, KVH * CB))
        inp("k_w1", (2048, 2048)); inp("k_b1c", (128, 16))
        inp("k_w2", (DH, 2048)); inp("k_b2c", (128, 1))
        inp("v_w1", (2048, 2048)); inp("v_b1c", (128, 16))
        inp("v_w2", (DH, 2048)); inp("v_b2b", (128, DH))
        inp("mem_kT", (DH, KVH)); inp("mem_vb", (1, KVH * DH))
        inp("w_gate", (96, D)); inp("b_gateb", (128, 96))
        inp("U1", (D, RANK)); inp("V1", (RANK, 2 * INTER))
        inp("U2", (INTER, RANK)); inp("V2", (RANK, D))
        inp("b1c", (128, 2 * NF)); inp("b2b", (128, D))
        inp("l1_mask", (D, D), I32); inp("l1_w", (D, D))
        inp("l1_biasb", (128, D))
        inp("lf_W", (D, D)); inp("lf_A", (32, D)); inp("lf_B", (D, 32))
        inp("lf_magb", (1, D)); inp("e4", (128, 4)); inp("ident", (128, 128))
        inp("cosq", (TOK, 64)); inp("sinq", (TOK, 64))
        inp("cosk", (S, 64)); inp("sink", (S, 64))
        inp("cmask", (TOK, 128)); inp("ownmask", (TOK, 256))
        inp("slmask", (TOK, 384))

        self.y_out = nc.dram_tensor("y_out", [TOK, D], F32,
                                    kind="ExternalOutput").ap()

        sc = self.scr

        def scratch(name, shape, dtype=BF16):
            sc[name] = nc.dram_tensor(name, list(shape), dtype,
                                      kind="Internal").ap()

        scratch("wq_bf", (6144, D)); scratch("wo_bf", (D, D))
        scratch("kw1_bf", (2048, 2048)); scratch("vw1_bf", (2048, 2048))
        scratch("kw2_bf", (DH, 2048)); scratch("vw2_bf", (DH, 2048))
        scratch("wg_bf", (96, D)); scratch("l1w_bf", (D, D))
        scratch("lfW_bf", (D, D)); scratch("lfB_bf", (D, 128))
        scratch("x_bf", (S, D)); scratch("xown_bf", (TOK, D))
        scratch("k_scr", (S, KVH * DH)); scratch("rk_scr", (S, KVH * DH))
        scratch("v_scr", (S, KVH * DH))
        scratch("q_scr", (TOK, D)); scratch("rq_scr", (TOK, D))
        scratch("res32", (TOK, D), F32)
        sc["comb_scr"] = sc["x_bf"][0:TOK, :]
        sc["resbf"] = sc["x_bf"][TOK:2 * TOK, :]
        sc["effW_scr"] = sc["wq_bf"][0:D, :]
        sc["yh_scr"] = sc["q_scr"][:, :].bitcast(F32)
        scratch("bm_scr", (128, 128), F32)
        scratch("row_scr", (1, D), F32)
        scratch("gates_scr", (128, NT * 96), F32)

    def p0_casts(self):
        nc, ii, sc = self.nc, self.inp, self.scr
        sg = self.sg
        ss, sa, sv = self.ss, self.sa, self.sv
        sw0 = self.sw[0]
        with (
            nc.sbuf_tensor("p0xa", [128, 2 * D], F32) as xstg,
            nc.sbuf_tensor("p0xb", [128, 2 * D], BF16) as xbst,
            nc.Block("p0") as blk,
        ):
            @blk.gpsimd
            def _(g):
                sg.dinc(g.dma_start(sc["wq_bf"][4096:6144, :],
                                    ii["w_qkv"][4096:6144, :]))
                sg.wait(g)

            def _waves23(g):
                sw0.dinc(g.dma_start(sc["xown_bf"][:, :], ii["x_own"][:, :]))
                sw0.dinc(g.dma_start(sc["wq_bf"][0:4096, :],
                                     ii["w_qkv"][0:4096, :]))
                sw0.dinc(g.dma_start(sc["wg_bf"][:, :], ii["w_gate"][:, :]))
                self._p0_wave2 = sw0.n
                for dst, src in [
                    ("wo_bf", "w_o"), ("kw1_bf", "k_w1"), ("vw1_bf", "v_w1"),
                    ("kw2_bf", "k_w2"), ("vw2_bf", "v_w2"),
                    ("l1w_bf", "l1_w"), ("lfW_bf", "lf_W"),
                ]:
                    sg.dinc(g.dma_start(sc[dst][:, :], ii[src][:, :]))
                sg.dinc(g.dma_start(sc["lfB_bf"][:, 0:32], ii["lf_B"][:, :]))
                self._p0_all = sg.n
            self._waves23 = _waves23

            sv_b = sv.n
            sk_b = [self.sk[0].n, self.sk[1].n]
            sx_b = [self.sx[0].n, self.sx[1].n]

            @blk.sync
            def _(s):
                for tc in range(NTF):
                    par = tc % 2
                    if tc >= 2:
                        s.wait_ge(sv.h, sv_b + tc - 1)
                    self.sk[par].dinc(s.dma_start(
                        xstg[:, par * D:(par + 1) * D],
                        ii["x_full"][tc * 128:(tc + 1) * 128, :]))

            @blk.vector
            def _(v):
                for tc in range(NTF):
                    par = tc % 2
                    v.wait_ge(self.sk[par].h, sk_b[par] + 16 * (tc // 2 + 1))
                    if tc >= 2:
                        v.wait_ge(self.sx[par].h, sx_b[par] + 16 * (tc // 2))
                    sv.inc(v.tensor_copy(xbst[:, par * D:(par + 1) * D],
                                         xstg[:, par * D:(par + 1) * D]))

            @blk.scalar
            def _(s2):
                for tc in range(NTF):
                    par = tc % 2
                    s2.wait_ge(sv.h, sv_b + tc + 1)
                    self.sx[par].dinc(s2.dma_start(
                        sc["x_bf"][tc * 128:(tc + 1) * 128, :],
                        xbst[:, par * D:(par + 1) * D]))
                self.sx[0].wait(s2)
                self.sx[1].wait(s2)

    def tm_gemm(self, tag, xsrc, M, wsrc, n0, Nsec, yacc, bufs,
                pre_t=False, rhs_cb=None, gp_hook=None):
        nc = self.nc
        ss, st, sp, sv = self.ss, self.st, self.sp, self.sv
        xg, wgb, psA, psB = bufs
        NTC = M // 128
        GK = 16
        ps_base = sp.n
        for grp in range(2):
            kcs = list(range(grp * GK, (grp + 1) * GK))
            with nc.Block(f"{tag}_g{grp}") as blk:
                if grp == 0 and gp_hook is not None:
                    blk.gpsimd(gp_hook)
                @blk.sync
                def _(s):
                    for j, kc in enumerate(kcs):
                        ss.dinc(s.dma_start(
                            xg[:, j * M:(j + 1) * M],
                            xsrc[:, kc * 128:(kc + 1) * 128], transpose=True))
                        if pre_t:
                            ss.dinc(s.dma_start(
                                wgb[:, j * Nsec:(j + 1) * Nsec],
                                wsrc[kc * 128:(kc + 1) * 128, n0:n0 + Nsec]))
                        else:
                            ss.dinc(s.dma_start(
                                wgb[:, j * Nsec:(j + 1) * Nsec],
                                wsrc[n0:n0 + Nsec, kc * 128:(kc + 1) * 128],
                                transpose=True))
                    ss.wait(s)

                if rhs_cb is not None:
                    @blk.vector
                    def _(v):
                        ss.wait(v)
                        for j, kc in enumerate(kcs):
                            rhs_cb(v, wgb, j, kc, n0)
                        sv.inc(v.drain())

                rdy_v = sv.n
                rdy_s = ss.n

                @blk.tensor
                def _(t):
                    t.wait_ge(ss.h, rdy_s)
                    if rhs_cb is not None:
                        t.wait_ge(sv.h, rdy_v)
                    for tc in range(NTC):
                        r = grp * NTC + tc
                        ps = psA if r % 2 == 0 else psB
                        t.wait_ge(sp.h, ps_base + max(0, r - 1))
                        last = None
                        for j in range(GK):
                            lhs = xg[:, j * M + tc * 128:j * M + (tc + 1) * 128]
                            for nn0 in range(0, Nsec, 512):
                                nn = min(512, Nsec - nn0)
                                last = t.matmul(
                                    ps[:, nn0:nn0 + nn], lhs,
                                    wgb[:, j * Nsec + nn0:j * Nsec + nn0 + nn],
                                    start=(j == 0), stop=(j == GK - 1))
                        st.inc(last)

                base = st.n - NTC

                @blk.vector
                def _(v):
                    v.drain()
                    for tc in range(NTC):
                        r = grp * NTC + tc
                        ps = psA if r % 2 == 0 else psB
                        v.wait_ge(st.h, base + tc + 1)
                        dst = yacc[:, tc * Nsec:(tc + 1) * Nsec]
                        if grp == 0:
                            sp.inc(v.tensor_copy(dst, ps[:, 0:Nsec]))
                        else:
                            sp.inc(v.tensor_tensor(dst, dst, ps[:, 0:Nsec],
                                                   op=ALU.add))

    def rope_tm(self, v, out, src, cos_ap, sin_ap, nh, tmp):
        s3 = src.rearrange("p (h d two) -> p h d two", h=nh, two=2)
        o3 = out.rearrange("p (h d two) -> p h d two", h=nh, two=2)
        t3 = tmp.rearrange("p (h d) -> p h d", h=nh)
        cb = cos_ap.unsqueeze(1).to_broadcast([128, nh, 64])
        sb = sin_ap.unsqueeze(1).to_broadcast([128, nh, 64])
        v.tensor_tensor(t3, s3[:, :, :, 1], sb, op=ALU.mult)
        v.tensor_tensor(o3[:, :, :, 0], s3[:, :, :, 0], cb, op=ALU.mult)
        v.drain()
        v.tensor_tensor(o3[:, :, :, 0], o3[:, :, :, 0], t3, op=ALU.subtract)
        v.drain()
        v.tensor_tensor(t3, s3[:, :, :, 0], sb, op=ALU.mult)
        v.tensor_tensor(o3[:, :, :, 1], s3[:, :, :, 1], cb, op=ALU.mult)
        v.drain()
        v.tensor_tensor(o3[:, :, :, 1], o3[:, :, :, 1], t3, op=ALU.add)

    def p1_qkv(self):
        nc, ii, sc = self.nc, self.inp, self.scr
        ss, sv, sa = self.ss, self.sv, self.sa

        with (
            nc.sbuf_tensor("xg1", [128, 16 * S], BF16) as xg,
            nc.sbuf_tensor("wgb1", [128, 16 * 1024], BF16) as wgb,
            nc.sbuf_tensor("yacc1", [128, NTF * 1024], F32) as yacc,
            nc.psum_tensor("psA1", [128, 1024], F32) as psA,
            nc.psum_tensor("psB1", [128, 1024], F32) as psB,
            nc.sbuf_tensor("ctab", [128, NTF * 64], F32) as ctab,
            nc.sbuf_tensor("stab", [128, NTF * 64], F32) as stab,
            nc.sbuf_tensor("stg", [128, 2 * 1024], BF16) as stg,
            nc.sbuf_tensor("stg2", [128, 2 * 1024], BF16) as stg2,
            nc.sbuf_tensor("rtmp", [128, KVH * 64], F32) as rtmp,
        ):
            for which in ("k", "v"):
                row0 = 4096 if which == "k" else 5120
                self.tm_gemm(f"p1{which}", sc["x_bf"], S, sc["wq_bf"], row0,
                             1024, yacc, (xg, wgb, psA, psB),
                             gp_hook=(self._waves23 if which == "k" else None))
                self._waves23 = None
                ndma = 2 if which == "k" else 1
                with nc.Block(f"p1{which}p") as blk:
                    @blk.sync
                    def _(s):
                        if which == "k":
                            ss.dinc(s.dma_start(
                                ctab[:, :].rearrange("p (n d) -> p n d", d=64),
                                ii["cosk"][:, :].rearrange("(n p) d -> p n d", p=128)))
                            ss.dinc(s.dma_start(
                                stab[:, :].rearrange("p (n d) -> p n d", d=64),
                                ii["sink"][:, :].rearrange("(n p) d -> p n d", p=128)))
                        ss.wait(s)

                    dma_rdy = self.ss.n
                    ps_rdy = self.sp.n
                    sx_b = [self.sx[0].n, self.sx[1].n]

                    @blk.vector
                    def _(v):
                        v.wait_ge(ss.h, dma_rdy)
                        v.wait_ge(self.sp.h, ps_rdy)
                        v.drain()
                        for tc in range(NTF):
                            if tc >= 2:
                                v.wait_ge(self.sx[tc % 2].h,
                                          sx_b[tc % 2] + 16 * ndma * (tc // 2))
                            hb = 1024 * (tc % 2)
                            ych = yacc[:, tc * 1024:(tc + 1) * 1024]
                            v.tensor_copy(stg[:, hb:hb + 1024], ych)
                            if which == "k":
                                self.rope_tm(v, stg2[:, hb:hb + 1024], ych,
                                             ctab[:, tc * 64:(tc + 1) * 64],
                                             stab[:, tc * 64:(tc + 1) * 64],
                                             KVH, rtmp)
                            sv.inc(v.drain())

                    vbase = sv.n - NTF

                    @blk.scalar
                    def _(s2):
                        for tc in range(NTF):
                            s2.wait_ge(sv.h, vbase + tc + 1)
                            hb = 1024 * (tc % 2)
                            dst = sc["k_scr"] if which == "k" else sc["v_scr"]
                            self.sx[tc % 2].dinc(s2.dma_start(
                                dst[tc * 128:(tc + 1) * 128, :],
                                stg[:, hb:hb + 1024]))
                            if which == "k":
                                self.sx[tc % 2].dinc(s2.dma_start(
                                    sc["rk_scr"][tc * 128:(tc + 1) * 128, :],
                                    stg2[:, hb:hb + 1024]))
                        self.sx[0].wait(s2)
                        self.sx[1].wait(s2)

        with (
            nc.sbuf_tensor("xg2", [128, 16 * TOK], BF16) as xg,
            nc.sbuf_tensor("wgb2", [128, 16 * 2048], BF16) as wgb,
            nc.sbuf_tensor("yacc2", [128, NT * 2048], F32) as yacc,
            nc.psum_tensor("psA2", [128, 2048], F32) as psA,
            nc.psum_tensor("psB2", [128, 2048], F32) as psB,
            nc.sbuf_tensor("ctab2", [128, NT * 64], F32) as ctab,
            nc.sbuf_tensor("stab2", [128, NT * 64], F32) as stab,
            nc.sbuf_tensor("stg3", [128, 2 * 2048], BF16) as stg,
            nc.sbuf_tensor("stg4", [128, 2 * 2048], BF16) as stg2,
            nc.sbuf_tensor("rtmp2", [128, 16 * 64], F32) as rtmp,
        ):
            with nc.Block("p1qt") as blk:
                @blk.sync
                def _(s):
                    s.wait_ge(self.sw[0].h, self._p0_wave2)
                    ss.dinc(s.dma_start(
                        ctab[:, :].rearrange("p (n d) -> p n d", d=64),
                        ii["cosq"][:, :].rearrange("(n p) d -> p n d", p=128)))
                    ss.dinc(s.dma_start(
                        stab[:, :].rearrange("p (n d) -> p n d", d=64),
                        ii["sinq"][:, :].rearrange("(n p) d -> p n d", p=128)))
                    ss.wait(s)

            for half in range(2):
                self.tm_gemm(f"p1q{half}", sc["xown_bf"], TOK, sc["wq_bf"],
                             half * 2048, 2048, yacc, (xg, wgb, psA, psB))
                with nc.Block(f"p1q{half}p") as blk:
                    ps_rdy = self.sp.n
                    sx_b = [self.sx[0].n, self.sx[1].n]

                    @blk.vector
                    def _(v):
                        v.wait_ge(self.sp.h, ps_rdy)
                        v.drain()
                        for tc in range(NT):
                            if tc >= 2:
                                v.wait_ge(self.sx[tc % 2].h,
                                          sx_b[tc % 2] + 32 * (tc // 2))
                            hb = 2048 * (tc % 2)
                            ych = yacc[:, tc * 2048:(tc + 1) * 2048]
                            v.tensor_copy(stg[:, hb:hb + 2048], ych)
                            self.rope_tm(v, stg2[:, hb:hb + 2048], ych,
                                         ctab[:, tc * 64:(tc + 1) * 64],
                                         stab[:, tc * 64:(tc + 1) * 64],
                                         16, rtmp)
                            sv.inc(v.drain())

                    vbase = sv.n - NT

                    @blk.scalar
                    def _(s2):
                        for tc in range(NT):
                            s2.wait_ge(sv.h, vbase + tc + 1)
                            hb = 2048 * (tc % 2)
                            cs = slice(half * 2048, (half + 1) * 2048)
                            self.sx[tc % 2].dinc(s2.dma_start(
                                sc["q_scr"][tc * 128:(tc + 1) * 128, cs],
                                stg[:, hb:hb + 2048]))
                            self.sx[tc % 2].dinc(s2.dma_start(
                                sc["rq_scr"][tc * 128:(tc + 1) * 128, cs],
                                stg2[:, hb:hb + 2048]))
                        self.sx[0].wait(s2)
                        self.sx[1].wait(s2)

        with (
            nc.sbuf_tensor("xg3", [128, 16 * TOK], BF16) as xg,
            nc.sbuf_tensor("wgb3", [128, 16 * 96], BF16) as wgb,
            nc.psum_tensor("psA3", [128, 96], F32) as psA,
            nc.psum_tensor("psB3", [128, 96], F32) as psB,
            nc.sbuf_tensor("bgt", [128, 96], F32) as bgt,
            nc.sbuf_tensor("gac", [128, NT * 96], F32) as gac,
        ):
            self.tm_gemm("p1g", sc["xown_bf"], TOK, sc["wg_bf"], 0, 96,
                         gac, (xg, wgb, psA, psB))
            with nc.Block("p1gp") as blk:
                @blk.sync
                def _(s):
                    ss.dinc(s.dma_start(bgt[:, :], ii["b_gateb"][:, :]))
                    ss.wait(s)

                ps_rdy = self.sp.n
                dma_rdy = ss.n

                @blk.vector
                def _(v):
                    v.wait_ge(self.sp.h, ps_rdy)
                    v.wait_ge(ss.h, dma_rdy)
                    v.drain()
                    bb = bgt[:, :].unsqueeze(1).to_broadcast([128, NT, 96])
                    v.tensor_tensor(
                        gac[:, :].rearrange("p (n g) -> p n g", g=96),
                        gac[:, :].rearrange("p (n g) -> p n g", g=96),
                        bb, op=ALU.add)
                    sv.inc(v.drain())

                @blk.scalar
                def _(s2):
                    self.sv.wait(s2)
                    self.sq.inc(s2.activation(gac[:, :], gac[:, :], ACT.Sigmoid))
                    self.sq.wait(s2)
                    sa.dinc(s2.dma_start(sc["gates_scr"][:, :], gac[:, :]))
                    sa.wait(s2)


    def p2_prep(self, rkp, vtm, ckT, cvt, gts):
        nc, ii, sc = self.nc, self.inp, self.scr
        ss, sa, sv, st, sq, sp = self.ss, self.sa, self.sv, self.st, self.sq, self.sp

        with nc.Block("p2a") as blk:
            @blk.gpsimd
            def _(g):
                sg = self.sg
                sg.dinc(g.dma_start(
                    ckT[:, :].rearrange("p (k c) -> p k c", c=128)[:, :, 127:128],
                    ii["mem_kT"][:, :].unsqueeze(2),
                    allow_slow_non_contiguous=True))
                sg.dinc(g.dma_start(cvt[127:128, :], ii["mem_vb"][:, :]))
                sg.wait(g)

            @blk.vector
            def _(v):
                for kvh in range(KVH):
                    o = kvh * 2176
                    v.memset(rkp[:, o:o + 64], 0.0)
                    v.memset(rkp[:, o + 2112:o + 2176], 0.0)
                    v.memset(vtm[:, kvh * 18 * 128:(kvh * 18 + 1) * 128], 0.0)
                    v.memset(vtm[:, (kvh * 18 + 17) * 128:(kvh * 18 + 18) * 128], 0.0)
                sv.inc(v.drain())

            vfence = sv.n

            @blk.sync
            def _(s):
                s.wait_ge(sv.h, vfence)
                for kvh in range(KVH):
                    ss.dinc(s.dma_start(
                        rkp[:, kvh * 2176 + 64:kvh * 2176 + 2112],
                        sc["rk_scr"][:, kvh * 128:(kvh + 1) * 128],
                        transpose=True))
                    vt4 = vtm[:, :].rearrange("p (k b c) -> p k b c", b=18, c=128)
                    ss.dinc(s.dma_start(
                        vt4[:, kvh, 1:17, :],
                        sc["v_scr"][:, kvh * 128:(kvh + 1) * 128].rearrange(
                            "(n p) c -> p n c", p=128)))
                ss.dinc(s.dma_start(gts[:, :], sc["gates_scr"][:, :]))
                ss.wait(s)

        with (
            nc.sbuf_tensor("w1T", [128, 16 * 2048], BF16) as w1T,
            nc.sbuf_tensor("w2T", [128, 16 * 128], BF16) as w2T,
            nc.sbuf_tensor("kt", [128, 2 * 2048], BF16) as kt,
            nc.sbuf_tensor("kpk", [128, 2 * 2048], BF16) as kpk,
            nc.sbuf_tensor("h1t", [128, 2 * 16 * 128], BF16) as h1t,
            nc.sbuf_tensor("posb", [128, 2 * 16], F32) as posb,
            nc.sbuf_tensor("b1b", [128, 16], F32) as b1b,
            nc.sbuf_tensor("b2c", [128, 1], F32) as b2c,
            nc.sbuf_tensor("b2r", [128, 128], F32) as b2r,
            nc.psum_tensor("psh1", [128, 128], F32) as psh1,
            nc.psum_tensor("psh2", [128, 128], F32) as psh2,
            nc.psum_tensor("psck1", [128, 128], F32) as psck1,
            nc.psum_tensor("psck2", [128, 128], F32) as psck2,
        ):
            for which in ("k", "v"):
                w1_bf = sc["kw1_bf"] if which == "k" else sc["vw1_bf"]
                w2_bf = sc["kw2_bf"] if which == "k" else sc["vw2_bf"]
                with nc.Block(f"p2{which}w") as blk:
                    @blk.sync
                    def _(s):
                        s.wait_ge(self.sg.h, self.sg.n)
                        for ic in range(16):
                            ss.dinc(s.dma_start(
                                w1T[:, ic * 2048:(ic + 1) * 2048],
                                w1_bf[:, ic * 128:(ic + 1) * 128], transpose=True))
                        for oc in range(16):
                            ss.dinc(s.dma_start(
                                w2T[:, oc * 128:(oc + 1) * 128],
                                w2_bf[:, oc * 128:(oc + 1) * 128], transpose=True))
                        ss.dinc(s.dma_start(
                            b1b[:, :], ii["k_b1c" if which == "k" else "v_b1c"][:, :]))
                        if which == "k":
                            ss.dinc(s.dma_start(b2c[:, :], ii["k_b2c"][:, :]))
                        else:
                            ss.dinc(s.dma_start(b2r[:, :], ii["v_b2b"][:, :]))
                        ss.wait(s)

                fz = {}
                with nc.Block(f"p2{which}c") as blk:
                    for kvh in range(KVH):
                        par = kvh % 2
                        ktp = kt[:, par * 2048:(par + 1) * 2048]
                        kpkp = kpk[:, par * 2048:(par + 1) * 2048]
                        h1p = h1t[:, par * 2048:(par + 1) * 2048]
                        posp = posb[:, par * 16:(par + 1) * 16]
                        psck = psck1 if par == 0 else psck2

                        kt_free = fz.get(("kpk_done", kvh - 2))

                        skp = self.sk[par]

                        @blk.sync
                        def _(s):
                            if kt_free:
                                s.wait_ge(kt_free[0].h, kt_free[1])
                            src_ = sc["k_scr"] if which == "k" else sc["v_scr"]
                            skp.dinc(s.dma_start(
                                ktp, src_[:, kvh * 128:(kvh + 1) * 128],
                                transpose=True))
                            skp.dinc(s.dma_start(
                                posp,
                                ii["k_posT" if which == "k" else "v_posT"]
                                [:, kvh * 16:(kvh + 1) * 16]))

                        dmaf = skp.n
                        kpk_free = fz.get(("h1_done", kvh - 2))

                        @blk.vector
                        def _(v):
                            v.wait_ge(skp.h, dmaf)
                            if kpk_free:
                                v.wait_ge(kpk_free[0].h, kpk_free[1])
                            v.drain()
                            pb = posp.unsqueeze(1).to_broadcast([128, 128, 16])
                            v.tensor_tensor(
                                kpkp.rearrange("p (c t) -> p c t", t=16),
                                ktp.rearrange("p (c t) -> p c t", t=16),
                                pb, op=ALU.add)
                            sv.inc(v.drain())
                        fz[("kpk_done", kvh)] = (sv, sv.n)

                        vfen = sv.n
                        sq_base = sq.n
                        h1_free = fz.get(("ck_done", kvh - 2))

                        @blk.tensor
                        def _(t):
                            t.wait_ge(sv.h, vfen)
                            for oc in range(16):
                                psh = psh1 if oc % 2 == 0 else psh2
                                if oc >= 2:
                                    t.wait_ge(sq.h, sq_base + oc - 1)
                                last = None
                                for ic in range(16):
                                    last = t.matmul(
                                        psh[:, :],
                                        w1T[:, ic * 2048 + oc * 128:
                                            ic * 2048 + (oc + 1) * 128],
                                        kpkp[:, ic::16],
                                        start=(ic == 0), stop=(ic == 15))
                                st.inc(last)
                        fz[("h1_done", kvh)] = (st, st.n)

                        tbase = st.n - 16

                        @blk.scalar
                        def _(s2):
                            if h1_free:
                                s2.wait_ge(h1_free[0].h, h1_free[1])
                            for oc in range(16):
                                psh = psh1 if oc % 2 == 0 else psh2
                                s2.wait_ge(st.h, tbase + oc + 1)
                                sq.inc(s2.activation(
                                    h1p[:, oc * 128:(oc + 1) * 128], psh[:, :],
                                    ACT.Relu, bias=b1b[:, oc:oc + 1]))

                        sqf = sq.n
                        psck_free = fz.get(("evac_done", kvh - 2))

                        @blk.tensor
                        def _(t):
                            t.wait_ge(sq.h, sqf)
                            if psck_free:
                                t.wait_ge(psck_free[0].h, psck_free[1])
                            last = None
                            for oc in range(16):
                                if which == "k":
                                    last = t.matmul(
                                        psck[:, :], w2T[:, oc * 128:(oc + 1) * 128],
                                        h1p[:, oc * 128:(oc + 1) * 128],
                                        start=(oc == 0), stop=(oc == 15))
                                else:
                                    last = t.matmul(
                                        psck[:, :], h1p[:, oc * 128:(oc + 1) * 128],
                                        w2T[:, oc * 128:(oc + 1) * 128],
                                        start=(oc == 0), stop=(oc == 15))
                            st.inc(last)
                        fz[("ck_done", kvh)] = (st, st.n)

                        tf2 = st.n

                        @blk.vector
                        def _(v):
                            v.wait_ge(st.h, tf2)
                            v.drain()
                            if which == "k":
                                v.tensor_scalar(
                                    ckT[:, kvh * 128:kvh * 128 + 127],
                                    psck[:, 0:127], b2c[:, 0:1], None,
                                    op0=ALU.add)
                            else:
                                v.tensor_tensor(
                                    cvt[0:127, kvh * 128:(kvh + 1) * 128],
                                    psck[0:127, :], b2r[0:127, :], op=ALU.add)
                            sv.inc(v.drain())
                        fz[("evac_done", kvh)] = (sv, sv.n)

    def p3_attn(self, rkp, vtm, ckT, cvt, gts):
        nc, ii, sc = self.nc, self.inp, self.scr
        ss, sa, sv, st, sq, sp = self.ss, self.sa, self.sv, self.st, self.sq, self.sp
        self.f3 = {}

        from contextlib import ExitStack
        with ExitStack() as _es:
            E = _es.enter_context
            qh = E(nc.sbuf_tensor("qh", [128, G * 1024], BF16))
            rqh = E(nc.sbuf_tensor("rqh", [128, G * 1024], BF16))
            cmk = E(nc.sbuf_tensor("cmk", [128, NT * 128], BF16))
            omk = E(nc.sbuf_tensor("omk", [128, NT * 256], BF16))
            smk = E(nc.sbuf_tensor("smk", [128, NT * 384], BF16))
            cs = E(nc.sbuf_tensor("cs", [128, 2 * G * 128], F32))
            imp = E(nc.sbuf_tensor("imp", [128, 2 * 128], F32))
            impsm = E(nc.sbuf_tensor("impsm", [128, 2 * 128], F32))
            m8 = E(nc.sbuf_tensor("m8", [128, 2 * 8], F32))
            repl = E(nc.sbuf_tensor("repl", [128, 2 * 127], F32))
            fmask = E(nc.sbuf_tensor("fmask", [128, 2 * 128], F32))
            maskb = E(nc.sbuf_tensor("maskb", [128, 2 * 2048], BF16))
            pcc = E(nc.sbuf_tensor("pcc", [128, 2 * G * 128], BF16))
            pTc = E(nc.sbuf_tensor("pTc", [128, 2 * G * 128], BF16))
            couts = E(nc.sbuf_tensor("couts", [128, 2 * G * 128], F32))
            comb = E(nc.sbuf_tensor("comb", [128, 2 * G * 128], BF16))
            rmg = E(nc.sbuf_tensor("rmg", [128, 2 * 4], F32))
            rsg = E(nc.sbuf_tensor("rsg", [128, 2 * 4], F32))
            rrg = E(nc.sbuf_tensor("rrg", [128, 2 * 1], F32))
            r1 = E(nc.sbuf_tensor("r1", [128, 2 * 1], F32))
            scog = E(nc.sbuf_tensor("scog", [128, G * 2048], F32))
            slsco = E(nc.sbuf_tensor("slsco", [128, G * 384], F32))
            pbfg = E(nc.sbuf_tensor("pbfg", [128, G * 2048], BF16))
            pTg = E(nc.sbuf_tensor("pTg", [128, 4 * 128], BF16))
            slpg = E(nc.sbuf_tensor("slpg", [128, G * 512], BF16))
            rmF = E(nc.sbuf_tensor("rmF", [128, 4], F32))
            rsF = E(nc.sbuf_tensor("rsF", [128, 4], F32))
            rmS = E(nc.sbuf_tensor("rmS", [128, 4], F32))
            rsS = E(nc.sbuf_tensor("rsS", [128, 4], F32))
            prd = E(nc.sbuf_tensor("prd", [128, 12], F32))
            idb = E(nc.sbuf_tensor("idb", [128, 128], BF16))
            idf = E(nc.sbuf_tensor("idf", [128, 128], F32))
            psc0 = E(nc.psum_tensor("psc0", [128, 512], F32))
            psc1 = E(nc.psum_tensor("psc1", [128, 512], F32))
            psf0 = E(nc.psum_tensor("psf0", [128, 512], F32))
            psf1 = E(nc.psum_tensor("psf1", [128, 512], F32))
            pso = E(nc.psum_tensor("pso", [128, 1024], F32))
            ptr0 = E(nc.psum_tensor("ptr0", [128, 1024], BF16))
            ptr1 = E(nc.psum_tensor("ptr1", [128, 1024], BF16))

            bufs = dict(qh=qh, rqh=rqh, cmk=cmk, omk=omk, smk=smk, cs=cs,
                        imp=imp, impsm=impsm, m8=m8, repl=repl,
                        fmask=fmask, maskb=maskb, pcc=pcc, pTc=pTc,
                        couts=couts, comb=comb, rmg=rmg, rsg=rsg, rrg=rrg,
                        r1=r1, scog=scog, slsco=slsco, pbfg=pbfg, pTg=pTg,
                        slpg=slpg, rmF=rmF, rsF=rsF, rmS=rmS,
                        rsS=rsS, prd=prd, idb=idb,
                        psc=[psc0, psc1], psf=[psf0, psf1], pso=pso,
                        ptr=[ptr0[:, 0:128], ptr1[:, 0:128]],
                        rkp=rkp, vtm=vtm, ckT=ckT,
                        cvt=cvt, gts=gts)

            with nc.Block("p3m") as blk:
                @blk.gpsimd
                def _(g):
                    for dst, key in ((cmk, "cmask"), (omk, "ownmask"),
                                     (smk, "slmask")):
                        self.sg.dinc(g.dma_start(
                            dst[:, :].rearrange("p (n c) -> p n c", n=NT),
                            ii[key][:, :].rearrange("(n p) c -> p n c", p=128)))
                    self.sg.dinc(g.dma_start(idf[:, :], ii["ident"][:, :]))
                    self.sg.wait(g)

                @blk.vector
                def _(v):
                    self.sg.wait(v)
                    self.sv.inc(v.tensor_copy(idb[:, :], idf[:, :]))

            for kvh in range(KVH):
                with nc.Block(f"p3q{kvh}") as blk:
                    fw = self.f3.get("qh")

                    @blk.sync
                    def _(s):
                        if fw:
                            s.wait_ge(fw[0].h, fw[1])
                        for g in range(G):
                            hd = kvh * G + g
                            ss.dinc(s.dma_start(
                                qh[:, g * 1024:(g + 1) * 1024],
                                sc["q_scr"][:, hd * 128:(hd + 1) * 128],
                                transpose=True))
                            ss.dinc(s.dma_start(
                                rqh[:, g * 1024:(g + 1) * 1024],
                                sc["rq_scr"][:, hd * 128:(hd + 1) * 128],
                                transpose=True))
                    self.f3["qhload"] = (ss, ss.n)

                with nc.Block(f"ku{kvh}") as blk:
                    for t in range(NT + 1):
                        if t < NT:
                            self._p3_comp(blk, kvh, t, bufs)
                        if t >= 1:
                            self._p3_fine(blk, kvh, t - 1, bufs)
                self.f3["qh"] = (st, st.n)

    def _p3_comp(self, blk, kvh, tau, bb):
        ss, sa, sv, st, sq = self.ss, self.sa, self.sv, self.st, self.sq
        f3 = self.f3
        par = tau % 2
        psc = bb["psc"][par]
        cs = bb["cs"][:, par * 512:(par + 1) * 512]
        imp = bb["imp"][:, par * 128:(par + 1) * 128]
        impsm = bb["impsm"][:, par * 128:(par + 1) * 128]
        m8 = bb["m8"][:, par * 8:(par + 1) * 8]
        repl = bb["repl"][:, par * 127:(par + 1) * 127]
        fmask = bb["fmask"][:, par * 128:(par + 1) * 128]
        maskb = bb["maskb"][:, par * 2048:(par + 1) * 2048]
        pcc = bb["pcc"][:, par * 512:(par + 1) * 512]
        pTc = bb["pTc"][:, par * 512:(par + 1) * 512]
        couts = bb["couts"][:, par * 512:(par + 1) * 512]
        rmg = bb["rmg"][:, par * 4:(par + 1) * 4]
        rsg = bb["rsg"][:, par * 4:(par + 1) * 4]
        rrg = bb["rrg"][:, par * 1:(par + 1) * 1]
        r1 = bb["r1"][:, par * 1:(par + 1) * 1]
        qh, ckT, cvt, cmk, omk = (bb["qh"], bb["ckT"], bb["cvt"], bb["cmk"],
                                  bb["omk"])

        def FW(eng, *keys):
            for k in keys:
                pair = f3.get(k)
                if pair:
                    eng.wait_ge(pair[0].h, pair[1])

        @blk.tensor
        def _(t):
            FW(t, "qhload", f"psc{par}")
            last = None
            for g in range(G):
                last = t.matmul(
                    psc[:, g * 128:(g + 1) * 128],
                    qh[:, g * 1024 + tau * 128:g * 1024 + (tau + 1) * 128],
                    ckT[:, kvh * 128:(kvh + 1) * 128],
                    start=True, stop=True)
            st.inc(last)

        @blk.vector
        def _(v):
            st.wait(v)
            FW(v, f"cs{par}", f"imp{par}")
            v.drain()
            for g in range(G):
                v.scalar_tensor_tensor(
                    cs[:, g * 128:(g + 1) * 128],
                    psc[:, g * 128:(g + 1) * 128], SCALE,
                    cmk[:, tau * 128:(tau + 1) * 128],
                    op0=ALU.mult, op1=ALU.add)
            v.drain()
            v.tensor_tensor(imp, cs[:, 0:128], cs[:, 128:256], op=ALU.add)
            v.drain()
            v.tensor_tensor(imp, imp, cs[:, 256:384], op=ALU.add)
            v.drain()
            v.tensor_tensor(imp, imp, cs[:, 384:512], op=ALU.add)
            v.drain()
            v.memset(imp[:, 127:128], -4000.0)
            v.drain()
            v.tensor_reduce(r1, imp, axis=AX, op=ALU.max, negate=True)
            for g in range(G):
                v.tensor_reduce(rmg[:, g:g + 1], cs[:, g * 128:(g + 1) * 128],
                                axis=AX, op=ALU.max, negate=True)
            v.drain()
            v.tensor_scalar(r1, r1, 0.25, None, op0=ALU.mult)
            sv.inc(v.drain())
        f3[f"psc{par}"] = (sv, sv.n)

        @blk.scalar
        def _(s2):
            sv.wait(s2)
            FW(s2, f"impsm{par}", f"rsg{par}", f"pbfc{par}")
            for g in range(G):
                s2.activation(pcc[:, g * 128:(g + 1) * 128],
                              cs[:, g * 128:(g + 1) * 128], ACT.Exp,
                              bias=rmg[:, g:g + 1],
                              accum_out=rsg[:, g:g + 1])
            sq.inc(s2.activation(impsm, imp, ACT.Exp, scale=0.25,
                                 bias=r1[:, 0:1], accum_out=rrg[:, 0:1]))
        f3[f"cs{par}"] = (sq, sq.n)
        f3[f"imp{par}"] = (sq, sq.n)

        @blk.vector
        def _(v):
            sq.wait(v)
            FW(v, f"maskb{par}")
            v.reciprocal(rrg[:, 0:1], rrg[:, 0:1])
            v.reciprocal(rsg, rsg)
            v.drain()
            v.tensor_scalar(impsm, impsm, rrg[:, 0:1], None, op0=ALU.mult)
            v.drain()
            v.max(out=m8, in_=impsm[:, 0:127])
            v.drain()
            v.memset(m8[:, 4:8], -1.0)
            v.drain()
            v.match_replace(out=repl, in_to_replace=m8,
                            in_values=impsm[:, 0:127], imm_value=-1e30)
            v.drain()
            v.tensor_scalar(fmask[:, 0:127], repl, -1e29, None, op0=ALU.is_lt)
            v.drain()
            v.tensor_scalar(repl, impsm[:, 0:127], 1e-10, None, op0=ALU.is_gt)
            v.drain()
            v.tensor_tensor(fmask[:, 0:127], fmask[:, 0:127], repl,
                            op=ALU.mult)
            v.drain()
            v.tensor_scalar(fmask[:, 0:127], fmask[:, 0:127], 1e30, -1e30,
                            op0=ALU.mult, op1=ALU.add)
            v.memset(fmask[:, 127:128], NEG)
            v.drain()
            KT = 256 * (tau + 1)
            nb = KT // 16
            v.tensor_copy(
                maskb[:, 0:KT].rearrange("p (b s) -> p b s", s=16),
                fmask[:, 0:nb].unsqueeze(2).to_broadcast([128, nb, 16]))
            v.drain()
            v.tensor_tensor(maskb[:, KT - 256:KT], maskb[:, KT - 256:KT],
                            omk[:, tau * 256:(tau + 1) * 256], op=ALU.max)
            sv.inc(v.drain())
        f3[f"impsm{par}"] = (sv, sv.n)
        f3[f"maskb_rdy{par}"] = (sv, sv.n)

        @blk.sync
        def _(s):
            sv.wait(s)
            FW(s, f"pTc_rd{par}")
            for g in range(G):
                ss.dinc(s.dma_start(pTc[:, g * 128:(g + 1) * 128],
                                    pcc[:, g * 128:(g + 1) * 128],
                                    transpose=True))
        f3[f"pbfc{par}"] = (ss, ss.n)
        ss5 = ss.n

        @blk.tensor
        def _(t):
            t.wait_ge(ss.h, ss5)
            FW(t, f"couts{par}")
            last = None
            for g in range(G):
                last = t.matmul(
                    psc[:, g * 128:(g + 1) * 128],
                    pTc[:, g * 128:(g + 1) * 128],
                    cvt[:, kvh * 128:(kvh + 1) * 128],
                    start=True, stop=True)
            st.inc(last)
        f3[f"pTc_rd{par}"] = (st, st.n)

        @blk.vector
        def _(v):
            st.wait(v)
            v.drain()
            sv.inc(v.tensor_copy(couts, psc[:, 0:512]))
        f3[f"psc{par}"] = (sv, sv.n)

    def _p3_fine(self, blk, kvh, tau, bb):
        ss, sa, sv, st, sq = self.ss, self.sa, self.sv, self.st, self.sq
        sc = self.scr
        f3 = self.f3
        par = tau % 2
        KT = 256 * (tau + 1)
        NKC = KT // 128
        NSC = (KT + 511) // 512
        NCH = NSC + 1
        rko = kvh * 2176
        vto = kvh * 18 * 128
        maskb = bb["maskb"][:, par * 2048:par * 2048 + 2048]
        couts = bb["couts"][:, par * 512:(par + 1) * 512]
        comb = bb["comb"][:, par * 512:(par + 1) * 512]
        rsg = bb["rsg"][:, par * 4:(par + 1) * 4]
        rqh, smk, gts, rkp, vtm = (bb["rqh"], bb["smk"], bb["gts"],
                                   bb["rkp"], bb["vtm"])
        scog, slsco, pbfg, pTg = bb["scog"], bb["slsco"], bb["pbfg"], bb["pTg"]
        slpg = bb["slpg"]
        rmF, rsF, rmS, rsS, prd = (bb["rmF"], bb["rsF"], bb["rmS"],
                                   bb["rsS"], bb["prd"])
        psf, pso, ptrs, idb = bb["psf"], bb["pso"], bb["ptr"], bb["idb"]

        def FW(eng, *keys):
            for k in keys:
                pair = f3.get(k)
                if pair:
                    eng.wait_ge(pair[0].h, pair[1])

        sv_base = sv.n

        @blk.tensor
        def _(t):
            FW(t, "psf", f"maskb_rdy{par}")
            c = 0
            for g in range(G):
                lhs_q = rqh[:, g * 1024 + tau * 128:g * 1024 + (tau + 1) * 128]
                for i in range(NCH):
                    ps = psf[c % 2]
                    if c >= 2:
                        t.wait_ge(sv.h, sv_base + (c - 1))
                    if i < NSC:
                        nn = min(512, KT - i * 512)
                        st.inc(t.matmul(
                            ps[:, 0:nn], lhs_q,
                            rkp[:, rko + 64 + i * 512:rko + 64 + i * 512 + nn],
                            start=True, stop=True))
                    else:
                        st.inc(t.matmul(
                            ps[:, 0:384], lhs_q,
                            rkp[:, rko + 2 * tau * 128:rko + 2 * tau * 128 + 384],
                            start=True, stop=True))
                    c += 1

        tb8 = st.n - G * NCH

        @blk.vector
        def _(v):
            FW(v, "scog", "slsco")
            v.drain()
            c = 0
            for g in range(G):
                for i in range(NCH):
                    ps = psf[c % 2]
                    v.wait_ge(st.h, tb8 + c + 1)
                    if i < NSC:
                        nn = min(512, KT - i * 512)
                        sv.inc(v.scalar_tensor_tensor(
                            scog[:, g * 2048 + i * 512:g * 2048 + i * 512 + nn],
                            ps[:, 0:nn], SCALE,
                            maskb[:, i * 512:i * 512 + nn],
                            op0=ALU.mult, op1=ALU.add))
                    else:
                        sv.inc(v.scalar_tensor_tensor(
                            slsco[:, g * 384:(g + 1) * 384],
                            ps[:, 0:384], SCALE,
                            smk[:, tau * 384:(tau + 1) * 384],
                            op0=ALU.mult, op1=ALU.add))
                    c += 1
            v.drain()
            for g in range(G):
                v.tensor_reduce(rmF[:, g:g + 1],
                                scog[:, g * 2048:g * 2048 + KT],
                                axis=AX, op=ALU.max, negate=True)
                v.tensor_reduce(rmS[:, g:g + 1],
                                slsco[:, g * 384:(g + 1) * 384],
                                axis=AX, op=ALU.max, negate=True)
            sv.inc(v.drain())
        f3["psf"] = (sv, sv.n)
        f3[f"maskb{par}"] = (sv, sv.n)
        sv9 = sv.n

        @blk.scalar
        def _(s2):
            s2.wait_ge(sv.h, sv9)
            FW(s2, "pbf_rd", "slp_rd", "rsFS")
            for g in range(G):
                s2.activation(pbfg[:, g * 2048:g * 2048 + KT],
                              scog[:, g * 2048:g * 2048 + KT], ACT.Exp,
                              bias=rmF[:, g:g + 1], accum_out=rsF[:, g:g + 1])
                s2.drain()
                sq.inc(s2.activation(
                    slpg[:, g * 512 + 64:g * 512 + 448],
                    slsco[:, g * 384:(g + 1) * 384], ACT.Exp,
                    bias=rmS[:, g:g + 1], accum_out=rsS[:, g:g + 1]))
        f3["scog"] = (sq, sq.n)
        f3["slsco"] = (sq, sq.n)

        if tau == 0 and kvh == 0:
            @blk.vector
            def _(v):
                for g in range(G):
                    v.memset(slpg[:, g * 512:g * 512 + 64], 0.0)
                    v.memset(slpg[:, g * 512 + 448:(g + 1) * 512], 0.0)
                sv.inc(v.drain())

        sq10 = sq.n - G
        sqe_base = sq.n
        tb12 = st.n
        svB = sv.n

        NTOT = NKC + 4
        tb13s = []

        def inc_T(i):
            return 1 if i == 0 else 2 * i

        def inc_P(i):
            return 2 * i + 3 if i < NTOT - 1 else 2 * NTOT

        @blk.scalar
        def _(s2):
            for g in range(G):
                bg = tb12 + g * 2 * NTOT
                for i in range(NTOT):
                    s2.wait_ge(st.h, bg + inc_T(i))
                    if i >= 4:
                        s2.wait_ge(st.h, bg + inc_P(i - 4))
                    elif g >= 1:
                        s2.wait_ge(st.h, bg - 2 * NTOT + inc_P(NTOT - 4 + i))
                    sq.inc(s2.copy(pTg[:, (i % 4) * 128:(i % 4 + 1) * 128],
                                   ptrs[i % 2][:, :]))

        @blk.tensor
        def _(t):
            FW(t, "pso", "pT_rd")
            t.wait_ge(sv.h, svB)
            for g in range(G):
                seg = (g % 2) * 512
                sqe_g = sqe_base + g * NTOT
                t.wait_ge(sq.h, sq10 + g + 1)
                if g >= 2:
                    t.wait_ge(sv.h, svB + g - 1)

                def srcp_of(i):
                    if i < NKC:
                        return pbfg[:, g * 2048 + i * 128:g * 2048 + (i + 1) * 128]
                    i_ = i - NKC
                    return slpg[:, g * 512 + i_ * 128:g * 512 + (i_ + 1) * 128]

                def pv(i):
                    if i < NKC:
                        return t.matmul(
                            pso[:, seg:seg + 128],
                            pTg[:, (i % 4) * 128:(i % 4 + 1) * 128],
                            vtm[:, vto + (1 + i) * 128:vto + (2 + i) * 128],
                            start=(i == 0), stop=(i == NKC - 1))
                    i_ = i - NKC
                    bi = 2 * tau + i_
                    return t.matmul(
                        pso[:, seg + 128:seg + 256],
                        pTg[:, (i % 4) * 128:(i % 4 + 1) * 128],
                        vtm[:, vto + bi * 128:vto + (bi + 1) * 128],
                        start=(i_ == 0), stop=(i_ == 3))

                for i in range(NTOT):
                    if i >= 2:
                        t.wait_ge(sq.h, sqe_g + i - 1)
                    st.inc(t.transpose(ptrs[i % 2][:, :], srcp_of(i), idb[:, :]))
                    if i >= 1:
                        t.wait_ge(sq.h, sqe_g + i)
                        st.inc(pv(i - 1))
                t.wait_ge(sq.h, sqe_g + NTOT)
                st.inc(pv(NTOT - 1))
        f3["pT_rd"] = (st, st.n)
        f3["pbf_rd"] = (st, st.n)
        f3["slp_rd"] = (st, st.n)
        tb13 = st.n

        @blk.vector
        def _(v):
            FW(v, f"comb{par}")
            v.wait_ge(sq.h, sqe_base)
            v.reciprocal(rsF[:, :], rsF[:, :])
            v.reciprocal(rsS[:, :], rsS[:, :])
            v.drain()
            for g in range(G):
                hd3 = 3 * (kvh * G + g)
                gc = lambda b: gts[:, tau * 96 + hd3 + b:tau * 96 + hd3 + b + 1]
                v.tensor_tensor(prd[:, g:g + 1], gc(0), rsg[:, g:g + 1],
                                op=ALU.mult)
                v.tensor_tensor(prd[:, 4 + g:5 + g], gc(1), rsF[:, g:g + 1],
                                op=ALU.mult)
                v.tensor_tensor(prd[:, 8 + g:9 + g], gc(2), rsS[:, g:g + 1],
                                op=ALU.mult)
            v.drain()
            for g in range(G):
                seg = (g % 2) * 512
                v.wait_ge(st.h, tb13 - (G - 1 - g) * 2 * NTOT)
                v.tensor_scalar(comb[:, g * 128:(g + 1) * 128],
                                couts[:, g * 128:(g + 1) * 128],
                                prd[:, g:g + 1], None, op0=ALU.mult)
                v.drain()
                v.scalar_tensor_tensor(comb[:, g * 128:(g + 1) * 128],
                                       pso[:, seg:seg + 128],
                                       prd[:, 4 + g:5 + g],
                                       comb[:, g * 128:(g + 1) * 128],
                                       op0=ALU.mult, op1=ALU.add)
                v.drain()
                sv.inc(v.scalar_tensor_tensor(
                    comb[:, g * 128:(g + 1) * 128],
                    pso[:, seg + 128:seg + 256], prd[:, 8 + g:9 + g],
                    comb[:, g * 128:(g + 1) * 128],
                    op0=ALU.mult, op1=ALU.add))
                v.drain()
        f3["pso"] = (sv, sv.n)
        f3[f"couts{par}"] = (sv, sv.n)
        f3[f"rsg{par}"] = (sv, sv.n)
        f3["rsFS"] = (sv, sv.n)

        @blk.scalar
        def _(s2):
            sv.wait(s2)
            sa.dinc(s2.dma_start(
                sc["comb_scr"][tau * 128:(tau + 1) * 128,
                               kvh * 512:(kvh + 1) * 512], comb))
            sa.wait(s2)
        f3[f"comb{par}"] = (sa, sa.n)

    def emit_ln_tc(self, blk, make_ybuf, lb, out32_ap, out16_ap):
        sv, sq, sa = self.sv, self.sq, self.sa
        ybuf, sqs = lb["ybuf"], lb["sqs"]
        m, vv, sd, rs = lb["m"], lb["vv"], lb["sd"], lb["rs"]
        ot32, ot16 = lb["ot32"], lb["ot16"]

        @blk.vector
        def _(v):
            make_ybuf(v)
            v.drain()
            v.tensor_reduce(m[:, :], ybuf[:, :], axis=AX, op=ALU.add)
            sv.inc(v.drain())

        @blk.scalar
        def _(s2):
            sv.wait(s2)
            sq.inc(s2.activation(sqs[:, :], ybuf[:, :], ACT.Square,
                                 accum_out=vv[:, 0:1]))

        @blk.vector
        def _(v):
            sq.wait(v)
            v.tensor_scalar(m[:, :], m[:, :], 1.0 / D, None, op0=ALU.mult)
            v.tensor_scalar(vv[:, :], vv[:, :], 1.0 / D, None, op0=ALU.mult)
            v.drain()
            v.tensor_tensor(sd[:, :], m[:, :], m[:, :], op=ALU.mult)
            v.drain()
            v.tensor_tensor(vv[:, :], vv[:, :], sd[:, :], op=ALU.subtract)
            v.drain()
            v.tensor_scalar(vv[:, :], vv[:, :], EPS, None, op0=ALU.add)
            sv.inc(v.drain())

        @blk.scalar
        def _(s2):
            sv.wait(s2)
            sq.inc(s2.activation(sd[:, :], vv[:, :], ACT.Sqrt))

        @blk.vector
        def _(v):
            sq.wait(v)
            v.reciprocal(rs[:, :], sd[:, :])
            v.drain()
            v.tensor_tensor(sd[:, :], m[:, :], rs[:, :], op=ALU.mult)
            v.drain()
            v.tensor_scalar(sd[:, :], sd[:, :], -1.0, None, op0=ALU.mult)
            if out16_ap is not None:
                v.drain()
                v.tensor_scalar(ot16[:, :], ybuf[:, :], m[:, 0:1], rs[:, 0:1],
                                op0=ALU.subtract, op1=ALU.mult)
            sv.inc(v.drain())

        @blk.scalar
        def _(s2):
            sv.wait(s2)
            s2.activation(ot32[:, :], ybuf[:, :], ACT.Identity,
                          scale=rs[:, 0:1], bias=sd[:, 0:1])
            s2.drain()
            sa.dinc(s2.dma_start(out32_ap, ot32[:, :]))
            sa.wait(s2)
            if out16_ap is not None:
                sa.dinc(s2.dma_start(out16_ap, ot16[:, :]))
                sa.wait(s2)

    _ln_ctr = [0]

    def _ln_bufs(self, es):
        nc = self.nc
        self._ln_ctr[0] += 1
        u = f"_{self._ln_ctr[0]}"
        return dict(
            ybuf=es.enter_context(nc.sbuf_tensor("lnyb" + u, [128, D], F32)),
            sqs=es.enter_context(nc.sbuf_tensor("lnsq" + u, [128, D], F32)),
            ot32=es.enter_context(nc.sbuf_tensor("lno3" + u, [128, D], F32)),
            ot16=es.enter_context(nc.sbuf_tensor("lno1" + u, [128, D], BF16)),
            yh=es.enter_context(nc.sbuf_tensor("lnyh" + u, [128, D // 2], F32)),
            xo=es.enter_context(nc.sbuf_tensor("lnxo" + u, [128, D], F32)),
            m=es.enter_context(nc.sbuf_tensor("lnm" + u, [128, 1], F32)),
            vv=es.enter_context(nc.sbuf_tensor("lnv" + u, [128, 1], F32)),
            sd=es.enter_context(nc.sbuf_tensor("lnsd" + u, [128, 1], F32)),
            rs=es.enter_context(nc.sbuf_tensor("lnrs" + u, [128, 1], F32)),
        )

    def p_linear_block(self, tag, xsrc, wsrc, resid, out32_ap, out16_ap,
                       pre_t=False, rhs_cb=None, bias_dram=None,
                       scale_sb=None):
        nc, sc = self.nc, self.scr
        ss, sv, sp, sa = self.ss, self.sv, self.sp, self.sa
        from contextlib import ExitStack
        with ExitStack() as es:
            yacc = es.enter_context(
                nc.sbuf_tensor(f"{tag}ya", [128, NT * 2048], F32))
            with ExitStack() as es2:
                xg = es2.enter_context(
                    nc.sbuf_tensor(f"{tag}xg", [128, 16 * TOK], BF16))
                wgb = es2.enter_context(
                    nc.sbuf_tensor(f"{tag}wg", [128, 16 * 2048], BF16))
                psA = es2.enter_context(
                    nc.psum_tensor(f"{tag}pA", [128, 2048], F32))
                psB = es2.enter_context(
                    nc.psum_tensor(f"{tag}pB", [128, 2048], F32))
                self.tm_gemm(f"{tag}0", xsrc, TOK, wsrc, 0, 2048, yacc,
                             (xg, wgb, psA, psB), pre_t=pre_t, rhs_cb=rhs_cb)
                with nc.Block(f"{tag}dump") as blk:
                    @blk.sync
                    def _(s):
                        sp.wait(s)
                        ss.dinc(s.dma_start(
                            sc["yh_scr"][:, :].rearrange("(n p) c -> p n c", p=128),
                            yacc[:, :].rearrange("p (n c) -> p n c", c=2048)))
                        ss.wait(s)
                self.tm_gemm(f"{tag}1", xsrc, TOK, wsrc, 2048, 2048, yacc,
                             (xg, wgb, psA, psB), pre_t=pre_t, rhs_cb=rhs_cb)

            lb = self._ln_bufs(es)
            bias_sb = None
            if bias_dram is not None:
                bias_sb = es.enter_context(
                    nc.sbuf_tensor(f"{tag}bi", [128, D], F32))
                with nc.Block(f"{tag}bl") as blk:
                    @blk.sync
                    def _(s):
                        ss.dinc(s.dma_start(bias_sb[:, :], bias_dram))
                        ss.wait(s)
            for tc in range(NT):
                with nc.Block(f"{tag}e{tc}") as blk:
                    @blk.sync
                    def _(s):
                        sv.wait(s)
                        self.sq.wait(s)
                        ss.dinc(s.dma_start(
                            lb["ybuf"][:, 0:2048],
                            sc["yh_scr"][tc * 128:(tc + 1) * 128, :]))
                        ss.dinc(s.dma_start(
                            lb["xo"][:, :], resid[tc * 128:(tc + 1) * 128, :]))
                        ss.wait(s)

                    spf = sp.n
                    ssf = ss.n

                    def make_ybuf(v, tc=tc):
                        v.wait_ge(ss.h, ssf)
                        v.wait_ge(sp.h, spf)
                        v.drain()
                        ybuf = lb["ybuf"]
                        v.tensor_copy(ybuf[:, 2048:4096],
                                      yacc[:, tc * 2048:(tc + 1) * 2048])
                        v.drain()
                        if scale_sb is not None:
                            v.tensor_tensor(ybuf[:, :], ybuf[:, :],
                                            scale_sb[:, :], op=ALU.mult)
                            v.drain()
                        if bias_sb is not None:
                            v.tensor_tensor(ybuf[:, :], ybuf[:, :],
                                            bias_sb[:, :], op=ALU.add)
                            v.drain()
                        v.tensor_tensor(ybuf[:, :], ybuf[:, :], lb["xo"][:, :],
                                        op=ALU.add)

                    self.emit_ln_tc(
                        blk, make_ybuf, lb,
                        out32_ap[tc * 128:(tc + 1) * 128, :],
                        None if out16_ap is None
                        else out16_ap[tc * 128:(tc + 1) * 128, :])

    def p5_ffn(self):
        nc, ii, sc = self.nc, self.inp, self.scr
        sg, ss, sa, sv, st, sq, sp = (self.sg, self.ss, self.sa, self.sv,
                                      self.st, self.sq, self.sp)
        from contextlib import ExitStack
        with ExitStack() as es:
            zbf = es.enter_context(nc.sbuf_tensor("f5zb", [128, 8 * TOK], BF16))
            b1s = es.enter_context(nc.sbuf_tensor("f5b1", [128, 2 * NF], F32))
            pfm = es.enter_context(nc.sbuf_tensor("f5p", [128, 8 * TOK], BF16))

            with ExitStack() as es2:
                xfm = es2.enter_context(nc.sbuf_tensor("f5xf", [128, KC * TOK], BF16))
                u1s = es2.enter_context(nc.sbuf_tensor("f5u1", [128, KC * 128], BF16))
                psA = es2.enter_context(nc.psum_tensor("f5pA", [128, 512], F32))
                psB = es2.enter_context(nc.psum_tensor("f5pB", [128, 512], F32))
                with nc.Block("p5ax") as blk:
                    @blk.sync
                    def _(s):
                        for kc in range(KC):
                            ss.dinc(s.dma_start(
                                xfm[:, kc * TOK:(kc + 1) * TOK],
                                sc["resbf"][:, kc * 128:(kc + 1) * 128],
                                transpose=True))
                        ss.dinc(s.dma_start(b1s[:, :], ii["b1c"][:, :]))
                        ss.wait(s)
                ps_base = sp.n
                for rc in range(8):
                    with nc.Block(f"p5ar{rc}") as blk:
                        @blk.gpsimd
                        def _(g):
                            sg.dinc(g.dma_start(
                                u1s[:, :].rearrange("p (k c) -> p k c", c=128),
                                ii["U1"][:, rc * 128:(rc + 1) * 128].rearrange(
                                    "(k p) c -> p k c", p=128)))
                            sg.wait(g)

                        sgf = self.sg.n
                        ssf = ss.n

                        @blk.tensor
                        def _(t):
                            t.wait_ge(sg.h, sgf)
                            t.wait_ge(ss.h, ssf)
                            for th in range(2):
                                r = rc * 2 + th
                                ps = psA if r % 2 == 0 else psB
                                t.wait_ge(sp.h, ps_base + max(0, r - 1))
                                last = None
                                for kc in range(KC):
                                    last = t.matmul(
                                        ps[:, :],
                                        u1s[:, kc * 128:(kc + 1) * 128],
                                        xfm[:, kc * TOK + th * 512:
                                            kc * TOK + (th + 1) * 512],
                                        start=(kc == 0), stop=(kc == KC - 1))
                                st.inc(last)

                        tb = st.n - 2

                        @blk.vector
                        def _(v):
                            v.drain()
                            for th in range(2):
                                r = rc * 2 + th
                                ps = psA if r % 2 == 0 else psB
                                v.wait_ge(st.h, tb + th + 1)
                                sp.inc(v.tensor_copy(
                                    pfm[:, rc * TOK + th * 512:
                                        rc * TOK + (th + 1) * 512], ps[:, :]))

            GSZ = 8
            groups = [list(range(a, min(a + GSZ, NF))) for a in range(0, NF, GSZ)]
            with ExitStack() as es2:
                zsb = es2.enter_context(nc.sbuf_tensor("f5z", [128, 8 * TOK], F32))
                v1G = es2.enter_context(
                    nc.sbuf_tensor("f5v1", [128, 2 * GSZ * 16 * 128], BF16))
                u2G = es2.enter_context(
                    nc.sbuf_tensor("f5u2", [128, 2 * GSZ * 1024], BF16))
                sgb = es2.enter_context(nc.sbuf_tensor("f5sg", [128, 2 * 256], F32))
                upb = es2.enter_context(nc.sbuf_tensor("f5up", [128, 256], F32))
                abf = es2.enter_context(nc.sbuf_tensor("f5ab", [128, 2 * 256], BF16))
                hgP = [es2.enter_context(nc.psum_tensor("f5hA", [128, 256], F32)),
                       es2.enter_context(nc.psum_tensor("f5hC", [128, 256], F32))]
                huP = [es2.enter_context(nc.psum_tensor("f5hB", [128, 256], F32)),
                       es2.enter_context(nc.psum_tensor("f5hD", [128, 256], F32))]
                zps = [es2.enter_context(nc.psum_tensor(f"f5z{b}", [128, 512], F32))
                       for b in range(4)]

                def load_group(g, gi):
                    par = gi % 2
                    swp = self.sw[par]
                    for fj, f in enumerate(groups[gi]):
                        base = par * GSZ * 16 * 128 + fj * 16 * 128
                        swp.dinc(g.dma_start(
                            v1G[:, base:base + 8 * 128].rearrange(
                                "p (k c) -> p k c", c=128),
                            ii["V1"][:, f * 128:(f + 1) * 128].rearrange(
                                "(k p) c -> p k c", p=128)))
                        swp.dinc(g.dma_start(
                            v1G[:, base + 8 * 128:base + 16 * 128].rearrange(
                                "p (k c) -> p k c", c=128),
                            ii["V1"][:, INTER + f * 128:INTER + (f + 1) * 128]
                            .rearrange("(k p) c -> p k c", p=128)))
                        ub = par * GSZ * 1024 + fj * 1024
                        swp.dinc(g.dma_start(
                            u2G[:, ub:ub + 1024],
                            ii["U2"][f * 128:(f + 1) * 128, :]))

                with nc.Block("p5bw") as blk:
                    @blk.gpsimd
                    def _(g):
                        load_group(g, 0)

                for gi, grp in enumerate(groups):
                    ng = len(grp)
                    N = 4 * ng
                    par = gi % 2
                    with nc.Block(f"p5b{gi}") as blk:
                        if gi + 1 < len(groups):
                            stf_pre = st.n

                            @blk.gpsimd
                            def _(g):
                                g.wait_ge(st.h, stf_pre)
                                load_group(g, gi + 1)

                        sw_t = self.sw[par].n
                        st_base = st.n
                        sv_base = sv.n
                        sq_base = sq.n
                        sv_pre = sv.n

                        def inc_H(c):
                            return 1 if c == 0 else 2 * c

                        def inc_Z(c):
                            return 2 * c + 3 if c < N - 1 else 2 * N

                        def act_inc(c):
                            return c + (c // ng) + 1

                        @blk.tensor
                        def _(t):
                            t.wait_ge(self.sw[par].h, sw_t)
                            t.wait_ge(sv.h, sv_pre)

                            def emit_H(c):
                                tt, fj = c // ng, c % ng
                                if c >= 2:
                                    t.wait_ge(sq.h, sq_base + c - 1)
                                    t.wait_ge(sv.h, sv_base + act_inc(c - 2))
                                base = par * GSZ * 16 * 128 + fj * 16 * 128
                                lastm = None
                                for rc in range(8):
                                    lastm = t.matmul(
                                        hgP[c % 2][:, :],
                                        v1G[:, base + rc * 128:base + (rc + 1) * 128],
                                        pfm[:, rc * TOK + tt * 256:
                                            rc * TOK + (tt + 1) * 256],
                                        start=(rc == 0), stop=(rc == 7))
                                for rc in range(8):
                                    lastm = t.matmul(
                                        huP[c % 2][:, :],
                                        v1G[:, base + (8 + rc) * 128:
                                            base + (9 + rc) * 128],
                                        pfm[:, rc * TOK + tt * 256:
                                            rc * TOK + (tt + 1) * 256],
                                        start=(rc == 0), stop=(rc == 7))
                                st.inc(lastm)

                            def emit_Z(c):
                                tt, fj = c // ng, c % ng
                                t.wait_ge(sv.h, sv_base + act_inc(c))
                                ub = par * GSZ * 1024 + (c % ng) * 1024
                                lastm = None
                                for rc in range(8):
                                    b, half = rc // 2, rc % 2
                                    lastm = t.matmul(
                                        zps[b][:, half * 256:(half + 1) * 256],
                                        u2G[:, ub + rc * 128:ub + (rc + 1) * 128],
                                        abf[:, (c % 2) * 256:(c % 2) * 256 + 256],
                                        start=(fj == 0 and half == 0),
                                        stop=(fj == ng - 1 and half == 1))
                                st.inc(lastm)

                            for c in range(N):
                                emit_H(c)
                                if c >= 1:
                                    emit_Z(c - 1)
                            emit_Z(N - 1)

                        @blk.scalar
                        def _(s2):
                            for c in range(N):
                                tt, fj = c // ng, c % ng
                                f = grp[fj]
                                s2.wait_ge(st.h, st_base + inc_H(c))
                                if c >= 2:
                                    s2.wait_ge(sv.h, sv_base + act_inc(c - 2))
                                sq.inc(s2.activation(
                                    sgb[:, (c % 2) * 256:(c % 2) * 256 + 256],
                                    hgP[c % 2][:, :], ACT.Silu,
                                    bias=b1s[:, f:f + 1]))

                        @blk.vector
                        def _(v):
                            v.drain()
                            for c in range(N):
                                tt, fj = c // ng, c % ng
                                f = grp[fj]
                                v.wait_ge(st.h, st_base + inc_H(c))
                                v.wait_ge(sq.h, sq_base + c + 1)
                                if c >= 2:
                                    v.wait_ge(st.h, st_base + inc_Z(c - 2))
                                v.tensor_scalar(upb[:, :], huP[c % 2][:, :],
                                                b1s[:, NF + f:NF + f + 1],
                                                None, op0=ALU.add)
                                v.drain()
                                sv.inc(v.tensor_tensor(
                                    abf[:, (c % 2) * 256:(c % 2) * 256 + 256],
                                    sgb[:, (c % 2) * 256:(c % 2) * 256 + 256],
                                    upb[:, :], op=ALU.mult))
                                v.drain()
                                if fj == ng - 1:
                                    tt_ = tt
                                    v.wait_ge(st.h, st_base + inc_Z(c))
                                    for b in range(4):
                                        for half in range(2):
                                            rc = b * 2 + half
                                            dst = zsb[:, rc * TOK + tt_ * 256:
                                                      rc * TOK + (tt_ + 1) * 256]
                                            srcp = zps[b][:, half * 256:
                                                          (half + 1) * 256]
                                            if gi == 0:
                                                v.tensor_copy(dst, srcp)
                                            else:
                                                v.tensor_tensor(dst, dst, srcp,
                                                                op=ALU.add)
                                    sv.inc(v.drain())

                with nc.Block("p5zc") as blk:
                    svf = sv.n

                    @blk.vector
                    def _(v):
                        v.wait_ge(sv.h, svf)
                        v.drain()
                        v.tensor_copy(zbf[:, :], zsb[:, :])
                        sv.inc(v.drain())

            with ExitStack() as es2:
                v2s = es2.enter_context(nc.sbuf_tensor("f5v2", [128, 8 * D], BF16))
                b2s = es2.enter_context(nc.sbuf_tensor("f5b2", [128, D], F32))
                ps8 = es2.enter_context(nc.psum_tensor("f5p8", [128, D], F32))
                lb = self._ln_bufs(es2)
                with nc.Block("p5cl") as blk:
                    @blk.gpsimd
                    def _(g):
                        for rc in range(8):
                            sg.dinc(g.dma_start(
                                v2s[:, rc * D:(rc + 1) * D],
                                ii["V2"][rc * 128:(rc + 1) * 128, :]))
                        sg.wait(g)

                    @blk.sync
                    def _(s):
                        ss.dinc(s.dma_start(b2s[:, :], ii["b2b"][:, :]))
                        ss.wait(s)

                for tc in range(NT):
                    with nc.Block(f"p5ct{tc}") as blk:
                        svf = sv.n
                        sgf = sg.n

                        @blk.sync
                        def _(s):
                            s.wait_ge(sv.h, svf)
                            ss.dinc(s.dma_start(
                                lb["xo"][:, :],
                                sc["res32"][tc * 128:(tc + 1) * 128, :]))
                            ss.wait(s)

                        ssf = ss.n

                        @blk.tensor
                        def _(t):
                            t.wait_ge(sv.h, svf)
                            t.wait_ge(sg.h, sgf)
                            t.wait_ge(ss.h, ssf)
                            last = None
                            for rc in range(8):
                                for nt_ in range(8):
                                    last = t.matmul(
                                        ps8[:, nt_ * 512:(nt_ + 1) * 512],
                                        zbf[:, rc * TOK + tc * 128:
                                            rc * TOK + (tc + 1) * 128],
                                        v2s[:, rc * D + nt_ * 512:
                                            rc * D + (nt_ + 1) * 512],
                                        start=(rc == 0), stop=(rc == 7))
                            st.inc(last)

                        stf = st.n

                        def make_ybuf(v, tc=tc, stf=stf):
                            v.wait_ge(st.h, stf)
                            v.drain()
                            v.tensor_tensor(lb["ybuf"][:, :], ps8[:, :],
                                            b2s[:, :], op=ALU.add)
                            v.drain()
                            v.tensor_tensor(lb["ybuf"][:, :], lb["ybuf"][:, :],
                                            lb["xo"][:, :], op=ALU.add)

                        self.emit_ln_tc(
                            blk, make_ybuf, lb,
                            sc["res32"][tc * 128:(tc + 1) * 128, :],
                            sc["resbf"][tc * 128:(tc + 1) * 128, :])

    def _p5_fchunk(self, blk, gi, tt, fj, f, ng, pfm, v1s, u2s, b1s,
                   sgb, upb, abf, hgA, huA, zps):
        nc, ii, sc = self.nc, self.inp, self.scr
        sg, ss, sv, st, sq = self.sg, self.ss, self.sv, self.st, self.sq

        @blk.gpsimd
        def _(g):
            st.wait(g)
            sg.dinc(g.dma_start(
                v1s[:, 0:8 * 128].rearrange("p (k c) -> p k c", c=128),
                ii["V1"][:, f * 128:(f + 1) * 128].rearrange(
                    "(k p) c -> p k c", p=128)))
            sg.dinc(g.dma_start(
                v1s[:, 8 * 128:16 * 128].rearrange("p (k c) -> p k c", c=128),
                ii["V1"][:, INTER + f * 128:INTER + (f + 1) * 128].rearrange(
                    "(k p) c -> p k c", p=128)))
            sg.dinc(g.dma_start(u2s[:, :], ii["U2"][f * 128:(f + 1) * 128, :]))
            sg.wait(g)

        sgf = sg.n
        sqf_pre = sq.n
        svf_pre0 = sv.n

        @blk.tensor
        def _(t):
            t.wait_ge(sg.h, sgf)
            t.wait_ge(sq.h, sqf_pre)
            t.wait_ge(sv.h, svf_pre0)
            last = None
            for rc in range(8):
                last = t.matmul(hgA[:, :], v1s[:, rc * 128:(rc + 1) * 128],
                                pfm[:, rc * TOK + tt * 256:rc * TOK + (tt + 1) * 256],
                                start=(rc == 0), stop=(rc == 7))
            for rc in range(8):
                last = t.matmul(huA[:, :], v1s[:, (8 + rc) * 128:(9 + rc) * 128],
                                pfm[:, rc * TOK + tt * 256:rc * TOK + (tt + 1) * 256],
                                start=(rc == 0), stop=(rc == 7))
            st.inc(last)

        stf = st.n

        @blk.scalar
        def _(s2):
            st.wait(s2)
            s2.activation(sgb[:, :], hgA[:, :], ACT.Silu,
                          bias=b1s[:, f:f + 1])
            sq.inc(s2.drain())

        sqf = sq.n
        svf_pre = sv.n

        @blk.vector
        def _(v):
            v.wait_ge(st.h, stf)
            v.wait_ge(sq.h, sqf)
            v.drain()
            v.tensor_scalar(upb[:, :], huA[:, :], b1s[:, NF + f:NF + f + 1],
                            None, op0=ALU.add)
            v.drain()
            v.tensor_tensor(abf[:, :], sgb[:, :], upb[:, :], op=ALU.mult)
            sv.inc(v.drain())

        svf = sv.n

        @blk.tensor
        def _(t):
            t.wait_ge(sv.h, svf)
            last = None
            for rc in range(8):
                b, half = rc // 2, rc % 2
                last = t.matmul(
                    zps[b][:, half * 256:(half + 1) * 256],
                    u2s[:, rc * 128:(rc + 1) * 128], abf[:, :],
                    start=(fj == 0 and half == 0),
                    stop=(fj == ng - 1 and half == 1))
            st.inc(last)

    def p6_maskprep(self, bmx):
        nc, ii, sc = self.nc, self.inp, self.scr
        ss, sv, st, sa = self.ss, self.sv, self.st, self.sa
        from contextlib import ExitStack
        with ExitStack() as es:
            msb = es.enter_context(nc.sbuf_tensor("m6i", [128, D], I32))
            rsf = es.enter_context(nc.sbuf_tensor("m6g", [128, 128], F32))
            rsb = es.enter_context(nc.sbuf_tensor("m6b", [128, 128], BF16))
            rsbT = es.enter_context(nc.sbuf_tensor("m6t", [128, 128], BF16))
            bmT = es.enter_context(nc.sbuf_tensor("m6m", [128, 128], F32))

            for i in range(32):
                with nc.Block(f"p6mt{i}") as blk:
                    svf = sv.n
                    ssf0 = ss.n

                    @blk.sync
                    def _(s):
                        s.wait_ge(sv.h, svf)
                        ss.dinc(s.dma_start(
                            msb[:, :], ii["l1_mask"][i * 128:(i + 1) * 128, :]))
                        ss.wait(s)

                    ssf = ss.n

                    @blk.vector
                    def _(v):
                        v.wait_ge(ss.h, ssf)
                        v.drain()
                        v.tensor_reduce(
                            rsf[:, :],
                            msb[:, :].rearrange("p (a b) -> p a b", b=32),
                            axis=AX, op=ALU.add)
                        v.drain()
                        v.tensor_copy(rsb[:, :], rsf[:, :])
                        sv.inc(v.drain())

                    @blk.sync
                    def _(s):
                        sv.wait(s)
                        ss.dinc(s.dma_start(rsbT[:, :], rsb[:, :],
                                            transpose=True))
                        ss.wait(s)

                    ssf2 = ss.n

                    @blk.vector
                    def _(v):
                        v.wait_ge(ss.h, ssf2)
                        v.drain()
                        v.tensor_reduce(
                            bmT[:, i * 4:(i + 1) * 4],
                            rsbT[:, :].rearrange("p (a b) -> p a b", b=32),
                            axis=AX, op=ALU.add)
                        sv.inc(v.drain())

            with nc.Block("p6mf") as blk:
                svf = sv.n

                @blk.vector
                def _(v):
                    v.wait_ge(sv.h, svf)
                    v.drain()
                    v.tensor_scalar(bmT[:, :], bmT[:, :], 0.0, None,
                                    op0=ALU.is_gt)
                    sv.inc(v.drain())

                @blk.sync
                def _(s):
                    sv.wait(s)
                    ss.dinc(s.dma_start(sc["bm_scr"][0:128, 0:128], bmT[:, :]))
                    ss.wait(s)
                    for kc in range(KC):
                        for pa in range(4):
                            kb = kc * 4 + pa
                            dst = bmx[pa * 32:(pa + 1) * 32,
                                      kc * 128:(kc + 1) * 128]
                            srcr = sc["bm_scr"][kb:kb + 1, 0:128]
                            ss.dinc(s.dma_start(
                                dst, srcr.to_broadcast([32, 128])))
                    ss.wait(s)

    def p7_prep(self, s_bcast):
        nc, ii, sc = self.nc, self.inp, self.scr
        sg, ss, sv, st, sq, sa = (self.sg, self.ss, self.sv, self.st,
                                  self.sq, self.sa)
        from contextlib import ExitStack
        with ExitStack() as es:
            a_sb = es.enter_context(nc.sbuf_tensor("d7a", [32, D], BF16))
            bT = es.enter_context(nc.sbuf_tensor("d7b", [128, D], BF16))
            wt = es.enter_context(nc.sbuf_tensor("d7w", [128, D], BF16))
            et = es.enter_context(nc.sbuf_tensor("d7e", [128, D], BF16))
            sqb = es.enter_context(nc.sbuf_tensor("d7s", [128, D], BF16))
            ones = es.enter_context(nc.sbuf_tensor("d7o", [128, 1], BF16))
            cnr = es.enter_context(nc.sbuf_tensor("d7c", [1, D], F32))
            mag = es.enter_context(nc.sbuf_tensor("d7m", [1, D], F32))

            with nc.Block("p7a") as blk:
                @blk.gpsimd
                def _(g):
                    sg.dinc(g.dma_start(a_sb[:, :], ii["lf_A"][:, :]))
                    sg.wait(g)

                saf = sa.n + 16

                @blk.sync
                def _(s):
                    s.wait_ge(sa.h, saf)
                    ss.dinc(s.dma_start(bT[:, :], sc["lfB_bf"][:, :],
                                        transpose=True))
                    ss.dinc(s.dma_start(mag[:, :], ii["lf_magb"][:, :]))
                    ss.wait(s)

                @blk.vector
                def _(v):
                    v.memset(ones[:, :], 1.0)
                    v.memset(sqb[:, 0:96], 0.0)
                    sv.inc(v.drain())

                @blk.scalar
                def _(s2):
                    sv.wait(s2)
                    sa.dinc(s2.dma_start(
                        sc["lfB_bf"][:, 32:128].rearrange(
                            "(a p) c -> p a c", p=128),
                        sqb[:, 0:96].unsqueeze(1).to_broadcast([128, 32, 96])))
                    sa.wait(s2)

            es1 = ExitStack()
            ps8 = es1.enter_context(nc.psum_tensor("d7p", [128, D], F32))
            for kc in range(KC):
                with nc.Block(f"p7b{kc}") as blk:
                    svf = sv.n

                    @blk.sync
                    def _(s):
                        s.wait_ge(sv.h, svf)
                        ss.dinc(s.dma_start(
                            wt[:, :], sc["lfW_bf"][:, kc * 128:(kc + 1) * 128],
                            transpose=True))
                        ss.wait(s)

                    ssf = ss.n
                    sgf = sg.n

                    @blk.tensor
                    def _(t):
                        t.wait_ge(ss.h, ssf)
                        t.wait_ge(sg.h, sgf)
                        t.wait_ge(sv.h, svf)
                        last = None
                        for nt_ in range(8):
                            last = t.matmul(
                                ps8[:, nt_ * 512:(nt_ + 1) * 512],
                                a_sb[:, kc * 128:(kc + 1) * 128],
                                bT[0:32, nt_ * 512:(nt_ + 1) * 512],
                                start=True, stop=True)
                        st.inc(last)

                    stf = st.n

                    @blk.vector
                    def _(v):
                        v.wait_ge(st.h, stf)
                        v.drain()
                        v.tensor_tensor(et[:, :], wt[:, :], ps8[:, 0:D],
                                        op=ALU.add)
                        sv.inc(v.drain())

                    @blk.scalar
                    def _(s2):
                        sv.wait(s2)
                        sa.dinc(s2.dma_start(
                            sc["effW_scr"][kc * 128:(kc + 1) * 128, :], et[:, :]))
                        sa.wait(s2)

            es1.close()
            with ExitStack() as es2:
                pscn = es2.enter_context(nc.psum_tensor("d7n", [1, D], F32))
                for kc in range(KC):
                    with nc.Block(f"p7c{kc}") as blk:
                        svf = sv.n

                        @blk.sync
                        def _(s):
                            s.wait_ge(sv.h, svf)
                            ss.dinc(s.dma_start(
                                et[:, :],
                                sc["effW_scr"][kc * 128:(kc + 1) * 128, :]))
                            ss.wait(s)

                        ssf = ss.n

                        @blk.vector
                        def _(v):
                            v.wait_ge(ss.h, ssf)
                            v.drain()
                            v.tensor_tensor(sqb[:, :], et[:, :], et[:, :],
                                            op=ALU.mult)
                            sv.inc(v.drain())

                        svf2 = sv.n

                        @blk.tensor
                        def _(t):
                            t.wait_ge(sv.h, svf2)
                            last = None
                            for nt_ in range(8):
                                last = t.matmul(
                                    pscn[:, nt_ * 512:(nt_ + 1) * 512],
                                    ones[:, :],
                                    sqb[:, nt_ * 512:(nt_ + 1) * 512],
                                    start=(kc == 0), stop=(kc == KC - 1))
                            st.inc(last)

                with nc.Block("p7d") as blk:
                    stf = st.n

                    @blk.vector
                    def _(v):
                        v.wait_ge(st.h, stf)
                        v.drain()
                        v.tensor_copy(cnr[:, :], pscn[:, 0:D])
                        sv.inc(v.drain())

                    @blk.scalar
                    def _(s2):
                        sv.wait(s2)
                        sq.inc(s2.activation(cnr[:, :], cnr[:, :], ACT.Sqrt))

                    @blk.vector
                    def _(v):
                        sq.wait(v)
                        v.reciprocal(cnr[:, :], cnr[:, :])
                        v.drain()
                        v.tensor_tensor(cnr[:, :], cnr[:, :], mag[:, :],
                                        op=ALU.mult)
                        sv.inc(v.drain())

                    @blk.sync
                    def _(s):
                        sv.wait(s)
                        ss.dinc(s.dma_start(sc["row_scr"][:, :], cnr[:, :]))
                        ss.wait(s)
                        ss.dinc(s.dma_start(
                            s_bcast[:, :],
                            sc["row_scr"][0:1, :].to_broadcast([128, D])))
                        ss.wait(s)


def _build(upto=100):
    p = P(upto)
    nc = p.nc
    sc = p.scr
    p.p0_casts()
    if upto >= 1:
        p.p1_qkv()
    if upto >= 2:
        with (
            nc.sbuf_tensor("rkp", [128, KVH * 2176], BF16) as rkp,
            nc.sbuf_tensor("vtm", [128, KVH * 18 * 128], BF16) as vtm,
            nc.sbuf_tensor("ckT", [128, KVH * 128], BF16) as ckT,
            nc.sbuf_tensor("cvt", [128, KVH * 128], BF16) as cvt,
            nc.sbuf_tensor("gts", [128, NT * 96], F32) as gts,
        ):
            p.p2_prep(rkp, vtm, ckT, cvt, gts)
            if upto >= 3:
                p.p3_attn(rkp, vtm, ckT, cvt, gts)
    if upto >= 4:
        p.p_linear_block("p4", sc["comb_scr"], sc["wo_bf"],
                         p.inp["x_own"][:, :], sc["res32"], sc["resbf"])
    if upto >= 5:
        p.p5_ffn()
    if upto >= 6:
        with nc.sbuf_tensor("bmx", [128, KC * 128], F32) as bmx:
            p.p6_maskprep(bmx)

            def mask_cb(v, wgb, j, kc, n0, bmx=bmx):
                nb0 = n0 // 32
                nbs = 2048 // 32
                sl = wgb[:, j * 2048:(j + 1) * 2048].rearrange(
                    "p (a b) -> p a b", b=32)
                mk = bmx[:, kc * 128 + nb0:kc * 128 + nb0 + nbs]
                v.tensor_tensor(sl, sl, mk.unsqueeze(2).to_broadcast(
                    [128, nbs, 32]), op=ALU.mult)

            p.p_linear_block("p6", sc["resbf"], sc["l1w_bf"],
                             sc["res32"][:, :], sc["res32"], sc["resbf"],
                             rhs_cb=mask_cb, bias_dram=p.inp["l1_biasb"][:, :])
    if upto >= 7:
        with nc.sbuf_tensor("sbc", [128, D], F32) as s_bcast:
            p.p7_prep(s_bcast)
            p.p_linear_block("p7", sc["resbf"], sc["effW_scr"],
                             sc["res32"][:, :], p.y_out, None,
                             pre_t=True, scale_sb=s_bcast)
    return p


def _stage_inputs(inputs):
    x = np.ascontiguousarray(inputs["L0_attn_hidden_states"])
    g = lambda n: np.ascontiguousarray(inputs[n])
    mem_kv = g("L0_attn_mem_kv")
    shared = {
        "w_qkv": g("L0_attn_w_qkv"), "w_o": g("L0_attn_w_o"),
        "k_posT": np.ascontiguousarray(
            inputs["L0_attn_k_pos"].transpose(2, 0, 1).reshape(DH, KVH * CB)),
        "v_posT": np.ascontiguousarray(
            inputs["L0_attn_v_pos"].transpose(2, 0, 1).reshape(DH, KVH * CB)),
        "k_w1": g("L0_attn_k_w1"),
        "k_b1c": np.ascontiguousarray(
            inputs["L0_attn_k_b1"].reshape(16, 128).T),
        "k_w2": g("L0_attn_k_w2"),
        "k_b2c": np.ascontiguousarray(inputs["L0_attn_k_b2"].reshape(128, 1)),
        "v_w1": g("L0_attn_v_w1"),
        "v_b1c": np.ascontiguousarray(
            inputs["L0_attn_v_b1"].reshape(16, 128).T),
        "v_w2": g("L0_attn_v_w2"),
        "v_b2b": np.ascontiguousarray(np.broadcast_to(
            inputs["L0_attn_v_b2"][None, :], (128, DH))),
        "mem_kT": np.ascontiguousarray(mem_kv[0, :, 0, :].T),
        "mem_vb": np.ascontiguousarray(mem_kv[1, :, 0, :].reshape(1, KVH * DH)),
        "w_gate": g("L0_attn_w_gate"),
        "b_gateb": np.ascontiguousarray(np.broadcast_to(
            inputs["L0_attn_b_gate"][None, :], (128, 96))),
        "U1": g("L0_ffn_U1"), "V1": g("L0_ffn_V1"),
        "U2": g("L0_ffn_U2"), "V2": g("L0_ffn_V2"),
        "b1c": np.ascontiguousarray(
            inputs["L0_ffn_b1"].reshape(2 * NF, 128).T),
        "b2b": np.ascontiguousarray(np.broadcast_to(
            inputs["L0_ffn_b2"][None, :], (128, D))),
        "l1_mask": g("L1_attn_mask"), "l1_w": g("L1_attn_weight"),
        "l1_biasb": np.ascontiguousarray(np.broadcast_to(
            inputs["L1_attn_bias"][None, :], (128, D))),
        "lf_W": g("L1_ffn_W"), "lf_A": g("L1_ffn_lora_A"),
        "lf_B": g("L1_ffn_lora_B"),
        "lf_magb": np.ascontiguousarray(
            inputs["L1_ffn_magnitude"].reshape(1, D)),
        "e4": np.ascontiguousarray(
            (np.arange(128)[:, None] // 32 == np.arange(4)[None, :])
            .astype(np.float32)),
        "ident": np.eye(128, dtype=np.float32),
    }
    tabs = [_host_tables(0), _host_tables(1)]
    in_maps = []
    own_rows = []
    for c in range(N_CORES):
        b, h = c // 2, c % 2
        rows = np.concatenate(
            [np.arange(j * 128, (j + 1) * 128) for j in range(h, NTF, 2)])
        own_rows.append(rows)
        m = dict(shared)
        m["x_full"] = np.ascontiguousarray(x[b])
        m["x_own"] = np.ascontiguousarray(x[b][rows])
        for k, v in tabs[h].items():
            m[k] = v
        in_maps.append(m)
    return in_maps, own_rows


def kernel(**inputs):
    in_maps, own_rows = _stage_inputs(inputs)
    p = _build()
    res = run_bass_kernel_spmd(p.nc, in_maps, core_ids=list(range(N_CORES)))
    out = np.zeros((B, S, D), dtype=np.float32)
    for c in range(N_CORES):
        out[c // 2, own_rows[c]] = res.results[c]["y_out"]
    return out
